# version 08
# baseline (speedup 1.0000x reference)
import contextlib
import math
import numpy as np
import ml_dtypes
import concourse.bass as bass
import concourse.mybir as mybir
from concourse.bass_utils import run_bass_kernel_spmd

try:
    import resource
    _soft, _hard = resource.getrlimit(resource.RLIMIT_STACK)
    resource.setrlimit(resource.RLIMIT_STACK, (_hard, _hard))
except Exception:
    pass

F32 = mybir.dt.float32
BF16 = mybir.dt.bfloat16
I32 = mybir.dt.int32
AF = mybir.ActivationFunctionType
ALU = mybir.AluOpType
AX = mybir.AxisListType

T = 4096
D = 1024
NCH = 8
NTT = 8
TTW = 512
NTB = 32
NPAIR = 8
NEG = -1.0e30
CVALS = [1e-5, 1e-12, 1e-24, 0.5]

BF = ml_dtypes.bfloat16


NEXP = 8


def set_dims(t, nexp=8):
    global T, NTT, NTB, NEXP
    T = t
    NTT = t // TTW
    NTB = t // 128
    NEXP = nexp


_NAMEC = [0]
def uname(p):
    _NAMEC[0] += 1
    return f"{p}_{_NAMEC[0]}"


class Ctx:
    ENGS = ("te", "dv", "ac", "gp", "sy")

    def __init__(self, nc, blk, es):
        self.nc = nc
        self.blk = blk
        self.es = es
        self.h = {}
        self.cnt = {}
        self.lw = {}
        self.own = {"te": "pe", "dv": "dv", "ac": "ac", "gp": "gp", "sy": "sy"}
        for s in ("pe", "dv", "ac", "gp", "sy"):
            self.h[s] = es.enter_context(nc.semaphore(s))
            self.cnt[s] = 0
        self.global_deps = []
        self.dma_pool = {True: [], False: []}
        self.dma_all = []
        self._dq = 0

    def acquire_dsem(self, sw):
        if self.dma_pool[sw]:
            return self.dma_pool[sw].pop()
        name = f"q{self._dq}"
        self._dq += 1
        self.h[name] = self.es.enter_context(self.nc.semaphore(name))
        self.cnt[name] = 0
        self.dma_all.append(name)
        return name

    def release_dsem(self, name, sw):
        self.dma_pool[sw].append(name)

    def _bf(self, eng):
        return {"te": self.blk.tensor, "dv": self.blk.vector, "ac": self.blk.scalar,
                "gp": self.blk.gpsimd, "sy": self.blk.sync}[eng]

    def run(self, eng, fn, after=(), tok=False, dma=False, dsem=None):
        after = [a for a in list(after) + self.global_deps if a is not None]
        if dma:
            assert dsem is not None
            semname = dsem
            k = 16
        else:
            semname = self.own[eng]
            k = 1
        newtok = None
        if tok:
            self.cnt[semname] += k
            newtok = (semname, self.cnt[semname])

        def body(e):
            for (s, v) in after:
                if self.lw.get((eng, s), 0) >= v:
                    continue
                e.wait_ge(self.h[s], v)
                self.lw[(eng, s)] = v
            inst = fn(e)
            if tok:
                inst.then_inc(self.h[semname], k)

        self._bf(eng)(body)
        return newtok

    def finish_dma_waits(self, eng="sy"):
        for s in self.dma_all:
            if self.cnt[s] > 0:
                self.run(eng, lambda e, s=s: e.wait_ge(self.h[s], self.cnt[s]))


class Ring:

    def __init__(self, ctx, depth=2, sw=False):
        self.ctx = ctx
        self.sw = sw
        self.sems = [ctx.acquire_dsem(sw) for _ in range(depth)]
        self.i = 0

    def dma(self, eng, dst, src, after=()):
        assert (eng == "gp") == self.sw
        s = self.sems[self.i % len(self.sems)]
        self.i += 1
        prev = (s, self.ctx.cnt[s]) if self.ctx.cnt[s] > 0 else None
        return self.ctx.run(eng, lambda e: e.dma_start(dst, src),
                            after=list(after) + [prev], tok=True, dma=True, dsem=s)

    def close(self):
        for s in self.sems:
            self.ctx.release_dsem(s, self.sw)


def misc_ring(ctx, eng="sy"):
    if not hasattr(ctx, "_misc"):
        ctx._misc = {}
    if eng not in ctx._misc:
        ctx._misc[eng] = Ring(ctx, 2, sw=(eng == "gp"))
    return ctx._misc[eng]


class Arena:
    def __init__(self, handle, slots):
        self.h = handle
        self.slots = slots
        self.off = 0
        self.marks = []

    def alloc(self, n):
        assert self.off + n <= self.slots, f"arena overflow {self.off}+{n}>{self.slots}"
        o = self.off
        self.off += n
        return o

    def push(self):
        self.marks.append(self.off)

    def pop(self):
        self.off = self.marks.pop()

    def view(self, off, n):
        return self.h[:, off:off + n]


def emit_x0_transpose(ctx, bufs, x0_ap, xres_ap):
    nc = ctx.nc
    xres_r = xres_ap.rearrange("(ch p) t -> p ch t", p=128)
    with contextlib.ExitStack() as es:
        xt = es.enter_context(nc.sbuf_tensor(uname("i_xt"), [128, 2 * 1024], F32))
        xo = es.enter_context(nc.sbuf_tensor(uname("i_xo"), [128, 2 * 1024], F32))
        pst = es.enter_context(nc.psum_tensor(uname("i_ps"), [128, 512], F32))
        rld = Ring(ctx, 2)
        rst = Ring(ctx, 2)
        ld = [None, None]
        ev = [None, None]
        stw = [None, None]
        for tb in range(NTB):
            b = tb % 2
            ld[b] = rld.dma("sy", xt[:, b * 1024:(b + 1) * 1024],
                            x0_ap[tb * 128:(tb + 1) * 128, :],
                            after=[ev[b]])
            for ch in range(NCH):
                bank = ch % 4
                tk = ctx.run("te", lambda e, ch=ch, b=b, bank=bank: e.transpose(
                    pst[:, bank * 128:bank * 128 + 128],
                    xt[:, b * 1024 + ch * 128: b * 1024 + ch * 128 + 128],
                    bufs["identf"][:, :]),
                    after=[ld[b], ev[b], stw[b]], tok=True)
                eng = "dv" if ch % 2 == 0 else "ac"
                ev[b] = ctx.run(eng, lambda e, ch=ch, b=b, bank=bank: (
                    e.tensor_copy(xo[:, b * 1024 + ch * 128: b * 1024 + ch * 128 + 128],
                                  pst[:, bank * 128:bank * 128 + 128])
                    if eng == "dv" else
                    e.copy(xo[:, b * 1024 + ch * 128: b * 1024 + ch * 128 + 128],
                           pst[:, bank * 128:bank * 128 + 128])),
                    after=[tk], tok=True)
            stw[b] = rst.dma("sy", xres_r[:, :, tb * 128:(tb + 1) * 128],
                             xo[:, b * 1024:(b + 1) * 1024].rearrange("p (ch m) -> p ch m", m=128),
                             after=[ev[b]])
        rld.close(); rst.close()
        return [stw[0], stw[1]]


def emit_final_transpose(ctx, bufs, xres_ap, out_ap, deps):
    nc = ctx.nc
    xres_r = xres_ap.rearrange("(ch p) t -> p ch t", p=128)
    with contextlib.ExitStack() as es:
        xt = es.enter_context(nc.sbuf_tensor(uname("f_xt"), [128, 2 * 1024], F32))
        xo = es.enter_context(nc.sbuf_tensor(uname("f_xo"), [128, 2 * 1024], F32))
        pst = es.enter_context(nc.psum_tensor(uname("f_ps"), [128, 512], F32))
        rld = Ring(ctx, 2)
        rst = Ring(ctx, 2)
        ld = [None, None]
        ev = [None, None]
        stw = [None, None]
        for tb in range(NTB):
            b = tb % 2
            ld[b] = rld.dma("sy", xt[:, b * 1024:(b + 1) * 1024].rearrange("p (ch m) -> p ch m", m=128),
                            xres_r[:, :, tb * 128:(tb + 1) * 128],
                            after=[ev[b]] + (deps if tb < 2 else []))
            for ch in range(NCH):
                bank = ch % 4
                tk = ctx.run("te", lambda e, ch=ch, b=b, bank=bank: e.transpose(
                    pst[:, bank * 128:bank * 128 + 128],
                    xt[:, b * 1024 + ch * 128: b * 1024 + ch * 128 + 128],
                    bufs["identf"][:, :]),
                    after=[ld[b], ev[b], stw[b]], tok=True)
                ev[b] = ctx.run("dv", lambda e, ch=ch, b=b, bank=bank: e.tensor_copy(
                    xo[:, b * 1024 + ch * 128: b * 1024 + ch * 128 + 128],
                    pst[:, bank * 128:bank * 128 + 128]),
                    after=[tk], tok=True)
            stw[b] = rst.dma("sy", out_ap[tb * 128:(tb + 1) * 128, :],
                             xo[:, b * 1024:(b + 1) * 1024], after=[ev[b]])
        rld.close(); rst.close()
        return [stw[0], stw[1]]


def emit_lnx(ctx, bufs, deps, produce, consume, eps=1e-5, bf16_src=None, sub_mean=True):
    nc = ctx.nc
    out_toks = []
    with contextlib.ExitStack() as es:
        XS = es.enter_context(nc.sbuf_tensor(uname("lx_xs"), [128, 4096], BF16))
        SCR = es.enter_context(nc.sbuf_tensor(uname("lx_scr"), [128, 4096], F32))
        MU = es.enter_context(nc.sbuf_tensor(uname("lx_mu"), [128, 512], F32))
        M2 = es.enter_context(nc.sbuf_tensor(uname("lx_m2"), [128, 512], F32))
        VAR = es.enter_context(nc.sbuf_tensor(uname("lx_var"), [128, 512], F32))
        RST = es.enter_context(nc.sbuf_tensor(uname("lx_rst"), [128, 512], F32))
        psmu = es.enter_context(nc.psum_tensor(uname("lx_psmu"), [128, 512], F32))
        psms = es.enter_context(nc.psum_tensor(uname("lx_psms"), [128, 512], F32))
        prev_cons = None
        prev_mm = None
        prev_sq = None
        prev_e5 = None
        prev_e1 = None
        prev_e3 = None
        for tt in range(NTT):
            t0 = tt * TTW
            if produce is not None:
                ptok, getsrc = produce(tt, [prev_cons, prev_mm, prev_sq] + (list(deps) if tt == 0 else []))
            else:
                ptok, getsrc = None, (lambda ch, t0=t0: bf16_src[:, ch * T + t0: ch * T + t0 + TTW])
            sq = None
            mm1 = None
            mm2 = None
            for ch in range(NCH):
                sq = ctx.run("ac", lambda e, ch=ch: e.activation(
                    XS[:, ch * 512:ch * 512 + 512], getsrc(ch), AF.Square),
                    after=[ptok, prev_mm] + (list(deps) if tt == 0 else []), tok=True)
            prev_sq = sq
            if sub_mean:
                for ch in range(NCH):
                    mm1 = ctx.run("te", lambda e, ch=ch: e.matmul(
                        psmu[:, :], bufs["inv1024"][:, :], getsrc(ch),
                        start=(ch == 0), stop=(ch == NCH - 1)),
                        after=[ptok, prev_e1] + (list(deps) if tt == 0 else []), tok=(ch == NCH - 1))
            for ch in range(NCH):
                mm2 = ctx.run("te", lambda e, ch=ch: e.matmul(
                    psms[:, :], bufs["inv1024"][:, :], XS[:, ch * 512:ch * 512 + 512],
                    start=(ch == 0), stop=(ch == NCH - 1)),
                    after=[sq, prev_e3], tok=(ch == NCH - 1))
            prev_mm = mm2
            if sub_mean:
                e1 = ctx.run("dv", lambda e: e.tensor_copy(MU[:, :], psmu[:, :]),
                             after=[mm1, prev_e5, prev_cons], tok=True)
                prev_e1 = e1
                e2 = ctx.run("ac", lambda e: e.activation(M2[:, :], MU[:, :], AF.Square), after=[e1], tok=True)
                e3 = ctx.run("dv", lambda e: e.tensor_sub(VAR[:, :], psms[:, :], M2[:, :]), after=[mm2, e2], tok=True)
            else:
                e3 = ctx.run("dv", lambda e: e.tensor_copy(VAR[:, :], psms[:, :]),
                             after=[mm2, prev_e5, prev_cons], tok=True)
            prev_e3 = e3
            e4 = ctx.run("ac", lambda e: e.activation(VAR[:, :], VAR[:, :], AF.Ln, bias=float(eps)), after=[e3], tok=True)
            e5 = ctx.run("ac", lambda e: e.activation(RST[:, :], VAR[:, :], AF.Exp, scale=-0.5),
                         after=[e4, prev_cons], tok=True)
            prev_e5 = e5
            ctok = None
            for ch in range(NCH):
                if sub_mean:
                    a1 = ctx.run("dv", lambda e, ch=ch: e.tensor_sub(
                        SCR[:, ch * 512:ch * 512 + 512], getsrc(ch), MU[:, :]),
                        after=[e5], tok=True)
                else:
                    a1 = ctx.run("dv", lambda e, ch=ch: e.tensor_copy(
                        SCR[:, ch * 512:ch * 512 + 512], getsrc(ch)),
                        after=[e5], tok=True)
                ctok = consume(tt, ch, SCR[:, ch * 512:ch * 512 + 512], RST[:, 0:512], [a1])
            prev_cons = ctok
            out_toks.append(ctok)
    return out_toks


def emit_ln(ctx, bufs, xsrc_ap, deps, xn_dram=None, eps=1e-5):
    nc = ctx.nc
    xsrc_r = xsrc_ap.rearrange("(ch p) t -> p ch t", p=128)
    XN = bufs["XN"]
    es = contextlib.ExitStack()
    XT = es.enter_context(nc.sbuf_tensor(uname("ln_xt"), [128, 4096], F32))
    XB = es.enter_context(nc.sbuf_tensor(uname("ln_xb"), [128, 4096], BF16))
    rlt = Ring(ctx, 1)
    rxw = Ring(ctx, 2)
    state = {"wtk": None}

    def produce(tt, after):
        t0 = tt * TTW
        lt = rlt.dma("sy", XT[:, :].rearrange("p (ch m) -> p ch m", m=512),
                     xsrc_r[:, :, t0:t0 + TTW], after=after + [state["wtk"]])
        ck = ctx.run("dv", lambda e: e.tensor_copy(XB[:, :], XT[:, :]), after=[lt] + after, tok=True)
        def getsrc(ch, which="stats"):
            return XB[:, ch * 512:ch * 512 + 512]
        return ck, getsrc

    def consume(tt, ch, scr, rst, after):
        t0 = tt * TTW
        mtok = ctx.run("dv", lambda e: e.tensor_mul(
            XN[:, ch * T + t0: ch * T + t0 + 512], scr, rst), after=after, tok=True)
        if xn_dram is not None and ch == NCH - 1:
            state["wtk"] = rxw.dma("sy", xn_dram.rearrange("(ch p) t -> p ch t", p=128)[:, :, t0:t0 + TTW],
                                   XN[:, :].rearrange("p (ch t) -> p ch t", t=T)[:, :, t0:t0 + TTW],
                                   after=[mtok])
            return state["wtk"]
        return mtok

    toks = emit_lnx(ctx, bufs, deps, produce, consume, eps=eps)
    rlt.close(); rxw.close()
    es.close()
    return toks


def wslab_load(ctx, ring, dst, w_ap, c0, cols, after=()):
    din = w_ap.shape[0]
    if din <= 128:
        return ring.dma("gp", dst[0:din, 0:cols], w_ap[:, c0:c0 + cols], after=after)
    assert din % 128 == 0
    src = w_ap[:, c0:c0 + cols].rearrange("(kc p) m -> p kc m", p=128)
    dstr = dst[:, 0:(din // 128) * cols].rearrange("p (kc m) -> p kc m", m=cols)
    return ring.dma("gp", dstr, src, after=after)


def emit_proj_fm(ctx, bufs, w_ap, src_sb, src_off, dout_list, evac, deps,
                 din=D, src_stride=None, nt=None, tw=TTW, wdma="gp"):
    nc = ctx.nc
    if src_stride is None:
        src_stride = T
    if nt is None:
        nt = NTT
    din_ch = (din + 127) // 128
    ksz = [min(128, din - 128 * kc) for kc in range(din_ch)]
    toks = []
    with contextlib.ExitStack() as es:
        wsb = [es.enter_context(nc.sbuf_tensor(uname(f"pf_w{i}"), [128, din_ch * 128], BF16)) for i in range(2)]
        pss = [es.enter_context(nc.psum_tensor(uname(f"pf_ps{i}"), [128, tw], F32)) for i in range(2)]
        rw = Ring(ctx, 2, sw=True)
        wld = [None, None]
        lastmm = [None, None]
        ev = [None, None]
        for di, (c0, cols) in enumerate(dout_list):
            wb = di % 2
            wld[wb] = wslab_load(ctx, rw, wsb[wb][:, 0:din_ch * cols], w_ap, c0, cols,
                                 after=[lastmm[wb]] + (list(deps) if di < 2 else []))
            for tt in range(nt):
                pb = tt % 2
                mm = None
                for kc in range(din_ch):
                    mm = ctx.run("te", lambda e, kc=kc, wb=wb, pb=pb, tt=tt, cols=cols: e.matmul(
                        pss[pb][0:cols, :],
                        wsb[wb][0:ksz[kc], kc * cols:kc * cols + cols],
                        src_sb[0:ksz[kc], src_off + kc * src_stride + tt * tw:
                               src_off + kc * src_stride + tt * tw + tw],
                        start=(kc == 0), stop=(kc == din_ch - 1)),
                        after=[wld[wb], ev[pb]] + (list(deps) if di == 0 and tt == 0 else []),
                        tok=(kc == din_ch - 1))
                lastmm[wb] = mm
                ev[pb] = evac(di, tt, pss[pb][0:cols, :], mm)
                toks.append(ev[pb])
        rw.close()
    return toks[-2:]


def emit_l0ffn(ctx, bufs, W, deps, g_dram):
    nc = ctx.nc
    toks = []
    with contextlib.ExitStack() as es:
        SIN = es.enter_context(nc.sbuf_tensor(uname("f0_sin"), [128, T], F32))
        HJ = es.enter_context(nc.sbuf_tensor(uname("f0_hj"), [128, T], F32))
        GA = es.enter_context(nc.sbuf_tensor(uname("f0_ga"), [128, T], F32))
        GB = es.enter_context(nc.sbuf_tensor(uname("f0_gb"), [128, T], BF16))
        PAR = es.enter_context(nc.sbuf_tensor(uname("f0_par"), [128, 4], F32))
        W1C = es.enter_context(nc.sbuf_tensor(uname("f0_w1c"), [128, 64], F32))
        B1C = es.enter_context(nc.sbuf_tensor(uname("f0_b1c"), [128, 64], F32))
        W2C = es.enter_context(nc.sbuf_tensor(uname("f0_w2c"), [128, 64], F32))
        GT = es.enter_context(nc.sbuf_tensor(uname("f0_gt"), [128, 2 * TTW], BF16))
        BOUT = es.enter_context(nc.sbuf_tensor(uname("f0_bout"), [128, NCH], F32))
        l1 = misc_ring(ctx).dma("sy", W1C[0:64, 0:16], W["L0_ffn_W1"], after=deps)
        l2 = misc_ring(ctx).dma("sy", B1C[0:64, 0:16], W["L0_ffn_b1"], after=deps)
        l3 = misc_ring(ctx).dma("sy", W2C[0:64, 0:16], W["L0_ffn_W2"], after=deps)
        l4 = misc_ring(ctx).dma("sy", PAR[0:64, 0:1], W["L0_ffn_bin_"].rearrange("(a b) -> b a", a=1), after=deps)
        l5 = misc_ring(ctx).dma("sy", PAR[0:64, 1:2], W["L0_ffn_b2"].rearrange("(a b) -> b a", a=1), after=deps)
        l6 = misc_ring(ctx).dma("sy", BOUT[:, :], W["L0_ffn_boutT"], after=deps)
        ptoks = [l1, l2, l3, l4, l5, l6]

        def ev_sin(di, tt, ps, mm):
            return ctx.run("dv", lambda e: e.tensor_scalar(
                SIN[0:64, tt * TTW:(tt + 1) * TTW], ps, PAR[0:64, 0:1], None, ALU.add),
                after=[mm] + ptoks, tok=True)
        t1 = emit_proj_fm(ctx, bufs, W["L0_ffn_Win"], bufs["XN"], 0, [(0, 64)], ev_sin, deps)
        htok = None
        gtok = t1[-1]
        atk = None
        for j in range(16):
            htok = ctx.run("ac", lambda e, j=j: e.activation(
                HJ[0:64, :], SIN[0:64, :], AF.Relu, bias=B1C[0:64, j:j + 1], scale=W1C[0:64, j:j + 1]),
                after=t1 + [atk], tok=True)
            if j == 0:
                atk = ctx.run("dv", lambda e: e.tensor_scalar(
                    GA[0:64, :], HJ[0:64, :], W2C[0:64, 0:1], None, ALU.mult),
                    after=[htok], tok=True)
            else:
                atk = ctx.run("dv", lambda e, j=j: e.scalar_tensor_tensor(
                    GA[0:64, :], HJ[0:64, :], W2C[0:64, j:j + 1], GA[0:64, :],
                    ALU.mult, ALU.add),
                    after=[htok], tok=True)
        gtok = ctx.run("dv", lambda e: e.tensor_scalar(
            GB[0:64, :], GA[0:64, :], PAR[0:64, 1:2], None, ALU.add),
            after=[atk], tok=True)
        g_r = g_dram.rearrange("(ch p) t -> p ch t", p=128)
        rgs = Ring(ctx, 2)
        st = [None]

        def ev_out(di, tt, ps, mm):
            b = (di * NTT + tt) % 2
            ak = ctx.run("ac", lambda e, di=di, tt=tt, b=b: e.activation(
                GT[:, b * TTW:(b + 1) * TTW], ps, AF.Silu, bias=BOUT[:, di:di + 1]),
                after=[mm, st[0]], tok=True)
            st[0] = rgs.dma("sy", g_r[:, di, tt * TTW:(tt + 1) * TTW], GT[:, b * TTW:(b + 1) * TTW],
                            after=[ak])
            return st[0]
        t2 = emit_proj_fm(ctx, bufs, W["L0_ffn_Wout"], GB, 0, [(c * 128, 128) for c in range(NCH)],
                          ev_out, [gtok] + ptoks, din=64)
        toks = t2 + [st[0]]
        rgs.close()
    return toks


def emit_attn_pairs(ctx, bufs, cfg, deps):
    nc = ctx.nc
    toks = []
    with contextlib.ExitStack() as es:
        QB = es.enter_context(nc.sbuf_tensor(uname("at_qb"), [128, T], BF16))
        KB = es.enter_context(nc.sbuf_tensor(uname("at_kb"), [128, T], BF16))
        VB = es.enter_context(nc.sbuf_tensor(uname("at_vb"), [128, T], BF16))
        MAXK = cfg.get("maxk", T)
        PB = es.enter_context(nc.sbuf_tensor(uname("at_pb"), [128, 4 * MAXK], BF16))
        AOD = cfg.get("ao_dram")
        AOP = es.enter_context(nc.sbuf_tensor(uname("at_aop"), [128, T], BF16)) if AOD is not None else None
        rao = Ring(ctx, 2) if AOD is not None else None
        PT = es.enter_context(nc.sbuf_tensor(uname("at_pt"), [128, 512], BF16))
        NM = [es.enter_context(nc.sbuf_tensor(uname("at_nm"), [128, 16], F32)) for _ in range(2)]
        SM = [es.enter_context(nc.sbuf_tensor(uname("at_sm"), [128, 16], F32)) for _ in range(2)]
        FC = [es.enter_context(nc.sbuf_tensor(uname("at_fc"), [128, 16], F32)) for _ in range(2)]
        WSB = [es.enter_context(nc.sbuf_tensor(uname(f"at_w{i}"), [128, NCH * 128], BF16)) for i in range(3)]
        psq = [es.enter_context(nc.psum_tensor(uname(f"at_psq{i}"), [128, TTW], F32)) for i in range(2)]
        psS = [es.enter_context(nc.psum_tensor(uname(f"at_s{i}"), [128, 1024], F32)) for i in range(2)]
        psT = es.enter_context(nc.psum_tensor(uname("at_pt_ps"), [128, 1024], BF16))
        psO = es.enter_context(nc.psum_tensor(uname("at_o"), [128, 512], F32))

        SC = 1024
        rw = Ring(ctx, 3, sw=True)
        zf = None
        for bb in (NM[0], NM[1], SM[0], SM[1], FC[0], FC[1]):
            zf = ctx.run("dv", lambda e, bb=bb: e.memset(bb[:, :], 1.0), after=deps, tok=zf is None or True)
        prev_pair_toks = list(deps) + [zf]
        for pair in range(NPAIR):
            wq = wslab_load(ctx, rw, WSB[0], cfg["wq_ap"], cfg.get("wq_off", 0) + pair * 128, 128, after=prev_pair_toks)
            wk = wslab_load(ctx, rw, WSB[1], cfg["wk_ap"], cfg.get("wk_off", 0) + pair * 128, 128, after=prev_pair_toks)
            wv = wslab_load(ctx, rw, WSB[2], cfg["wv_ap"], cfg.get("wv_off", 0) + pair * 128, 128, after=prev_pair_toks)
            qtoks = []
            evq = {0: None, 1: None}
            for tt in range(NTT):
                pb = tt % 2
                mmq = None
                for kc in range(NCH):
                    mmq = ctx.run("te", lambda e, kc=kc, tt=tt, pb=pb: e.matmul(
                        psq[pb][:, :], WSB[0][:, kc * 128:(kc + 1) * 128],
                        bufs["XN"][:, kc * T + tt * TTW: kc * T + tt * TTW + TTW],
                        start=(kc == 0), stop=(kc == NCH - 1)),
                        after=[wq, evq[pb]] + prev_pair_toks, tok=(kc == NCH - 1))
                evq[pb] = cfg["q_evac"](pair, tt, psq[pb][:, :], mmq, QB[:, tt * TTW:(tt + 1) * TTW])
                qtoks.append(evq[pb])
                mmk = None
                for kc in range(NCH):
                    mmk = ctx.run("te", lambda e, kc=kc, tt=tt, pb=pb: e.matmul(
                        psq[1 - pb][:, :], WSB[1][:, kc * 128:(kc + 1) * 128],
                        bufs["XN"][:, kc * T + tt * TTW: kc * T + tt * TTW + TTW],
                        start=(kc == 0), stop=(kc == NCH - 1)),
                        after=[wk, evq[1 - pb]] + prev_pair_toks, tok=(kc == NCH - 1))
                evq[1 - pb] = cfg["k_evac"](pair, tt, psq[1 - pb][:, :], mmk, KB[:, tt * TTW:(tt + 1) * TTW])
                qtoks.append(evq[1 - pb])
            vtoks = []
            vbias = cfg.get("v_bias")
            for tb in range(NTB):
                pb = tb % 2
                mmv = None
                for kc in range(NCH):
                    mmv = ctx.run("te", lambda e, kc=kc, tb=tb, pb=pb: e.matmul(
                        psq[pb][:, 0:128],
                        bufs["XN"][:, kc * T + tb * 128: kc * T + tb * 128 + 128],
                        WSB[2][:, kc * 128:(kc + 1) * 128],
                        start=(kc == 0), stop=(kc == NCH - 1) and vbias is None),
                        after=[wv, evq[pb]], tok=(vbias is None and kc == NCH - 1))
                if vbias is not None:
                    mmv = ctx.run("te", lambda e, pb=pb: e.matmul(
                        psq[pb][:, 0:128], vbias[0], vbias[1](pair),
                        start=False, stop=True), after=[], tok=True)
                evq[pb] = cfg["v_evac"](pair, tb, psq[pb][:, 0:128], mmv, VB[:, tb * 128:(tb + 1) * 128])
                vtoks.append(evq[pb])
            if cfg.get("qkln"):
                qtoks = cfg["qkln"](pair, QB, KB, qtoks, psq, psS)
            pend = None
    
            def stage_a(qb):
                k0, klen, masks = cfg["kset"](qb)
                par = qb % 2
                nsc = (klen + SC - 1) // SC
                sctoks = []
                for h in range(2):
                    hb = 64 * h
                    for sc in range(nsc):
                        scw = min(SC, klen - sc * SC)
                        sb = sc % 2
                        mm = None
                        nkt = (scw + 511) // 512
                        for kt in range(nkt):
                            nw = min(512, scw - kt * 512)
                            mm = ctx.run("te", lambda e, hb=hb, sb=sb, kt=kt, nw=nw, sc=sc, qb=qb, k0=k0: e.matmul(
                                psS[sb][:, kt * 512:kt * 512 + nw],
                                QB[hb:hb + 64, qb * 128:qb * 128 + 128],
                                KB[hb:hb + 64, k0 + sc * SC + kt * 512: k0 + sc * SC + kt * 512 + nw],
                                start=True, stop=True),
                                after=qtoks + [expo.get(sb)], tok=(kt == nkt - 1))
                        mtok = mm
                        for (moff, w, m_ap) in masks:
                            if sc * SC <= moff < sc * SC + scw:
                                mtok = ctx.run("dv", lambda e, moff=moff, w=w, m_ap=m_ap, sb=sb, sc=sc: e.tensor_add(
                                    psS[sb][:, moff - sc * SC: moff - sc * SC + w],
                                    psS[sb][:, moff - sc * SC: moff - sc * SC + w], m_ap),
                                    after=[mm], tok=True)
                        col = h * 4 + sc
                        nmt = ctx.run("dv", lambda e, sb=sb, scw=scw, col=col, par=par: e.tensor_reduce(
                            NM[par][:, col:col + 1], psS[sb][:, 0:scw], AX.X, ALU.max, negate=True),
                            after=[mtok, ftok.get(par)], tok=True)
                        ext = ctx.run("ac", lambda e, h=h, sb=sb, sc=sc, scw=scw, col=col, par=par, qb=qb: e.activation(
                            PB[:, par * 2 * MAXK + h * MAXK + sc * SC: par * 2 * MAXK + h * MAXK + sc * SC + scw],
                            psS[sb][:, 0:scw],
                            AF.Exp, bias=NM[par][:, col:col + 1], accum_out=SM[par][:, col:col + 1]),
                            after=[nmt, pvdone.get(par)], tok=True)
                        expo[sb] = ext
                        sctoks.append(ext)
                return (qb, k0, klen, nsc, sctoks)
    
            def stage_b(st):
                qb, k0, klen, nsc, sctoks = st
                par = qb % 2
                if nsc == 1:
                    f1 = ctx.run("dv", lambda e, par=par: e.reciprocal(
                        FC[par][:, 0:1], SM[par][:, 0:1]), after=sctoks, tok=False)
                    f1 = ctx.run("dv", lambda e, par=par: e.reciprocal(
                        FC[par][:, 4:5], SM[par][:, 4:5]), after=[], tok=True)
                    ftoks = [f1]
                else:
                    t = ctx.run("dv", lambda e, par=par, nsc=nsc: e.tensor_reduce(
                        FC[par][:, 14:16].rearrange("p (a x) -> p a x", a=2) if False else FC[par][:, 14:16],
                        NM[par][:, 0:8].rearrange("p (h s) -> p h s", h=2)[:, :, 0:nsc],
                        AX.X, ALU.min),
                        after=sctoks, tok=True)
                    tno = None
                    for sc in range(nsc):
                        tno = ctx.run("dv", lambda e, par=par, sc=sc: e.tensor_sub(
                            FC[par][:, sc:sc + 1].rearrange("p x -> p x") if False else FC[par][:, sc:sc + 1],
                            FC[par][:, 14:15], NM[par][:, sc:sc + 1]),
                            after=[t], tok=False)
                        tno = ctx.run("dv", lambda e, par=par, sc=sc: e.tensor_sub(
                            FC[par][:, 4 + sc:5 + sc], FC[par][:, 15:16], NM[par][:, 4 + sc:5 + sc]),
                            after=[], tok=(sc == nsc - 1))
                    t3 = ctx.run("ac", lambda e, par=par: e.activation(
                        FC[par][:, 0:8], FC[par][:, 0:8], AF.Exp), after=[tno], tok=True)
                    t4 = ctx.run("dv", lambda e, par=par: e.tensor_mul(
                        SM[par][:, 0:8], SM[par][:, 0:8], FC[par][:, 0:8]), after=[t3], tok=True)
                    t5 = ctx.run("dv", lambda e, par=par, nsc=nsc: e.tensor_reduce(
                        SM[par][:, 14:16], SM[par][:, 0:8].rearrange("p (h s) -> p h s", h=2)[:, :, 0:nsc],
                        AX.X, ALU.add), after=[t4], tok=True)
                    t6 = ctx.run("dv", lambda e, par=par: e.reciprocal(
                        SM[par][:, 12:14], SM[par][:, 14:16]), after=[t5], tok=True)
                    t7 = ctx.run("dv", lambda e, par=par: e.tensor_scalar(
                        FC[par][:, 0:4], FC[par][:, 0:4], SM[par][:, 12:13], None, ALU.mult),
                        after=[t6], tok=False)
                    t7 = ctx.run("dv", lambda e, par=par: e.tensor_scalar(
                        FC[par][:, 4:8], FC[par][:, 4:8], SM[par][:, 13:14], None, ALU.mult),
                        after=[], tok=True)
                    ftoks = [t7]
                ftok[par] = ftoks[-1]
                lastpv = None
                for h in range(2):
                    hb = 64 * h
                    rs = []
                    for sc in range(nsc):
                        scw = min(SC, klen - sc * SC)
                        rs.append(ctx.run("dv", lambda e, h=h, sc=sc, scw=scw, par=par: e.tensor_scalar(
                            PB[:, par * 2 * MAXK + h * MAXK + sc * SC: par * 2 * MAXK + h * MAXK + sc * SC + scw],
                            PB[:, par * 2 * MAXK + h * MAXK + sc * SC: par * 2 * MAXK + h * MAXK + sc * SC + scw],
                            FC[par][:, h * 4 + sc:h * 4 + sc + 1], None, ALU.mult),
                            after=ftoks, tok=(sc == nsc - 1)))
                    nkc = klen // 128
                    mmo = None
                    for g0 in range(0, nkc, 4):
                        gn = min(4, nkc - g0)
                        ttk = None
                        for kc in range(g0, g0 + gn):
                            ttk = ctx.run("te", lambda e, h=h, kc=kc, par=par: e.transpose(
                                psT[:, (kc % 4) * 128: (kc % 4) * 128 + 128],
                                PB[:, par * 2 * MAXK + h * MAXK + kc * 128: par * 2 * MAXK + h * MAXK + kc * 128 + 128],
                                bufs["identb"][:, :]),
                                after=[rs[-1], ptev.get(0)], tok=(kc == g0 + gn - 1))
                        pte = ctx.run("dv" if (g0 // 4) % 2 == 0 else "ac",
                                      lambda e, gn=gn, g0=g0: (
                            e.tensor_copy(PT[:, 0:gn * 128], psT[:, 0:gn * 128]) if (g0 // 4) % 2 == 0 else
                            e.copy(PT[:, 0:gn * 128], psT[:, 0:gn * 128])),
                            after=[ttk, mmo], tok=True)
                        ptev[0] = pte
                        for kc in range(g0, g0 + gn):
                            mmo = ctx.run("te", lambda e, h=h, hb=hb, kc=kc, k0=k0, nkc=nkc: e.matmul(
                                psO[hb:hb + 64, h * 128:h * 128 + 128],
                                VB[:, (k0 // 128 + kc) * 128 + hb: (k0 // 128 + kc) * 128 + hb + 64],
                                PT[:, (kc % 4) * 128: (kc % 4) * 128 + 128],
                                start=(kc == 0), stop=(kc == nkc - 1)),
                                after=[pte, evo[0], evo[1]], tok=(kc == nkc - 1))
                    evo[h] = ctx.run("dv" if h == 0 else "ac",
                                     lambda e, h=h, hb=hb, qb=qb, pair=pair, mmo=mmo: (
                        e.tensor_copy(cfg["AO"][hb:hb + 64, ao_base + qb * 128: ao_base + qb * 128 + 128],
                                      psO[hb:hb + 64, h * 128:h * 128 + 128]) if h == 0 else
                        e.copy(cfg["AO"][hb:hb + 64, ao_base + qb * 128: ao_base + qb * 128 + 128],
                               psO[hb:hb + 64, h * 128:h * 128 + 128])),
                        after=[mmo], tok=True)
                    lastpv = mmo
                pvdone[par] = evo[1]
    
            expo = {}
            evo = {0: None, 1: None}
            ftok = {}
            pvdone = {}
            ptev = {}
            if AOD is not None:
                cfg["AO"] = AOP
                ao_base = 0
            else:
                ao_base = pair * T
            pend = None
            for qb in range(NTB):
                st = stage_a(qb)
                if pend is not None:
                    stage_b(pend)
                pend = st
            stage_b(pend)
            pair_final = [evo[0], evo[1]]
            if AOD is not None:
                aot = rao.dma("sy", AOD.rearrange("(ch p) t -> p ch t", p=128)[:, pair, :],
                              AOP[:, :], after=pair_final)
                pair_final = pair_final + [aot]
            prev_pair_toks = pair_final + [qtoks[-1], vtoks[-1]]
            toks.extend(pair_final)
        rw.close()
    return toks[-4:]


def emit_outproj_combine(ctx, bufs, w_ap, ao_sb, xres_in, xres_out, g_dram, deps,
                         bias_col=None, add_xn=False, xn_dram=None, ao_dram=None):
    nc = ctx.nc
    xin_r = xres_in.rearrange("(ch p) t -> p ch t", p=128)
    xout_r = xres_out.rearrange("(ch p) t -> p ch t", p=128)
    g_r = g_dram.rearrange("(ch p) t -> p ch t", p=128)
    with contextlib.ExitStack() as es:
        XTI = es.enter_context(nc.sbuf_tensor(uname("oc_xi"), [128, 4 * TTW], F32))
        GTI = es.enter_context(nc.sbuf_tensor(uname("oc_gi"), [128, 4 * TTW], BF16))
        TMP = es.enter_context(nc.sbuf_tensor(uname("oc_tmp"), [128, 2 * TTW], F32))
        XO = es.enter_context(nc.sbuf_tensor(uname("oc_xo"), [128, 2 * TTW], F32))
        ld = {}
        stw = [None, None]
        rx = Ring(ctx, 2)
        rg = Ring(ctx, 2)
        rs = Ring(ctx, 2)
        rxn = Ring(ctx, 2) if xn_dram is not None else None
        XNI = es.enter_context(nc.sbuf_tensor(uname("oc_xn"), [128, 4 * TTW], BF16)) if xn_dram is not None else None
        xn_r = xn_dram.rearrange("(ch p) t -> p ch t", p=128) if xn_dram is not None else None

        def prefetch(idx, after):
            di, tt = idx // NTT, idx % NTT
            b = idx % 4
            t1 = rx.dma("sy", XTI[:, b * TTW:(b + 1) * TTW], xin_r[:, di, tt * TTW:(tt + 1) * TTW],
                        after=after)
            t2 = rg.dma("sy", GTI[:, b * TTW:(b + 1) * TTW], g_r[:, di, tt * TTW:(tt + 1) * TTW],
                        after=after)
            t3 = None
            if xn_dram is not None:
                t3 = rxn.dma("sy", XNI[:, b * TTW:(b + 1) * TTW], xn_r[:, di, tt * TTW:(tt + 1) * TTW],
                             after=after)
            ld[idx] = (t1, t2, b, t3)

        prefetch(0, deps)
        prefetch(1, deps)

        def evac(di, tt, ps, mm):
            idx = di * NTT + tt
            if idx + 2 < NCH * NTT:
                prefetch(idx + 2, [stw[idx % 2]])
            t1, t2, b, t3 = ld[idx]
            pb = idx % 2
            seq = [mm, t1, t2, t3]
            if xn_dram is not None:
                tk = ctx.run("dv", lambda e, pb=pb, b=b: e.tensor_add(
                    TMP[:, pb * TTW:(pb + 1) * TTW], ps, XNI[:, b * TTW:(b + 1) * TTW]),
                    after=seq, tok=True)
                tk = ctx.run("dv", lambda e, pb=pb, b=b: e.tensor_mul(
                    TMP[:, pb * TTW:(pb + 1) * TTW], TMP[:, pb * TTW:(pb + 1) * TTW],
                    GTI[:, b * TTW:(b + 1) * TTW]),
                    after=[tk], tok=True)
            elif add_xn:
                tk = ctx.run("dv", lambda e, di=di, tt=tt, pb=pb, b=b: e.tensor_add(
                    TMP[:, pb * TTW:(pb + 1) * TTW], ps,
                    bufs["XN"][:, di * T + tt * TTW: di * T + tt * TTW + TTW]),
                    after=seq, tok=True)
                tk = ctx.run("dv", lambda e, pb=pb, b=b: e.tensor_mul(
                    TMP[:, pb * TTW:(pb + 1) * TTW], TMP[:, pb * TTW:(pb + 1) * TTW],
                    GTI[:, b * TTW:(b + 1) * TTW]),
                    after=[tk], tok=True)
            elif bias_col is not None:
                tk = ctx.run("dv", lambda e, di=di, pb=pb, b=b: e.scalar_tensor_tensor(
                    TMP[:, pb * TTW:(pb + 1) * TTW], ps, bias_col[:, di:di + 1],
                    GTI[:, b * TTW:(b + 1) * TTW], ALU.add, ALU.mult),
                    after=seq, tok=True)
            else:
                tk = ctx.run("dv", lambda e, pb=pb, b=b: e.tensor_mul(
                    TMP[:, pb * TTW:(pb + 1) * TTW], ps, GTI[:, b * TTW:(b + 1) * TTW]),
                    after=seq, tok=True)
            tk2 = ctx.run("dv", lambda e, pb=pb, b=b: e.tensor_add(
                XO[:, pb * TTW:(pb + 1) * TTW], TMP[:, pb * TTW:(pb + 1) * TTW],
                XTI[:, b * TTW:(b + 1) * TTW]),
                after=[tk, stw[pb]], tok=True)
            stw[pb] = rs.dma("sy", xout_r[:, di, tt * TTW:(tt + 1) * TTW], XO[:, pb * TTW:(pb + 1) * TTW],
                             after=[tk2])
            return stw[pb]

        if ao_dram is None:
            toks = emit_proj_fm(ctx, bufs, w_ap, ao_sb, 0, [(c * 128, 128) for c in range(NCH)],
                                evac, deps)
        else:
            ao_r2 = ao_dram.rearrange("(ch p) t -> p ch t", p=128)
            MV = es.enter_context(nc.sbuf_tensor(uname("oc_mv"), [128, 2 * TTW], BF16))
            WSB2 = [es.enter_context(nc.sbuf_tensor(uname("oc_w"), [128, NCH * 128], BF16)) for _ in range(2)]
            pss2 = [es.enter_context(nc.psum_tensor(uname("oc_ps"), [128, TTW], F32)) for _ in range(2)]
            rmw = Ring(ctx, 2, sw=True)
            rmv = Ring(ctx, 2)
            wld = [None, None]
            lastmm = [None, None]
            evt = [None, None]
            mvl = {}
            for di in range(NCH):
                wb = di % 2
                wld[wb] = wslab_load(ctx, rmw, WSB2[wb], w_ap, di * 128, 128,
                                     after=[lastmm[wb]] + (list(deps) if di < 2 else []))
                for tt in range(NTT):
                    pb = tt % 2
                    mm = None
                    for kc in range(NCH):
                        mb = (tt * NCH + kc) % 2
                        mvt = rmv.dma("sy", MV[:, mb * TTW:(mb + 1) * TTW],
                                      ao_r2[:, kc, tt * TTW:(tt + 1) * TTW],
                                      after=[mvl.get(mb)] + (list(deps) if di == 0 and tt == 0 and kc < 2 else []))
                        mm = ctx.run("te", lambda e, kc=kc, wb=wb, pb=pb, mb=mb: e.matmul(
                            pss2[pb][:, :], WSB2[wb][:, kc * 128:(kc + 1) * 128],
                            MV[:, mb * TTW:(mb + 1) * TTW],
                            start=(kc == 0), stop=(kc == NCH - 1)),
                            after=[wld[wb], evt[pb], mvt], tok=True)
                        mvl[mb] = mm
                    lastmm[wb] = mm
                    evt[pb] = evac(di, tt, pss2[pb][:, :], mm)
            rmw.close(); rmv.close()
        rx.close(); rg.close(); rs.close()
        if rxn is not None:
            rxn.close()
    return [stw[0], stw[1]]


def emit_layer0(ctx, bufs, W, xres_in, xres_out, scratch, deps):
    nc = ctx.nc
    les = ctx.es.enter_context(contextlib.ExitStack())
    bufs["XN"] = les.enter_context(nc.sbuf_tensor(uname("XN"), [128, NCH * T], BF16))
    bufs["AO"] = les.enter_context(nc.sbuf_tensor(uname("AO"), [128, NCH * T], BF16))
    g_dram = scratch["g"]
    t_ln = emit_ln(ctx, bufs, xres_in, deps)
    t_g = emit_l0ffn(ctx, bufs, W, t_ln, g_dram)

    QLN = bufs["qln"]
    AO = bufs["AO"]

    def q_evac(pair, tt, ps, mm, dst):
        return ctx.run("dv", lambda e, ps=ps, dst=dst: e.tensor_copy(dst, ps), after=[mm], tok=True)
    k_evac = q_evac

    def v_evac(pair, tb, ps, mm, dst):
        return ctx.run("ac", lambda e, ps=ps, dst=dst: e.copy(dst, ps), after=[mm], tok=True)

    def qkln(pair, QB, KB, qtoks, psq, psS):
        ntoks = []
        with contextlib.ExitStack() as es:
            SQ = es.enter_context(nc.sbuf_tensor(uname("qk_sq"), [128, TTW], BF16))
            MU = es.enter_context(nc.sbuf_tensor(uname("qk_mu"), [128, TTW], F32))
            VARB = es.enter_context(nc.sbuf_tensor(uname("qk_var"), [128, TTW], F32))
            RST = es.enter_context(nc.sbuf_tensor(uname("qk_rst"), [128, TTW], F32))
            prev = {}
            for which, BUF, wcol, bcol, scl in (("q", QB, QLN[:, 0:1], QLN[:, 1:2], 0.125),
                                               ("k", KB, QLN[:, 2:3], QLN[:, 3:4], 1.0)):
                for tt in range(NTT):
                    t0 = tt * TTW
                    sqt = ctx.run("ac", lambda e, BUF=BUF, t0=t0: e.activation(
                        SQ[:, :], BUF[:, t0:t0 + TTW], AF.Square),
                        after=qtoks + [prev.get("mm")], tok=True)
                    mm1 = None
                    mm2 = None
                    for h in range(2):
                        hb = 64 * h
                        mm1 = ctx.run("te", lambda e, hb=hb, BUF=BUF, t0=t0: e.matmul(
                            psS[0][hb:hb + 64, 0:TTW], bufs["inv64"][hb:hb + 64, 0:64],
                            BUF[hb:hb + 64, t0:t0 + TTW], start=True, stop=True),
                            after=qtoks + [prev.get("e1")], tok=(h == 1))
                        mm2 = ctx.run("te", lambda e, hb=hb, t0=t0: e.matmul(
                            psS[1][hb:hb + 64, 0:TTW], bufs["inv64"][hb:hb + 64, 0:64],
                            SQ[hb:hb + 64, 0:TTW], start=True, stop=True),
                            after=[sqt, prev.get("e3")], tok=(h == 1))
                    prev["mm"] = mm2
                    e1 = ctx.run("dv", lambda e: e.tensor_copy(MU[:, :], psS[0][:, 0:TTW]),
                                 after=[mm1, prev.get("s3"), prev.get("e2")], tok=True)
                    prev["e1"] = e1
                    e2 = ctx.run("ac", lambda e: e.activation(VARB[:, :], MU[:, :], AF.Square),
                                 after=[e1, prev.get("e5")], tok=True)
                    prev["e2"] = e2
                    e3 = ctx.run("dv", lambda e: e.tensor_sub(VARB[:, :], psS[1][:, 0:TTW], VARB[:, :]),
                                 after=[mm2, e2], tok=True)
                    prev["e3"] = e3
                    e4 = ctx.run("ac", lambda e: e.activation(VARB[:, :], VARB[:, :], AF.Ln, bias=1e-5),
                                 after=[e3], tok=True)
                    e5 = ctx.run("ac", lambda e: e.activation(RST[:, :], VARB[:, :], AF.Exp, scale=-0.5),
                                 after=[e4, prev.get("s3")], tok=True)
                    prev["e5"] = e5
                    s1 = ctx.run("dv", lambda e, BUF=BUF, t0=t0: e.tensor_sub(
                        MU[:, :], BUF[:, t0:t0 + TTW], MU[:, :]), after=[e5], tok=True)
                    s2 = ctx.run("dv", lambda e, t0=t0: e.tensor_mul(MU[:, :], MU[:, :], RST[:, :]),
                                 after=[s1], tok=True)
                    if scl == 1.0:
                        s3 = ctx.run("dv", lambda e, BUF=BUF, t0=t0, wcol=wcol, bcol=bcol: e.tensor_scalar(
                            BUF[:, t0:t0 + TTW], MU[:, :], wcol, bcol, ALU.mult, ALU.add),
                            after=[s2], tok=True)
                        prev["s3"] = s3
                    else:
                        s3 = ctx.run("dv", lambda e, BUF=BUF, t0=t0, wcol=wcol, bcol=bcol: e.tensor_scalar(
                            BUF[:, t0:t0 + TTW], MU[:, :], QLN[:, 4:5], QLN[:, 5:6], ALU.mult, ALU.add),
                            after=[s2], tok=True)
                    prev["s3"] = s3
                    ntoks.append(s3)
        return ntoks

    def kset(qb):
        klen = (qb + 1) * 128
        return (0, klen, [(klen - 128, 128, bufs["m_tri"][:, :])])

    cfg = dict(wq_ap=W["L0_attn_WqT"], wk_ap=W["L0_attn_WkT"], wv_ap=W["L0_attn_WvT"],
               q_evac=q_evac, k_evac=k_evac, v_evac=v_evac, qkln=qkln, kset=kset, AO=AO)
    t_at = emit_attn_pairs(ctx, bufs, cfg, t_ln + t_g)
    t_oc = emit_outproj_combine(ctx, bufs, W["L0_attn_WuT"], AO, xres_in, xres_out,
                                g_dram, t_at + t_g)
    les.close()
    return t_oc




def emit_l1_router(ctx, bufs, W, deps):
    nc = ctx.nc
    CWF = bufs["CWF"]
    toks = []
    with contextlib.ExitStack() as es:
        LGT = es.enter_context(nc.sbuf_tensor(uname("r1_lg"), [128, T], F32))
        LTM = es.enter_context(nc.sbuf_tensor(uname("r1_ltm"), [128, 8], F32))
        SML = es.enter_context(nc.sbuf_tensor(uname("r1_sml"), [128, 16], F32))
        pst = es.enter_context(nc.psum_tensor(uname("r1_ps"), [128, 512], F32))
        def ev(di, tt, ps, mm):
            return ctx.run("dv", lambda e, tt=tt, ps=ps: e.tensor_scalar(
                LGT[0:8, tt * TTW:(tt + 1) * TTW], ps, bufs["l1rb"][0:8, 0:1], None, ALU.add),
                after=[mm], tok=True)
        t1 = emit_proj_fm(ctx, bufs, W["l1_routerT"], bufs["XN"], 0, [(0, 8)], ev, deps)
        prev = {}
        for tb in range(NTB):
            tk1 = ctx.run("te", lambda e, tb=tb: e.transpose(
                pst[:, 0:8], LGT[0:8, tb * 128:(tb + 1) * 128], bufs["identf"][0:8, 0:8]),
                after=t1 + [prev.get("ev1"), prev.get("ev2")], tok=True)
            ev1 = ctx.run("dv", lambda e: e.tensor_copy(LTM[:, 0:8], pst[:, 0:8]),
                          after=[tk1, prev.get("cw")], tok=True)
            m1 = ctx.run("dv", lambda e: e.tensor_reduce(SML[:, 0:1], LTM[:, 0:8], AX.X, ALU.max),
                         after=[ev1], tok=True)
            eq1 = ctx.run("dv", lambda e: e.tensor_scalar(SML[:, 2:10], LTM[:, 0:8], SML[:, 0:1], None, ALU.is_equal),
                          after=[m1], tok=True)
            l2 = ctx.run("dv", lambda e: e.scalar_tensor_tensor(LTM[:, 0:8], SML[:, 2:10], NEG, LTM[:, 0:8],
                                                                ALU.mult, ALU.add),
                         after=[eq1], tok=True)
            m2 = ctx.run("dv", lambda e: e.tensor_reduce(SML[:, 1:2], LTM[:, 0:8], AX.X, ALU.max),
                         after=[l2], tok=True)
            eq2 = ctx.run("dv", lambda e: e.tensor_scalar(LTM[:, 0:8], LTM[:, 0:8], SML[:, 1:2], None, ALU.is_equal),
                          after=[m2], tok=True)
            dd = ctx.run("dv", lambda e: e.tensor_sub(SML[:, 1:2], SML[:, 0:1], SML[:, 1:2]), after=[m2, eq2], tok=True)
            w1 = ctx.run("ac", lambda e: e.activation(SML[:, 0:1], SML[:, 1:2], AF.Sigmoid), after=[dd], tok=True)
            w2 = ctx.run("dv", lambda e: e.tensor_scalar(SML[:, 1:2], SML[:, 0:1], -1.0, 1.0, ALU.mult, ALU.add),
                         after=[w1], tok=True)
            c1 = ctx.run("dv", lambda e: e.tensor_scalar(SML[:, 2:10], SML[:, 2:10], SML[:, 0:1], None, ALU.mult),
                         after=[w2], tok=True)
            cw = ctx.run("dv", lambda e: e.scalar_tensor_tensor(LTM[:, 0:8], LTM[:, 0:8], SML[:, 1:2], SML[:, 2:10],
                                                                ALU.mult, ALU.add),
                         after=[c1], tok=True)
            prev["cw"] = cw
            tk2 = ctx.run("te", lambda e, tb=tb: e.transpose(
                pst[0:8, 128:256], LTM[:, 0:8], bufs["identf"][:, :]),
                after=[cw, prev.get("ev2")], tok=True)
            ev2 = ctx.run("dv", lambda e, tb=tb: e.tensor_copy(
                CWF[0:8, tb * 128:(tb + 1) * 128], pst[0:8, 128:256]),
                after=[tk2], tok=True)
            prev["ev1"] = ev1
            prev["ev2"] = ev2
            toks = [ev2]
    return toks


def emit_l1_moe(ctx, bufs, W, deps, xn_dram, gacc_dram):
    nc = ctx.nc
    CWF = bufs["CWF"]
    NHC = 32
    gacc_r = gacc_dram.rearrange("(ch p) t -> p ch t", p=128)
    xn_r = xn_dram.rearrange("(ch p) t -> p ch t", p=128)
    toks = []
    with contextlib.ExitStack() as es:
        W1S = es.enter_context(nc.sbuf_tensor(uname("m_w1"), [128, NCH * 4096], BF16))
        W2S = es.enter_context(nc.sbuf_tensor(uname("m_w2"), [128, NHC * 1024], BF16))
        Y1 = es.enter_context(nc.sbuf_tensor(uname("m_y1"), [128, NHC * 512], BF16))
        XT = es.enter_context(nc.sbuf_tensor(uname("m_xt"), [128, 2 * NCH * 512], BF16))
        CWB = es.enter_context(nc.sbuf_tensor(uname("m_cwb"), [128, 2 * 512], F32))
        GOB = es.enter_context(nc.sbuf_tensor(uname("m_gob"), [128, 2 * 512], BF16))
        pso = [es.enter_context(nc.psum_tensor(uname("m_ps"), [128, 512], F32)) for _ in range(8)]
        rw1 = Ring(ctx, 1, sw=True)
        rw2 = Ring(ctx, 1, sw=True)
        rxt = Ring(ctx, 2)
        rgo = Ring(ctx, 2, sw=True)
        w1tok = None
        w2tok = None
        w1mm = None
        w2mm = None
        xt_t = [None, None]
        xt_guard = {}
        cwb_guard = {}
        prev_y2 = [None]
        y1_t = {}
        cwb_t = [None, None]
        go_t = [None, None]
        evo = {}
        for e_i in range(NEXP):
            w1tok = rw1.dma("gp",
                            W1S[:, :].rearrange("p (kc m) -> p kc m", m=4096),
                            W["l1_w1T"][e_i].rearrange("(kc p) m -> p kc m", p=128),
                            after=[w1mm] + (list(deps) if e_i == 0 else []))
            w2tok = rw2.dma("gp",
                            W2S[:, :].rearrange("p (kc m) -> p kc m", m=1024),
                            W["l1_w2T"][e_i].rearrange("(kc p) m -> p kc m", p=128),
                            after=[w2mm] + (list(deps) if e_i == 0 else []))
            for ts_ in range(NTT):
                t0 = ts_ * TTW
                xb = ts_ % 2
                xt_t[xb] = rxt.dma("sy", XT[:, xb * 4096:(xb + 1) * 4096].rearrange("p (ch m) -> p ch m", m=512),
                                   xn_r[:, :, t0:t0 + TTW],
                                   after=[xt_guard.get(xb)] + (list(deps) if e_i == 0 and ts_ < 2 else []))
                cb = ts_ % 2
                cmm = ctx.run("te", lambda e, e_i=e_i, t0=t0: e.matmul(
                    pso[2][:, :], bufs["l1sel"][0:8, e_i * 128:(e_i + 1) * 128], CWF[0:8, t0:t0 + TTW],
                    start=True, stop=True),
                    after=[cwb_t[cb], evo.get(2)] + list(deps if e_i == 0 and ts_ == 0 else []), tok=True)
                cwb_t[cb] = ctx.run("dv", lambda e, cb=cb: e.tensor_copy(CWB[:, cb * 512:(cb + 1) * 512], pso[2][:, :]),
                                    after=[cmm, cwb_guard.get(cb)], tok=True)
                lastmm = None
                for hc in range(NHC):
                    pb = hc % 2
                    mm = None
                    for kc in range(NCH):
                        mm = ctx.run("te", lambda e, hc=hc, kc=kc, pb=pb, xb=xb: e.matmul(
                            pso[pb][:, :], W1S[:, kc * 4096 + hc * 128: kc * 4096 + hc * 128 + 128],
                            XT[0:128, xb * 4096 + kc * 512: xb * 4096 + kc * 512 + 512],
                            start=(kc == 0), stop=(kc == NCH - 1)),
                            after=[w1tok, xt_t[xb], y1_t.get(pb), evo.get(pb)], tok=(kc == NCH - 1))
                    y1_t[pb] = ctx.run("ac", lambda e, hc=hc, pb=pb, e_i=e_i: e.activation(
                        Y1[:, hc * 512:(hc + 1) * 512], pso[pb][:, :], AF.Gelu,
                        bias=bufs["l1b1"][:, e_i * 32 + hc:e_i * 32 + hc + 1]),
                        after=[mm, prev_y2[0]], tok=True)
                    lastmm = mm
                xt_guard[xb] = lastmm
                w1mm = lastmm
                l2mm = None
                for oc in range(NCH):
                    for hc in range(NHC):
                        l2mm = ctx.run("te", lambda e, oc=oc, hc=hc: e.matmul(
                            pso[oc][:, :], W2S[:, hc * 1024 + oc * 128: hc * 1024 + oc * 128 + 128],
                            Y1[:, hc * 512:(hc + 1) * 512],
                            start=(hc == 0), stop=(hc == NHC - 1)),
                            after=[w2tok, y1_t[0], y1_t[1], evo.get(oc)] +
                                  ([cwb_t[0], cwb_t[1]] if oc == 2 and hc == 0 else []),
                            tok=(hc == NHC - 1))
                    gb = oc % 2
                    gev = ctx.run("dv", lambda e, oc=oc, gb=gb, cb=cb, e_i=e_i: e.scalar_tensor_tensor(
                        GOB[:, gb * 512:(gb + 1) * 512], pso[oc][:, :], bufs["l1b2"][:, e_i * 8 + oc:e_i * 8 + oc + 1],
                        CWB[:, cb * 512:(cb + 1) * 512], ALU.add, ALU.mult),
                        after=[l2mm, go_t[gb], cwb_t[cb]], tok=True)
                    go_t[gb] = ctx.run("gp", lambda e, oc=oc, gb=gb, t0=t0, e_i=e_i: e.dma_start(
                        gacc_r[:, oc, t0:t0 + TTW], GOB[:, gb * 512:(gb + 1) * 512],
                        accum_op=(ALU.bypass if e_i == 0 else ALU.add)),
                        after=[gev], tok=True, dma=True, dsem=rgo.sems[gb]) or None
                    go_t[gb] = (rgo.sems[gb], ctx.cnt[rgo.sems[gb]])
                    evo[oc] = go_t[gb]
                w2mm = l2mm
                prev_y2[0] = l2mm
                cwb_guard[cb] = gev
        out = [go_t[0], go_t[1]]
        rw1.close(); rw2.close(); rxt.close(); rgo.close()
    return out


def emit_layer1(ctx, bufs, W, xres_in, xres_out, scratch, deps):
    nc = ctx.nc
    g_dram = scratch["g"]
    xn_dram = scratch["xn"]
    gacc_dram = scratch["gacc"]
    lces = contextlib.ExitStack()
    bufs["CWF"] = lces.enter_context(nc.sbuf_tensor(uname("CWF"), [8, T], F32))
    esA = contextlib.ExitStack()
    bufs["XN"] = esA.enter_context(nc.sbuf_tensor(uname("XN"), [128, NCH * T], BF16))
    t_ln = emit_ln(ctx, bufs, xres_in, deps, xn_dram=xn_dram)
    t_rt = emit_l1_router(ctx, bufs, W, t_ln)
    esA.close()
    t_moe = emit_l1_moe(ctx, bufs, W, t_ln + t_rt, xn_dram, gacc_dram)
    esC = contextlib.ExitStack()
    GS = esC.enter_context(nc.sbuf_tensor(uname("c1_gs"), [128, 4096], BF16))
    GAI = esC.enter_context(nc.sbuf_tensor(uname("c1_ga"), [128, 4096], BF16))
    XNI = esC.enter_context(nc.sbuf_tensor(uname("c1_xn"), [128, 4096], BF16))
    GT = esC.enter_context(nc.sbuf_tensor(uname("c1_gt"), [128, 2 * 512], BF16))
    rga = Ring(ctx, 2)
    rxn = Ring(ctx, 2)
    rgw = Ring(ctx, 2)
    gacc_r = gacc_dram.rearrange("(ch p) t -> p ch t", p=128)
    xn_r = xn_dram.rearrange("(ch p) t -> p ch t", p=128)
    g_r = g_dram.rearrange("(ch p) t -> p ch t", p=128)
    stt_ = [None, None]
    addtok = [None]

    def produce(tt, after):
        t0 = tt * TTW
        l1_ = rga.dma("sy", GAI[:, :].rearrange("p (ch m) -> p ch m", m=512),
                      gacc_r[:, :, t0:t0 + TTW], after=after)
        l2_ = rxn.dma("sy", XNI[:, :].rearrange("p (ch m) -> p ch m", m=512),
                      xn_r[:, :, t0:t0 + TTW], after=after)
        at = ctx.run("dv", lambda e: e.tensor_add(GS[:, :], GAI[:, :], XNI[:, :]),
                     after=[l1_, l2_] + after, tok=True)
        addtok[0] = at
        return at, (lambda ch: GS[:, ch * 512:ch * 512 + 512])

    def consume(tt, ch, scr, rst, after):
        t0 = tt * TTW
        k1 = ctx.run("dv", lambda e, scr=scr, rst=rst: e.tensor_mul(scr, scr, rst), after=after, tok=True)
        k2 = ctx.run("dv", lambda e, ch=ch, scr=scr: e.tensor_scalar(
            scr, scr, bufs["l1ab"][:, ch:ch + 1], bufs["l1ab"][:, 8 + ch:8 + ch + 1], ALU.mult, ALU.add),
            after=[k1], tok=True)
        sb = ch % 2
        k3 = ctx.run("ac", lambda e, scr=scr, sb=sb: e.activation(
            GT[:, sb * 512:(sb + 1) * 512], scr, AF.Silu), after=[k2, stt_[sb]], tok=True)
        stt_[sb] = rgw.dma("sy", g_r[:, ch, t0:t0 + TTW], GT[:, sb * 512:(sb + 1) * 512], after=[k3])
        return stt_[sb]

    t_g = emit_lnx(ctx, bufs, t_moe, produce, consume, eps=1e-12) + [stt_[0], stt_[1]]
    rga.close(); rxn.close(); rgw.close()
    esC.close()
    esD = ctx.es.enter_context(contextlib.ExitStack())
    bufs["XN"] = esD.enter_context(nc.sbuf_tensor(uname("XN"), [128, NCH * T], BF16))
    bufs["AO"] = esD.enter_context(nc.sbuf_tensor(uname("AO"), [128, NCH * T], BF16))
    rxl = Ring(ctx, 2)

    def produceD(tt, after):
        t0 = tt * TTW
        lt = rxl.dma("sy", bufs["XN"][:, :].rearrange("p (ch t) -> p ch t", t=T)[:, :, t0:t0 + TTW],
                     xn_r[:, :, t0:t0 + TTW], after=after)
        return lt, (lambda ch, t0=t0: bufs["XN"][:, ch * T + t0: ch * T + t0 + TTW])

    def consumeD(tt, ch, scr, rst, after):
        t0 = tt * TTW
        k1 = ctx.run("dv", lambda e, scr=scr, rst=rst: e.tensor_mul(scr, scr, rst), after=after, tok=True)
        k2 = ctx.run("dv", lambda e, ch=ch, t0=t0, scr=scr: e.tensor_scalar(
            bufs["XN"][:, ch * T + t0: ch * T + t0 + TTW], scr,
            bufs["l1gb"][:, ch:ch + 1], bufs["l1gb"][:, 8 + ch:8 + ch + 1], ALU.mult, ALU.add),
            after=[k1], tok=True)
        return k2
    t_ln2 = emit_lnx(ctx, bufs, t_g, produceD, consumeD, eps=1e-5)
    rxl.close()

    def q_evac(pair, tt, ps, mm, dst):
        return ctx.run("dv", lambda e, ps=ps, dst=dst: e.tensor_scalar(dst, ps, 0.125, None, ALU.mult),
                       after=[mm], tok=True)

    def k_evac(pair, tt, ps, mm, dst):
        return ctx.run("dv", lambda e, ps=ps, dst=dst: e.tensor_copy(dst, ps), after=[mm], tok=True)

    def v_evac(pair, tb, ps, mm, dst):
        return ctx.run("ac", lambda e, ps=ps, dst=dst: e.copy(dst, ps), after=[mm], tok=True)

    def kset(qb):
        if qb == 0:
            return (0, 128, [(0, 128, bufs["m_tri"][:, :])])
        if qb == 1:
            return (0, 256, [(128, 128, bufs["m_tri"][:, :])])
        return ((qb - 2) * 128, 384, [(0, 128, bufs["m_win"][:, :]),
                                      (256, 128, bufs["m_tri"][:, :])])

    cfg = dict(wq_ap=W["l1_qkvT"], wk_ap=W["l1_qkvT"], wv_ap=W["l1_qkvT"],
               wq_off=0, wk_off=1024, wv_off=2048,
               q_evac=q_evac, k_evac=k_evac, v_evac=v_evac, qkln=None, kset=kset,
               AO=bufs["AO"], maxk=512)
    t_at = emit_attn_pairs(ctx, bufs, cfg, t_ln2)
    t_oc = emit_outproj_combine(ctx, bufs, W["l1_woT"], bufs["AO"], xres_in, xres_out,
                                g_dram, t_at + t_g, xn_dram=xn_dram)
    esD.close()
    lces.close()
    return t_oc


def decl_l1(nc):
    W = {}
    W["l1_routerT"] = nc.dram_tensor("l1_routerT", [D, 8], F32, kind="ExternalInput").ap()
    W["l1_qkvT"] = nc.dram_tensor("l1_qkvT", [D, 3 * D], F32, kind="ExternalInput").ap()
    W["l1_woT"] = nc.dram_tensor("l1_woT", [D, D], F32, kind="ExternalInput").ap()
    W["l1_w1T"] = nc.dram_tensor("l1_w1T", [8, D, 4096], F32, kind="ExternalInput").ap()
    W["l1_w2T"] = nc.dram_tensor("l1_w2T", [8, 4096, D], F32, kind="ExternalInput").ap()
    return W


def setup_l1_bufs(ctx, bufs, es, deps=()):
    nc = ctx.nc
    names = [("l1rb", [128, 1]), ("l1sel", [8, 8 * 128]), ("l1b1", [128, 256]),
             ("l1b2", [128, 64]), ("l1ab", [128, 16]), ("l1gb", [128, 16])]
    toks = []
    mr = misc_ring(ctx, "gp")
    for nm, shp in names:
        bufs[nm] = es.enter_context(nc.sbuf_tensor(uname(nm), shp, F32))
        ap = nc.dram_tensor("c_" + nm, shp, F32, kind="ExternalInput").ap()
        toks.append(mr.dma("gp", bufs[nm][:, :] if shp[0] == 128 else bufs[nm][0:shp[0], :], ap,
                           after=deps))
    bufs["m_win"] = es.enter_context(nc.sbuf_tensor(uname("m_win"), [128, 128], F32))
    ap = nc.dram_tensor("c_mwin", [128, 128], F32, kind="ExternalInput").ap()
    toks.append(mr.dma("gp", bufs["m_win"][:, :], ap, after=deps))
    return toks


def stage_l1(inputs):
    f32 = np.float32
    tr = lambda a: np.ascontiguousarray(np.asarray(a, f32).T)
    d = {
        "l1_routerT": tr(inputs["L1_ffn_router_w"]),
        "l1_qkvT": tr(inputs["L1_attn_W_qkv"]),
        "l1_woT": tr(inputs["L1_attn_W_out"]),
        "l1_w1T": np.ascontiguousarray(np.asarray(inputs["L1_ffn_w1"], f32).transpose(0, 2, 1)),
        "l1_w2T": np.ascontiguousarray(np.asarray(inputs["L1_ffn_w2"], f32).transpose(0, 2, 1)),
    }
    rb = np.zeros((128, 1), f32); rb[0:8, 0] = np.asarray(inputs["L1_ffn_router_b"], f32)
    d["c_l1rb"] = rb
    sel = np.zeros((8, 8 * 128), f32)
    for e in range(8):
        sel[e, e * 128:(e + 1) * 128] = 1.0
    d["c_l1sel"] = sel
    b1 = np.zeros((128, 256), f32)
    for e in range(8):
        b1[:, e * 32:(e + 1) * 32] = np.asarray(inputs["L1_ffn_b1"][e], f32).reshape(32, 128).T
    d["c_l1b1"] = b1
    b2 = np.zeros((128, 64), f32)
    for e in range(8):
        b2[:, e * 8:(e + 1) * 8] = np.asarray(inputs["L1_ffn_b2"][e], f32).reshape(8, 128).T
    d["c_l1b2"] = b2
    ab = np.zeros((128, 16), f32)
    ab[:, 0:8] = np.asarray(inputs["L1_ffn_ln_w"], f32).reshape(8, 128).T
    ab[:, 8:16] = np.asarray(inputs["L1_ffn_ln_b"], f32).reshape(8, 128).T
    d["c_l1ab"] = ab
    gb = np.zeros((128, 16), f32)
    gb[:, 0:8] = np.asarray(inputs["L1_attn_gamma"], f32).reshape(8, 128).T
    gb[:, 8:16] = np.asarray(inputs["L1_attn_beta"], f32).reshape(8, 128).T
    d["c_l1gb"] = gb
    col = np.arange(128)[None, :]; row = np.arange(128)[:, None]
    d["c_mwin"] = np.where(col >= row, 0.0, NEG).astype(f32)
    return d


def build_prog_l1():
    nc = bass.Bass("TRN2", target_bir_lowering=False)
    es = contextlib.ExitStack()
    W = decl_l1(nc)
    xres_in = nc.dram_tensor("xres_in", [D, T], F32, kind="ExternalInput").ap()
    xres_out = nc.dram_tensor("xres_out", [D, T], F32, kind="ExternalOutput").ap()
    g_dram = nc.dram_tensor("g_scr", [D, T], BF16).ap()
    xn_dram = nc.dram_tensor("xn_scr", [D, T], BF16).ap()
    gacc_dram = nc.dram_tensor("gacc_scr", [D, T], BF16).ap()
    with es:
        ctx, bufs, ctoks = build_common(nc, es)
        ltoks = setup_l1_bufs(ctx, bufs, es)
        ctx.global_deps = ctoks + ltoks
        t1 = emit_layer1(ctx, bufs, W, xres_in, xres_out,
                         {"g": g_dram, "xn": xn_dram, "gacc": gacc_dram}, [])
        ctx.finish_dma_waits("sy")
    return nc


def emit_l3ffn(ctx, bufs, W, deps, g_dram):
    nc = ctx.nc
    g_r = g_dram.rearrange("(ch p) t -> p ch t", p=128)
    NH = 24
    with contextlib.ExitStack() as es:
        HALL = es.enter_context(nc.sbuf_tensor(uname("f3_h"), [128, NH * 512], BF16))
        WG = [es.enter_context(nc.sbuf_tensor(uname("f3_wg"), [128, NCH * 128], BF16)) for _ in range(2)]
        WU = [es.enter_context(nc.sbuf_tensor(uname("f3_wu"), [128, NCH * 128], BF16)) for _ in range(2)]
        WD = [es.enter_context(nc.sbuf_tensor(uname("f3_wd"), [128, 1024], BF16)) for _ in range(2)]
        SG = es.enter_context(nc.sbuf_tensor(uname("f3_sg"), [128, 2 * 512], BF16))
        GT = es.enter_context(nc.sbuf_tensor(uname("f3_gt"), [128, 2 * 512], BF16))
        pso = [es.enter_context(nc.psum_tensor(uname("f3_ps"), [128, 512], F32)) for _ in range(8)]
        rg = Ring(ctx, 2, sw=True)
        ru = Ring(ctx, 2, sw=True)
        rd = Ring(ctx, 2, sw=True)
        rstg = Ring(ctx, 2)
        wg_l = [None, None]; wu_l = [None, None]; wd_l = [None, None]
        mmg = [None, None]; mmu = [None, None]; wdmm = [None, None]
        sg_t = [None, None]
        h_t = {}
        evo = {}
        stok = [None, None]
        p2last = None
        for tt in range(NTT):
            t0 = tt * TTW
            for hj in range(NH):
                b = hj % 2
                wg_l[b] = wslab_load(ctx, rg, WG[b], W["l3_wgT"], hj * 128, 128,
                                     after=[mmg[b]] + (list(deps) if tt == 0 and hj < 2 else []))
                wu_l[b] = wslab_load(ctx, ru, WU[b], W["l3_wuT"], hj * 128, 128,
                                     after=[mmu[b]] + (list(deps) if tt == 0 and hj < 2 else []))
                gmm = None
                for kc in range(NCH):
                    gmm = ctx.run("te", lambda e, kc=kc, b=b, t0=t0: e.matmul(
                        pso[0][:, :], WG[b][:, kc * 128:(kc + 1) * 128],
                        bufs["XN"][:, kc * T + t0: kc * T + t0 + TTW],
                        start=(kc == 0), stop=(kc == NCH - 1)),
                        after=[wg_l[b], sg_t[0], evo.get(0)], tok=(kc == NCH - 1))
                mmg[b] = gmm
                sg_t[0] = ctx.run("ac", lambda e, hj=hj: e.activation(
                    SG[:, (hj % 2) * 512:(hj % 2) * 512 + 512], pso[0][:, :], AF.Silu, scale=1.0 / 127.0),
                    after=[gmm, h_t.get(hj % 2)], tok=True)
                umm = None
                for kc in range(NCH):
                    umm = ctx.run("te", lambda e, kc=kc, b=b, t0=t0: e.matmul(
                        pso[1][:, :], WU[b][:, kc * 128:(kc + 1) * 128],
                        bufs["XN"][:, kc * T + t0: kc * T + t0 + TTW],
                        start=(kc == 0), stop=(kc == NCH - 1)),
                        after=[wu_l[b], h_t.get("u"), evo.get(1)], tok=(kc == NCH - 1))
                mmu[b] = umm
                ht = ctx.run("dv", lambda e, hj=hj: e.scalar_tensor_tensor(
                    HALL[:, hj * 512:(hj + 1) * 512], pso[1][:, :], 1.0 / 127.0,
                    SG[:, (hj % 2) * 512:(hj % 2) * 512 + 512], ALU.mult, ALU.mult),
                    after=[umm, sg_t[0], p2last], tok=True)
                h_t[hj % 2] = ht
                h_t["u"] = ht
            lastmm = None
            for hj in range(NH):
                b = hj % 2
                wd_l[b] = rd.dma("gp", WD[b][:, :], W["l3_wdT"][hj * 128:(hj + 1) * 128, :],
                                 after=[wdmm[b]])
                for oc in range(NCH):
                    lastmm = ctx.run("te", lambda e, hj=hj, oc=oc, b=b: e.matmul(
                        pso[oc][:, :], WD[b][:, oc * 128:(oc + 1) * 128],
                        HALL[:, hj * 512:(hj + 1) * 512],
                        start=(hj == 0), stop=(hj == NH - 1)),
                        after=[wd_l[b], h_t[0], h_t[1], evo.get(oc)], tok=(oc == NCH - 1))
                wdmm[b] = lastmm
            p2last = lastmm
            for oc in range(NCH):
                sb = oc % 2
                ev = ctx.run("ac", lambda e, oc=oc, sb=sb: e.activation(
                    GT[:, sb * 512:sb * 512 + 512], pso[oc][:, :], AF.Silu, scale=1.0 / 127.0),
                    after=[lastmm, stok[sb]], tok=True)
                stok[sb] = rstg.dma("sy", g_r[:, oc, t0:t0 + TTW], GT[:, sb * 512:sb * 512 + 512],
                                    after=[ev])
                evo[oc] = stok[sb]
        out = [stok[0], stok[1]]
        rg.close(); ru.close(); rd.close(); rstg.close()
    return out


def emit_l3attn(ctx, bufs, W, deps, gate_dram, xres_in, xres_out, g_dram):
    nc = ctx.nc
    gate_r = gate_dram.rearrange("(ch p) t -> p ch t", p=128)
    L3C = bufs["l3c"]
    CPP = 4 * 128 + 8
    toks = []
    with contextlib.ExitStack() as es:
        QB = es.enter_context(nc.sbuf_tensor(uname("a3_qb"), [128, T], BF16))
        KB = es.enter_context(nc.sbuf_tensor(uname("a3_kb"), [128, T], BF16))
        KTB = es.enter_context(nc.sbuf_tensor(uname("a3_kt"), [128, T], BF16))
        VB = es.enter_context(nc.sbuf_tensor(uname("a3_vb"), [128, T], BF16))
        QSB = es.enter_context(nc.sbuf_tensor(uname("a3_qs"), [128, 128], BF16))
        KDB = es.enter_context(nc.sbuf_tensor(uname("a3_kd"), [128, 128], BF16))
        PSB = es.enter_context(nc.sbuf_tensor(uname("a3_pb"), [128, 256], BF16))
        KVS = es.enter_context(nc.sbuf_tensor(uname("a3_kvs"), [128, 64], F32))
        KVB = es.enter_context(nc.sbuf_tensor(uname("a3_kvb"), [128, 64], BF16))
        WSB = [es.enter_context(nc.sbuf_tensor(uname("a3_w"), [128, NCH * 128], BF16)) for _ in range(3)]
        psq = [es.enter_context(nc.psum_tensor(uname("a3_psq"), [128, 512], F32)) for _ in range(2)]
        psS = [es.enter_context(nc.psum_tensor(uname("a3_s"), [128, 128], F32)) for _ in range(2)]
        psO = es.enter_context(nc.psum_tensor(uname("a3_o"), [128, 128], F32))
        psD = es.enter_context(nc.psum_tensor(uname("a3_d"), [128, 64], F32))
        rw = Ring(ctx, 3, sw=True)
        AO = bufs["AO"]
        prev_pair = list(deps)
        for pair in range(NPAIR):
            c0 = pair * CPP
            mA = L3C[:, c0:c0 + 128]
            mB = L3C[:, c0 + 128:c0 + 256]
            LQ = L3C[:, c0 + 256:c0 + 384]
            LK = L3C[:, c0 + 384:c0 + 512]
            lamc = L3C[:, c0 + 512:c0 + 513]
            wq = wslab_load(ctx, rw, WSB[0], W["l3_wq"], pair * 128, 128, after=prev_pair)
            wk = wslab_load(ctx, rw, WSB[1], W["l3_wk"], pair * 128, 128, after=prev_pair)
            wv = wslab_load(ctx, rw, WSB[2], W["l3_wv"], pair * 128, 128, after=prev_pair)
            evq = {0: None, 1: None}
            qtoks = []
            for tt in range(NTT):
                pb = tt % 2
                mmq = None
                for kc in range(NCH):
                    mmq = ctx.run("te", lambda e, kc=kc, tt=tt, pb=pb: e.matmul(
                        psq[pb][:, :], WSB[0][:, kc * 128:(kc + 1) * 128],
                        bufs["XN"][:, kc * T + tt * TTW: kc * T + tt * TTW + TTW],
                        start=(kc == 0), stop=(kc == NCH - 1)),
                        after=[wq, evq[pb]] + prev_pair, tok=(kc == NCH - 1))
                evq[pb] = ctx.run("ac", lambda e, tt=tt, pb=pb: e.activation(
                    QB[:, tt * TTW:(tt + 1) * TTW], psq[pb][:, :], AF.Silu), after=[mmq], tok=True)
                qtoks.append(evq[pb])
                mmk = None
                for kc in range(NCH):
                    mmk = ctx.run("te", lambda e, kc=kc, tt=tt, pb=pb: e.matmul(
                        psq[1 - pb][:, :], WSB[1][:, kc * 128:(kc + 1) * 128],
                        bufs["XN"][:, kc * T + tt * TTW: kc * T + tt * TTW + TTW],
                        start=(kc == 0), stop=(kc == NCH - 1)),
                        after=[wk, evq[1 - pb]] + prev_pair, tok=(kc == NCH - 1))
                evq[1 - pb] = ctx.run("ac", lambda e, tt=tt, pb=pb: e.activation(
                    KB[:, tt * TTW:(tt + 1) * TTW], psq[1 - pb][:, :], AF.Silu), after=[mmk], tok=True)
                qtoks.append(evq[1 - pb])
            vtoks = []
            for tb in range(NTB):
                pb = tb % 2
                mmk = None
                for kc in range(NCH):
                    mmk = ctx.run("te", lambda e, kc=kc, tb=tb, pb=pb: e.matmul(
                        psq[pb][:, 0:128],
                        bufs["XN"][:, kc * T + tb * 128: kc * T + tb * 128 + 128],
                        WSB[1][:, kc * 128:(kc + 1) * 128],
                        start=(kc == 0), stop=(kc == NCH - 1)),
                        after=[wk, evq[pb]], tok=(kc == NCH - 1))
                evq[pb] = ctx.run("ac", lambda e, tb=tb, pb=pb: e.activation(
                    KTB[:, tb * 128:(tb + 1) * 128], psq[pb][:, 0:128], AF.Silu), after=[mmk], tok=True)
                mmv = None
                for kc in range(NCH):
                    mmv = ctx.run("te", lambda e, kc=kc, tb=tb, pb=pb: e.matmul(
                        psq[1 - pb][:, 0:128],
                        bufs["XN"][:, kc * T + tb * 128: kc * T + tb * 128 + 128],
                        WSB[2][:, kc * 128:(kc + 1) * 128],
                        start=(kc == 0), stop=(kc == NCH - 1)),
                        after=[wv, evq[1 - pb]], tok=(kc == NCH - 1))
                evq[1 - pb] = ctx.run("ac", lambda e, tb=tb, pb=pb: e.activation(
                    VB[:, tb * 128:(tb + 1) * 128], psq[1 - pb][:, 0:128], AF.Silu), after=[mmv], tok=True)
                vtoks.append(evq[1 - pb])
                vtoks.append(evq[pb])
            ready = qtoks[-2:] + vtoks[-2:]
            z1 = ctx.run("dv", lambda e: e.memset(KVS[:, :], 0.0), after=prev_pair, tok=True)
            z2 = ctx.run("dv", lambda e: e.memset(KVB[:, :], 0.0), after=[z1], tok=True)
            kvb_t = z2
            kvs_t = z1
            ev_o = None
            pmask = {}
            dmm = None
            qs_t = None
            kd_t = None
            for i in range(NTB):
                i0 = i * 128
                qs_t = ctx.run("dv", lambda e, i0=i0: e.tensor_mul(
                    QSB[:, :], QB[:, i0:i0 + 128], LQ), after=ready + [pmask.get("qs_r")], tok=True)
                kd_t = ctx.run("dv", lambda e, i0=i0: e.tensor_mul(
                    KDB[:, :], KTB[:, i0:i0 + 128], LK), after=ready + [pmask.get("kd_r")], tok=True)
                smm = {}
                for h in range(2):
                    hb = 64 * h
                    smm[h] = ctx.run("te", lambda e, h=h, hb=hb, i0=i0: e.matmul(
                        psS[h][:, :],
                        KB[hb:hb + 64, i0:i0 + 128], QB[hb:hb + 64, i0:i0 + 128],
                        start=True, stop=True),
                        after=ready + [pmask.get(h)], tok=True)
                for h in range(2):
                    pmask[h] = ctx.run("dv", lambda e, h=h: e.tensor_mul(
                        PSB[:, h * 128:h * 128 + 128], psS[h][:, :],
                        (mA if h == 0 else mB)), after=[smm[h], ev_o], tok=True)
                omm = None
                for h in range(2):
                    hb = 64 * h
                    omm = ctx.run("te", lambda e, h=h, hb=hb, i0=i0: e.matmul(
                        psO[hb:hb + 64, 0:128],
                        VB[:, i0 + hb:i0 + hb + 64], PSB[:, h * 128:h * 128 + 128],
                        start=True, stop=False),
                        after=[pmask[h], ev_o, kvb_t], tok=True)
                    omm = ctx.run("te", lambda e, h=h, hb=hb: e.matmul(
                        psO[hb:hb + 64, 0:128],
                        KVB[hb:hb + 64, 0:64], QSB[hb:hb + 64, 0:128],
                        start=False, stop=True),
                        after=[qs_t, kvb_t], tok=True)
                for h in range(2):
                    hb = 64 * h
                    dmm = ctx.run("te", lambda e, h=h, hb=hb, i0=i0: e.matmul(
                        psD[hb:hb + 64, 0:64],
                        KDB[:, hb:hb + 64], VB[:, i0 + hb:i0 + hb + 64],
                        start=True, stop=True),
                        after=[kd_t, kvs_t], tok=True)
                eng = "dv" if i % 2 == 0 else "ac"
                ev_o = ctx.run(eng, lambda e, i0=i0, pair=pair, eng=eng: (
                    e.tensor_copy(AO[:, pair * T + i0: pair * T + i0 + 128], psO[:, 0:128])
                    if eng == "dv" else
                    e.copy(AO[:, pair * T + i0: pair * T + i0 + 128], psO[:, 0:128])),
                    after=[omm], tok=True)
                pmask["qs_r"] = omm
                pmask["kd_r"] = dmm
                kvs_t = ctx.run("dv", lambda e: e.tensor_scalar(
                    KVS[:, :], KVS[:, :], lamc[:, 0:1], None, ALU.mult), after=[kvs_t, kvb_t], tok=True)
                kvs_t = ctx.run("dv", lambda e: e.tensor_add(
                    KVS[:, :], KVS[:, :], psD[:, 0:64]), after=[kvs_t, dmm], tok=True)
                kvb_t = ctx.run("dv", lambda e: e.tensor_copy(KVB[:, :], KVS[:, :]),
                                after=[kvs_t, omm], tok=True)
            prev_pair = [ev_o, kvb_t]
            toks = prev_pair
    return toks


def emit_layer3(ctx, bufs, W, xres_in, xres_out, scratch, deps):
    nc = ctx.nc
    les = ctx.es.enter_context(contextlib.ExitStack())
    bufs["XN"] = les.enter_context(nc.sbuf_tensor(uname("XN"), [128, NCH * T], BF16))
    bufs["AO"] = les.enter_context(nc.sbuf_tensor(uname("AO"), [128, NCH * T], BF16))
    g_dram = scratch["g"]
    gate_dram = scratch["gate"]
    t_ln = emit_ln(ctx, bufs, xres_in, deps)
    t_g = emit_l3ffn(ctx, bufs, W, t_ln, g_dram)
    with contextlib.ExitStack() as es:
        G1 = es.enter_context(nc.sbuf_tensor(uname("g3_1"), [128, T], BF16))
        GO = es.enter_context(nc.sbuf_tensor(uname("g3_o"), [128, 2 * TTW], BF16))
        rgs = Ring(ctx, 2)
        gate_r = gate_dram.rearrange("(ch p) t -> p ch t", p=128)
        st = [None, None]

        def ev1(di, tt, ps, mm):
            return ctx.run("dv", lambda e, tt=tt: e.tensor_copy(G1[0:64, tt * TTW:(tt + 1) * TTW], ps),
                           after=[mm], tok=True)
        tg1 = emit_proj_fm(ctx, bufs, W["l3_wg1"], bufs["XN"], 0, [(0, 64)], ev1, t_ln + t_g)

        def ev2(di, tt, ps, mm):
            sb = (di * NTT + tt) % 2
            ak = ctx.run("ac", lambda e, ps=ps, sb=sb: e.activation(
                GO[:, sb * TTW:(sb + 1) * TTW], ps, AF.Sigmoid), after=[mm, st[sb]], tok=True)
            st[sb] = rgs.dma("sy", gate_r[:, di, tt * TTW:(tt + 1) * TTW], GO[:, sb * TTW:(sb + 1) * TTW],
                             after=[ak])
            return st[sb]
        tg2 = emit_proj_fm(ctx, bufs, W["l3_wg2"], G1, 0, [(c * 128, 128) for c in range(NCH)],
                           ev2, tg1, din=64)
        t_gate = [st[0], st[1]]
        rgs.close()
    t_at = emit_l3attn(ctx, bufs, W, t_ln + t_gate, gate_dram, xres_in, xres_out, g_dram)
    with contextlib.ExitStack() as es:
        GTI = es.enter_context(nc.sbuf_tensor(uname("o3_g"), [128, NCH * 512], BF16))
        rgl = Ring(ctx, 2)
        gate_r = gate_dram.rearrange("(ch p) t -> p ch t", p=128)
        gb_cols = bufs["l3ab"]

        def produce(tt, after):
            t0 = tt * TTW
            gt = rgl.dma("sy", GTI[:, :].rearrange("p (ch m) -> p ch m", m=512),
                         gate_r[:, :, t0:t0 + TTW], after=after)
            return gt, (lambda ch, t0=t0: bufs["AO"][:, ch * T + t0: ch * T + t0 + TTW])

        def consume(tt, ch, scr, rst, after):
            t0 = tt * TTW
            k1 = ctx.run("dv", lambda e, scr=scr, rst=rst: e.tensor_mul(scr, scr, rst), after=after, tok=True)
            k2 = ctx.run("dv", lambda e, ch=ch, scr=scr: e.tensor_scalar(
                scr, scr, gb_cols[:, ch:ch + 1], gb_cols[:, 8 + ch:8 + ch + 1], ALU.mult, ALU.add),
                after=[k1], tok=True)
            k3 = ctx.run("dv", lambda e, ch=ch, t0=t0, scr=scr: e.tensor_mul(
                bufs["AO"][:, ch * T + t0: ch * T + t0 + TTW], scr,
                GTI[:, ch * 512:ch * 512 + 512]), after=[k2], tok=True)
            return k3
        t_m = emit_lnx(ctx, bufs, t_at, produce, consume, eps=1e-5)
        rgl.close()
    t_oc = emit_outproj_combine(ctx, bufs, W["l3_wo"], bufs["AO"], xres_in, xres_out,
                                g_dram, t_m + t_g)
    les.close()
    return t_oc


TPAD = 8


def _xl_slice(XL, ch, t0, w, shifted):
    base = ch * (T + TPAD) + (TPAD - 1 if shifted else TPAD)
    return XL[:, base + t0: base + t0 + w]


def emit_l2_inner_attn(ctx, bufs, W, li, XL, ao_dram, deps):
    nc = ctx.nc
    ao_r = ao_dram.rearrange("(ch p) t -> p ch t", p=128)
    toks = []
    tes = contextlib.ExitStack()
    for nm in ("csq", "ssq", "csk", "ssk"):
        bufs[nm] = tes.enter_context(nc.sbuf_tensor(uname(nm), [128, T], BF16))
    tbl_toks = []
    mr = misc_ring(ctx, "gp")
    for nm, col in (("csq", ("l2qs", 8.0)), ("ssq", ("l2qs", 8.0)),
                    ("csk", ("l2ks", None)), ("ssk", ("l2ks", None))):
        lt = mr.dma("gp", bufs[nm][:, :], W["c_" + nm], after=list(deps) + [bufs.get("_tblg")])
        cn, extra = col
        ft = ctx.run("dv", lambda e, nm=nm, cn=cn, extra=extra, li=li: (
            e.tensor_scalar(bufs[nm][:, :], bufs[nm][:, :], bufs[cn][:, li:li + 1], extra,
                            ALU.mult, ALU.mult) if extra is not None else
            e.tensor_scalar(bufs[nm][:, :], bufs[nm][:, :], bufs[cn][:, li:li + 1], None, ALU.mult)),
            after=[lt], tok=True)
        tbl_toks.append(ft)
    bufs["_tblg"] = None
    deps = list(deps) + tbl_toks
    with contextlib.ExitStack() as es:
        QB = es.enter_context(nc.sbuf_tensor(uname("i2_qb"), [128, T], BF16))
        KB = es.enter_context(nc.sbuf_tensor(uname("i2_kb"), [128, T], BF16))
        VB = es.enter_context(nc.sbuf_tensor(uname("i2_vb"), [128, T], BF16))
        SQ = es.enter_context(nc.sbuf_tensor(uname("i2_sq"), [128, 512], BF16))
        SSB = es.enter_context(nc.sbuf_tensor(uname("i2_ss"), [128, 512], F32))
        QH = es.enter_context(nc.sbuf_tensor(uname("i2_qh"), [128, 512], BF16))
        Q2 = es.enter_context(nc.sbuf_tensor(uname("i2_q2"), [128, 512], BF16))
        PB = es.enter_context(nc.sbuf_tensor(uname("i2_pb"), [128, 2 * 512], BF16))
        PT = es.enter_context(nc.sbuf_tensor(uname("i2_pt"), [128, 512], BF16))
        NM = es.enter_context(nc.sbuf_tensor(uname("i2_nm"), [128, 4], F32))
        AOP = es.enter_context(nc.sbuf_tensor(uname("i2_ao"), [128, T], BF16))
        WSB = [es.enter_context(nc.sbuf_tensor(uname("i2_w"), [128, NCH * 128], BF16)) for _ in range(3)]
        psq = [es.enter_context(nc.psum_tensor(uname("i2_psq"), [128, 512], F32)) for _ in range(2)]
        psS = [es.enter_context(nc.psum_tensor(uname("i2_s"), [128, 512], F32)) for _ in range(2)]
        psT = es.enter_context(nc.psum_tensor(uname("i2_t"), [128, 1024], BF16))
        psO = es.enter_context(nc.psum_tensor(uname("i2_o"), [128, 512], F32))
        psR = es.enter_context(nc.psum_tensor(uname("i2_r"), [128, 512], F32))
        rw = Ring(ctx, 3, sw=True)
        rao = Ring(ctx, 2)
        prev_pair = list(deps)
        ao_st = None
        for pair in range(NPAIR):
            wq = wslab_load(ctx, rw, WSB[0], W["l2f_qkv"][li], pair * 128, 128, after=prev_pair)
            wk = wslab_load(ctx, rw, WSB[1], W["l2f_qkv"][li], 1024 + pair * 128, 128, after=prev_pair)
            wv = wslab_load(ctx, rw, WSB[2], W["l2f_qkv"][li], 2048 + pair * 128, 128, after=prev_pair)
            evq = {0: None, 1: None}
            qtoks = []
            for which, WS, wtok, TBL_C, TBL_S, scol in (
                    (0, WSB[0], wq, "csq", "ssq", 0), (1, WSB[1], wk, "csk", "ssk", 1)):
                BUF = QB if which == 0 else KB
                for tt in range(NTT):
                    t0 = tt * TTW
                    pb = tt % 2
                    mmq = None
                    for kc in range(NCH):
                        mmq = ctx.run("te", lambda e, kc=kc, t0=t0, pb=pb, WS=WS: e.matmul(
                            psq[pb][:, :], WS[:, kc * 128:(kc + 1) * 128],
                            _xl_slice(XL, kc, t0, TTW, True),
                            start=(kc == 0), stop=(kc == NCH - 1)),
                            after=[wtok, evq[pb]] + prev_pair, tok=(kc == NCH - 1))
                    sqt = ctx.run("ac", lambda e, pb=pb: e.activation(SQ[:, :], psq[pb][:, :], AF.Square),
                                  after=[mmq, qtoks[-1] if qtoks else None], tok=True)
                    ssm = None
                    for h in range(2):
                        hb = 64 * h
                        ssm = ctx.run("te", lambda e, hb=hb: e.matmul(
                            psR[hb:hb + 64, 0:TTW], bufs["one64"][hb:hb + 64, 0:64], SQ[hb:hb + 64, 0:TTW],
                            start=True, stop=True),
                            after=[sqt, qtoks[-1] if qtoks else None], tok=(h == 1))
                    l1_ = ctx.run("ac", lambda e: e.activation(SSB[:, :], psR[:, 0:TTW], AF.Ln, bias=1e-24),
                                  after=[ssm], tok=True)
                    l2_ = ctx.run("ac", lambda e: e.activation(SSB[:, :], SSB[:, :], AF.Exp, scale=-0.5),
                                  after=[l1_], tok=True)
                    qh = ctx.run("dv", lambda e, pb=pb: e.tensor_mul(QH[:, :], psq[pb][:, :], SSB[:, :]),
                                 after=[l2_, mmq], tok=True)
                    evq[pb] = qh
                    q2 = ctx.run("gp", lambda e, t0=t0, TBL_S=TBL_S: e.tensor_mul(
                        Q2[:, :], QH[:, :], bufs[TBL_S][:, t0:t0 + TTW]), after=[qh], tok=True)
                    rmm = ctx.run("te", lambda e: e.matmul(
                        psR[:, 0:TTW], bufs["rot0"][:, :], Q2[:, :], start=True, stop=True),
                        after=[q2, l2_], tok=True)
                    t3 = ctx.run("gp", lambda e, t0=t0, TBL_C=TBL_C: e.tensor_mul(
                        Q2[:, :], QH[:, :], bufs[TBL_C][:, t0:t0 + TTW]), after=[q2, rmm], tok=True)
                    fin = ctx.run("dv", lambda e, BUF=BUF, t0=t0: e.tensor_add(
                        BUF[:, t0:t0 + TTW], Q2[:, :], psR[:, 0:TTW]), after=[t3, rmm], tok=True)
                    qtoks.append(fin)
            vtoks = []
            for tb in range(NTB):
                pb = tb % 2
                mmv = None
                for kc in range(NCH):
                    mmv = ctx.run("te", lambda e, kc=kc, tb=tb, pb=pb: e.matmul(
                        psq[pb][:, 0:128],
                        _xl_slice(XL, kc, tb * 128, 128, True),
                        WSB[2][:, kc * 128:(kc + 1) * 128],
                        start=(kc == 0), stop=(kc == NCH - 1)),
                        after=[wv, evq[pb], qtoks[-1]], tok=(kc == NCH - 1))
                evq[pb] = ctx.run("ac", lambda e, tb=tb, pb=pb: e.copy(
                    VB[:, tb * 128:(tb + 1) * 128], psq[pb][:, 0:128]), after=[mmv], tok=True)
                vtoks.append(evq[pb])
            ready = qtoks[-2:] + vtoks[-2:]
            expo = {}
            evo = {0: None, 1: None}
            for qb in range(NTB):
                if qb == 0:
                    k0, klen, mcls = 0, 256, 0
                elif qb == 1:
                    k0, klen, mcls = 0, 256, 1
                elif qb % 2 == 0:
                    k0, klen, mcls = (qb - 2) * 128, 512, 2
                else:
                    k0, klen, mcls = (qb - 3) * 128, 512, 3
                mask = bufs["l2m"][:, mcls * 512: mcls * 512 + klen]
                for h in range(2):
                    hb = 64 * h
                    sb = h
                    mm = ctx.run("te", lambda e, h=h, hb=hb, sb=sb, qb=qb, k0=k0, klen=klen: e.matmul(
                        psS[sb][:, 0:klen],
                        QB[hb:hb + 64, qb * 128:qb * 128 + 128],
                        KB[hb:hb + 64, k0:k0 + klen],
                        start=True, stop=True),
                        after=ready + [expo.get(sb)], tok=True)
                    mtok = ctx.run("dv", lambda e, sb=sb, klen=klen, mask=mask: e.tensor_add(
                        psS[sb][:, 0:klen], psS[sb][:, 0:klen], mask), after=[mm], tok=True)
                    nmt = ctx.run("dv", lambda e, sb=sb, klen=klen, h=h: e.tensor_reduce(
                        NM[:, h:h + 1], psS[sb][:, 0:klen], AX.X, ALU.max, negate=True),
                        after=[mtok], tok=True)
                    ext = ctx.run("ac", lambda e, h=h, sb=sb, klen=klen: e.activation(
                        PB[:, h * 512: h * 512 + klen], psS[sb][:, 0:klen],
                        AF.Exp, bias=NM[:, h:h + 1], accum_out=NM[:, 2 + h:3 + h]),
                        after=[nmt], tok=True)
                    expo[sb] = ext
                    rc = ctx.run("dv", lambda e, h=h: e.reciprocal(NM[:, h:h + 1], NM[:, 2 + h:3 + h]),
                                 after=[ext], tok=True)
                    rs = ctx.run("dv", lambda e, h=h, klen=klen: e.tensor_scalar(
                        PB[:, h * 512:h * 512 + klen], PB[:, h * 512:h * 512 + klen],
                        NM[:, h:h + 1], None, ALU.mult), after=[rc], tok=True)
                    nkc = klen // 128
                    evt = None
                    mmo = None
                    for kc in range(nkc):
                        tb4 = kc % 4
                        ttk = ctx.run("te", lambda e, h=h, kc=kc, tb4=tb4: e.transpose(
                            psT[:, tb4 * 128:tb4 * 128 + 128],
                            PB[:, h * 512 + kc * 128: h * 512 + kc * 128 + 128],
                            bufs["identb"][:, :]),
                            after=[rs, evt], tok=True)
                        evt = ctx.run("ac" if kc % 2 else "dv", lambda e, tb4=tb4, kc=kc: (
                            e.copy(PT[:, tb4 * 128:tb4 * 128 + 128], psT[:, tb4 * 128:tb4 * 128 + 128])
                            if kc % 2 else
                            e.tensor_copy(PT[:, tb4 * 128:tb4 * 128 + 128], psT[:, tb4 * 128:tb4 * 128 + 128])),
                            after=[ttk], tok=True)
                        mmo = ctx.run("te", lambda e, h=h, hb=hb, kc=kc, tb4=tb4, k0=k0, nkc=nkc: e.matmul(
                            psO[hb:hb + 64, h * 128:h * 128 + 128],
                            VB[:, k0 + kc * 128 + hb: k0 + kc * 128 + hb + 64],
                            PT[:, tb4 * 128:tb4 * 128 + 128],
                            start=(kc == 0), stop=(kc == nkc - 1)),
                            after=[evt, evo[0], evo[1]], tok=(kc == nkc - 1))
                    evo[h] = ctx.run("dv" if h == 0 else "ac",
                                     lambda e, h=h, hb=hb, qb=qb: (
                        e.tensor_copy(AOP[hb:hb + 64, qb * 128:qb * 128 + 128],
                                      psO[hb:hb + 64, h * 128:h * 128 + 128]) if h == 0 else
                        e.copy(AOP[hb:hb + 64, qb * 128:qb * 128 + 128],
                               psO[hb:hb + 64, h * 128:h * 128 + 128])),
                        after=[mmo], tok=True)
            ao_st = rao.dma("sy", ao_r[:, pair, :], AOP[:, :], after=[evo[0], evo[1], ao_st])
            prev_pair = [evo[0], evo[1], ao_st]
            toks = prev_pair
        rw.close(); rao.close()
    bufs["_tblg"] = toks[0]
    tes.close()
    return toks


def emit_l2_inner_wo(ctx, bufs, W, li, XL, ao_dram, deps):
    nc = ctx.nc
    ao_r = ao_dram.rearrange("(ch p) t -> p ch t", p=128)
    toks = []
    with contextlib.ExitStack() as es:
        MV = es.enter_context(nc.sbuf_tensor(uname("wo_mv"), [128, 2 * TTW], BF16))
        WSB = [es.enter_context(nc.sbuf_tensor(uname("wo_w"), [128, NCH * 128], BF16)) for _ in range(2)]
        pss = [es.enter_context(nc.psum_tensor(uname("wo_ps"), [128, TTW], F32)) for _ in range(2)]
        rw = Ring(ctx, 2, sw=True)
        rm = Ring(ctx, 2)
        wld = [None, None]
        lastmm = [None, None]
        ev = [None, None]
        mvl = {}
        for di in range(NCH):
            wb = di % 2
            wld[wb] = wslab_load(ctx, rw, WSB[wb], W["l2f_wo"][li], di * 128, 128,
                                 after=[lastmm[wb]] + (list(deps) if di < 2 else []))
            for tt in range(NTT):
                pb = tt % 2
                mm = None
                for kc in range(NCH):
                    mb = (tt * NCH + kc) % 2
                    mvt = rm.dma("sy", MV[:, mb * TTW:(mb + 1) * TTW], ao_r[:, kc, tt * TTW:(tt + 1) * TTW],
                                 after=[mvl.get(mb)] + (list(deps) if di == 0 and tt == 0 and kc < 2 else []))
                    mm = ctx.run("te", lambda e, kc=kc, wb=wb, pb=pb, mb=mb: e.matmul(
                        pss[pb][:, :], WSB[wb][:, kc * 128:(kc + 1) * 128],
                        MV[:, mb * TTW:(mb + 1) * TTW],
                        start=(kc == 0), stop=(kc == NCH - 1)),
                        after=[wld[wb], ev[pb], mvt], tok=True)
                    mvl[mb] = mm
                lastmm[wb] = mm
                ev[pb] = ctx.run("dv", lambda e, di=di, tt=tt, pb=pb: e.tensor_add(
                    _xl_slice(XL, di, tt * TTW, TTW, False),
                    _xl_slice(XL, di, tt * TTW, TTW, False), pss[pb][:, :]),
                    after=[mm], tok=True)
                toks.append(ev[pb])
        rw.close(); rm.close()
    return toks[-2:]


def emit_l2_inner_ffn(ctx, bufs, W, li, XL, deps):
    nc = ctx.nc
    NHJ = 22
    hsz = [min(128, 2730 - 128 * j) for j in range(NHJ)]
    toks = []
    with contextlib.ExitStack() as es:
        XSN = es.enter_context(nc.sbuf_tensor(uname("fi_xsn"), [128, NCH * T], BF16))
        def produce(tt, after):
            return None, (lambda ch, t0=tt * TTW: _xl_slice(XL, ch, t0, TTW, True))

        def consume(tt, ch, scr, rst, after):
            t0 = tt * TTW
            k1 = ctx.run("dv", lambda e, scr=scr, rst=rst: e.tensor_mul(scr, scr, rst), after=after, tok=True)
            k2 = ctx.run("dv", lambda e, ch=ch, t0=t0, scr=scr: e.tensor_scalar(
                XSN[:, ch * T + t0: ch * T + t0 + TTW], scr,
                bufs["l2lng"][:, li * 8 + ch:li * 8 + ch + 1] if False else bufs["l2lng"][:, li * 16 + ch:li * 16 + ch + 1],
                bufs["l2lng"][:, li * 16 + 8 + ch:li * 16 + 8 + ch + 1], ALU.mult, ALU.add),
                after=[k1], tok=True)
            return k2
        t_ln = emit_lnx(ctx, bufs, deps, produce, consume, eps=1e-5)
        with contextlib.ExitStack() as es2:
            HA = es2.enter_context(nc.sbuf_tensor(uname("fi_h"), [128, NHJ * 1024], BF16))
            WA = [es2.enter_context(nc.sbuf_tensor(uname("fi_wa"), [128, NCH * 128], BF16)) for _ in range(2)]
            WG = [es2.enter_context(nc.sbuf_tensor(uname("fi_wg"), [128, NCH * 128], BF16)) for _ in range(2)]
            W2 = [es2.enter_context(nc.sbuf_tensor(uname("fi_w2"), [128, 1024], BF16)) for _ in range(2)]
            GEL = es2.enter_context(nc.sbuf_tensor(uname("fi_gel"), [128, 2 * 512], BF16))
            pso = [es2.enter_context(nc.psum_tensor(uname("fi_ps"), [128, 512], F32)) for _ in range(8)]
            rwa = Ring(ctx, 2, sw=True)
            rwg = Ring(ctx, 2, sw=True)
            rw2 = Ring(ctx, 2, sw=True)
            wa_l = [None, None]; wg_l = [None, None]; w2_l = [None, None]
            wamm = [None, None]; wgmm = [None, None]; w2mm = [None, None]
            gel_t = [None, None]
            h_t = {}
            evo = {}
            p2last = None
            TSUB = min(1024, T)
            for th in range(T // TSUB):
                tof = th * TSUB
                for hj in range(NHJ):
                    b = hj % 2
                    wa_l[b] = wslab_load(ctx, rwa, WA[b], W["l2f_w1"][li], hj * 128, hsz[hj],
                                         after=[wamm[b]] + (list(t_ln) if th == 0 and hj < 2 else []))
                    wg_l[b] = wslab_load(ctx, rwg, WG[b], W["l2f_w1"][li], 2730 + hj * 128, hsz[hj],
                                         after=[wgmm[b]] + (list(t_ln) if th == 0 and hj < 2 else []))
                    for half in range(TSUB // 512):
                        t0 = tof + half * 512
                        hb2 = half
                        cols = hsz[hj]
                        gmm = None
                        for kc in range(NCH):
                            gmm = ctx.run("te", lambda e, kc=kc, b=b, t0=t0, cols=cols: e.matmul(
                                pso[1][0:cols, :], WG[b][:, kc * cols:(kc + 1) * cols],
                                XSN[:, kc * T + t0: kc * T + t0 + 512],
                                start=(kc == 0), stop=(kc == NCH - 1)),
                                after=[wg_l[b], gel_t[0], gel_t[1], evo.get(1)], tok=(kc == NCH - 1))
                        wgmm[b] = gmm
                        gel_t[hb2] = ctx.run("ac", lambda e, hj=hj, hb2=hb2, cols=cols, li=li: e.activation(
                            GEL[0:cols, hb2 * 512:(hb2 + 1) * 512], pso[1][0:cols, :], AF.Gelu,
                            bias=bufs["l2b1g"][0:cols, li * NHJ + hj:li * NHJ + hj + 1]),
                            after=[gmm, h_t.get(hb2)], tok=True)
                        amm = None
                        for kc in range(NCH):
                            amm = ctx.run("te", lambda e, kc=kc, b=b, t0=t0, cols=cols: e.matmul(
                                pso[0][0:cols, :], WA[b][:, kc * cols:(kc + 1) * cols],
                                XSN[:, kc * T + t0: kc * T + t0 + 512],
                                start=(kc == 0), stop=(kc == NCH - 1)),
                                after=[wa_l[b], h_t.get("a"), evo.get(0)], tok=(kc == NCH - 1))
                        wamm[b] = amm
                        ht = ctx.run("dv", lambda e, hj=hj, hb2=hb2, half=half, cols=cols, li=li: e.scalar_tensor_tensor(
                            HA[0:cols, hj * 1024 + half * 512: hj * 1024 + half * 512 + 512],
                            pso[0][0:cols, :], bufs["l2b1a"][0:cols, li * NHJ + hj:li * NHJ + hj + 1],
                            GEL[0:cols, hb2 * 512:(hb2 + 1) * 512], ALU.add, ALU.mult),
                            after=[amm, gel_t[hb2], p2last], tok=True)
                        h_t[hb2] = ht
                        h_t["a"] = ht
                for half in range(TSUB // 512):
                    t0 = tof + half * 512
                    lastmm = None
                    for hj in range(NHJ):
                        b = hj % 2
                        if half == 0:
                            w2_l[b] = rw2.dma("gp", W2[b][0:hsz[hj], :],
                                              W["l2f_w2"][li][hj * 128: hj * 128 + hsz[hj], :],
                                              after=[w2mm[b]])
                        for oc in range(NCH):
                            lastmm = ctx.run("te", lambda e, hj=hj, oc=oc, b=b, half=half, cols=hsz[hj]: e.matmul(
                                pso[oc][:, :], W2[b][0:cols, oc * 128:(oc + 1) * 128],
                                HA[0:cols, hj * 1024 + half * 512: hj * 1024 + half * 512 + 512],
                                start=(hj == 0), stop=(hj == NHJ - 1)),
                                after=[w2_l[b], h_t.get(0), h_t.get(1), evo.get(oc)], tok=(oc == NCH - 1))
                        if half == TSUB // 512 - 1:
                            w2mm[b] = lastmm
                    p2last = lastmm
                    for oc in range(NCH):
                        ev = ctx.run("dv", lambda e, oc=oc, t0=t0, li=li: e.scalar_tensor_tensor(
                            _xl_slice(XL, oc, t0, 512, False), pso[oc][:, :],
                            bufs["l2b2"][:, li * 8 + oc:li * 8 + oc + 1],
                            _xl_slice(XL, oc, t0, 512, False), ALU.add, ALU.add),
                            after=[lastmm, evo.get(oc)], tok=True)
                        evo[oc] = ev
                        toks.append(ev)
    return toks[-2:]


def emit_l2_xl_init(ctx, bufs, XL, src_dram, deps):
    nc = ctx.nc
    rxl = Ring(ctx, 2)
    src_r = src_dram.rearrange("(ch p) t -> p ch t", p=128)
    zt = ctx.run("dv", lambda e: e.memset(XL[:, :], 0.0), after=deps, tok=True)
    lts2 = []
    for ch in range(NCH):
        lts2.append(rxl.dma("sy", XL[:, ch * (T + TPAD) + TPAD: ch * (T + TPAD) + TPAD + T],
                            src_r[:, ch, :], after=[zt]))
    rxl.close()
    return lts2[-2:] + [zt]


def emit_l2_xl_save(ctx, bufs, XL, dst_dram, deps):
    ctx_toks = []
    rxl = Ring(ctx, 2)
    dst_r = dst_dram.rearrange("(ch p) t -> p ch t", p=128)
    for ch in range(NCH):
        ctx_toks.append(rxl.dma("sy", dst_r[:, ch, :],
                                XL[:, ch * (T + TPAD) + TPAD: ch * (T + TPAD) + TPAD + T],
                                after=deps if ch < 2 else []))
    rxl.close()
    return ctx_toks[-2:]


def emit_l2_inner_span(ctx, bufs, W, li0, li1, XL, ao_dram, deps):
    t_cur = deps
    for li in range(li0, li1):
        t_a = emit_l2_inner_attn(ctx, bufs, W, li, XL, ao_dram, t_cur)
        t_w = emit_l2_inner_wo(ctx, bufs, W, li, XL, ao_dram, t_a)
        t_f = emit_l2_inner_ffn(ctx, bufs, W, li, XL, t_w)
        t_cur = t_f
    return t_cur


def emit_l2_post(ctx, bufs, W, xres_in, xres_out, xn_dram, xl_dram, g_dram, deps):
    nc = ctx.nc
    esG = contextlib.ExitStack()
    XLI = esG.enter_context(nc.sbuf_tensor(uname("g2_xl"), [128, 4096], BF16))
    GT = esG.enter_context(nc.sbuf_tensor(uname("g2_gt"), [128, 2 * 512], BF16))
    rgw = Ring(ctx, 2)
    rxr = Ring(ctx, 2)
    xl_r = xl_dram.rearrange("(ch p) t -> p ch t", p=128)
    g_r = g_dram.rearrange("(ch p) t -> p ch t", p=128)
    stt_ = [None, None]
    lt = None
    for tt in range(NTT):
        lt = rxr.dma("sy", XLI[:, :].rearrange("p (ch m) -> p ch m", m=512),
                     xl_r[:, :, tt * TTW:(tt + 1) * TTW],
                     after=[stt_[0], stt_[1]] + (list(deps) if tt == 0 else []))
        for ch in range(NCH):
            sb = ch % 2
            ak = ctx.run("ac", lambda e, ch=ch, sb=sb: e.activation(
                GT[:, sb * 512:(sb + 1) * 512], XLI[:, ch * 512:ch * 512 + 512], AF.Silu),
                after=[lt, stt_[sb]], tok=True)
            stt_[sb] = rgw.dma("sy", g_r[:, ch, tt * TTW:(tt + 1) * TTW], GT[:, sb * 512:(sb + 1) * 512],
                               after=[ak])
    t_g = [stt_[0], stt_[1]]
    rgw.close(); rxr.close()
    esG.close()
    esD = ctx.es.enter_context(contextlib.ExitStack())
    bufs["XN"] = esD.enter_context(nc.sbuf_tensor(uname("XN"), [128, NCH * T], BF16))
    ao2_dram = nc.dram_tensor(uname("ao2"), [D, T], BF16).ap()
    xn_r = xn_dram.rearrange("(ch p) t -> p ch t", p=128)
    rx2 = Ring(ctx, 2)
    lts = []
    for tt in range(NTT):
        lts.append(rx2.dma("sy", bufs["XN"][:, :].rearrange("p (ch t) -> p ch t", t=T)[:, :, tt * TTW:(tt + 1) * TTW],
                           xn_r[:, :, tt * TTW:(tt + 1) * TTW], after=t_g if tt == 0 else []))
    rx2.close()

    def q_evac(pair, tt, ps, mm, dst):
        return ctx.run("dv", lambda e, ps=ps, dst=dst, pair=pair: e.tensor_scalar(
            dst, ps, bufs["l2bq"][:, pair:pair + 1], 0.125, ALU.add, ALU.mult),
            after=[mm], tok=True)

    def k_evac(pair, tt, ps, mm, dst):
        return ctx.run("dv", lambda e, ps=ps, dst=dst, pair=pair: e.tensor_scalar(
            dst, ps, bufs["l2bk"][:, pair:pair + 1], None, ALU.add),
            after=[mm], tok=True)

    def v_evac(pair, tb, ps, mm, dst):
        return ctx.run("ac", lambda e, ps=ps, dst=dst: e.copy(dst, ps), after=[mm], tok=True)

    def kset(qb):
        klen = (qb + 1) * 128
        return (0, klen, [(klen - 128, 128, bufs["m_tri"][:, :])])

    cfg = dict(wq_ap=W["l2_wq"], wk_ap=W["l2_wk"], wv_ap=W["l2_wv"],
               q_evac=q_evac, k_evac=k_evac, v_evac=v_evac, qkln=None, kset=kset,
               AO=None, ao_dram=ao2_dram,
               v_bias=(bufs["ones1"][0:1, 0:128], lambda pair: bufs["l2bvr"][0:1, pair * 128:(pair + 1) * 128]))
    t_at = emit_attn_pairs(ctx, bufs, cfg, lts[-2:])
    t_oc = emit_outproj_combine(ctx, bufs, W["l2_woT"], None, xres_in, xres_out,
                                g_dram, t_at + t_g, bias_col=bufs["l2bo"], ao_dram=ao2_dram)
    esD.close()
    return t_oc


def l2_window_classes():
    import numpy as np
    m = np.full((128, 4 * 512), NEG, np.float32)
    ir = np.arange(128)[:, None]
    j = np.arange(512)[None, :]
    for cls, d, klen in ((0, 0, 256), (1, 128, 256), (2, 256, 512), (3, 384, 512)):
        ok = (j <= ir + d) & (j >= ir + d - 256) & (j < klen)
        m[:, cls * 512: cls * 512 + 512] = np.where(ok, 0.0, NEG)
    return m


def decl_l2(nc):
    W = {}
    W["l2f_qkv"] = nc.dram_tensor("l2f_qkv", [6, D, 3072], F32, kind="ExternalInput").ap()
    W["l2f_wo"] = nc.dram_tensor("l2f_wo", [6, D, D], F32, kind="ExternalInput").ap()
    W["l2f_w1"] = nc.dram_tensor("l2f_w1", [6, D, 5460], F32, kind="ExternalInput").ap()
    W["l2f_w2"] = nc.dram_tensor("l2f_w2", [6, 2730, D], F32, kind="ExternalInput").ap()
    W["l2_wq"] = nc.dram_tensor("l2_wq", [D, D], F32, kind="ExternalInput").ap()
    W["l2_wk"] = nc.dram_tensor("l2_wk", [D, D], F32, kind="ExternalInput").ap()
    W["l2_wv"] = nc.dram_tensor("l2_wv", [D, D], F32, kind="ExternalInput").ap()
    W["l2_woT"] = nc.dram_tensor("l2_woT", [D, D], F32, kind="ExternalInput").ap()
    for nm in ("c_csq", "c_ssq", "c_csk", "c_ssk"):
        W[nm] = nc.dram_tensor(nm, [128, T], F32, kind="ExternalInput").ap()
    return W


def setup_l2_bufs(ctx, bufs, es, deps=()):
    nc = ctx.nc
    toks = []
    mr = misc_ring(ctx, "gp")
    specs = [("l2lng", [128, 96]), ("l2b1a", [128, 132]), ("l2b1g", [128, 132]),
             ("l2b2", [128, 48]), ("l2bq", [128, 8]), ("l2bk", [128, 8]),
             ("l2bo", [128, 8]), ("l2qs", [128, 6]), ("l2ks", [128, 6])]
    for nm, shp in specs:
        bufs[nm] = es.enter_context(nc.sbuf_tensor(uname(nm), shp, F32))
        ap = nc.dram_tensor("c_" + nm, shp, F32, kind="ExternalInput").ap()
        toks.append(mr.dma("gp", bufs[nm][:, :], ap, after=deps))
    for nm, shp in (("l2m", [128, 4 * 512]), ("l2bvr", [1, 8 * 128]),
                    ("rot0", [128, 128]), ("ones1", [128, 128]), ("one64", [128, 64])):
        bufs[nm] = es.enter_context(nc.sbuf_tensor(uname(nm), shp, BF16 if nm != "l2m" else F32))
        ap = nc.dram_tensor("c_" + nm, shp, F32, kind="ExternalInput").ap()
        toks.append(mr.dma("gp", bufs[nm][0:shp[0], :], ap, after=deps))
    return toks


def stage_l2(inputs):
    f32 = np.float32
    asf = lambda a: np.ascontiguousarray(np.asarray(a, f32))
    d = {
        "l2f_qkv": asf(inputs["L2_ffn_Wqkv"]),
        "l2f_wo": asf(inputs["L2_ffn_Wo"]),
        "l2f_w1": asf(inputs["L2_ffn_W1"]),
        "l2f_w2": asf(inputs["L2_ffn_W2"]),
        "l2_wq": np.ascontiguousarray(np.asarray(inputs["L2_attn_W_q"], f32).transpose(1, 0, 2).reshape(D, D)),
        "l2_wk": np.ascontiguousarray(np.asarray(inputs["L2_attn_W_k"], f32).transpose(1, 0, 2).reshape(D, D)),
        "l2_wv": np.ascontiguousarray(np.asarray(inputs["L2_attn_W_v"], f32).transpose(1, 0, 2).reshape(D, D)),
        "l2_woT": np.ascontiguousarray(np.asarray(inputs["L2_attn_W_o"], f32).T),
    }
    lng = np.zeros((128, 96), f32)
    for li in range(6):
        lng[:, li * 16:li * 16 + 8] = np.asarray(inputs["L2_ffn_ln_g"][li], f32).reshape(8, 128).T
        lng[:, li * 16 + 8:li * 16 + 16] = np.asarray(inputs["L2_ffn_ln_b"][li], f32).reshape(8, 128).T
    d["c_l2lng"] = lng
    b1a = np.zeros((128, 132), f32); b1g = np.zeros((128, 132), f32)
    for li in range(6):
        b1 = np.asarray(inputs["L2_ffn_b1"][li], f32)
        for hj in range(22):
            n = min(128, 2730 - hj * 128)
            b1a[0:n, li * 22 + hj] = b1[hj * 128: hj * 128 + n]
            b1g[0:n, li * 22 + hj] = b1[2730 + hj * 128: 2730 + hj * 128 + n]
    d["c_l2b1a"] = b1a; d["c_l2b1g"] = b1g
    b2 = np.zeros((128, 48), f32)
    for li in range(6):
        b2[:, li * 8:(li + 1) * 8] = np.asarray(inputs["L2_ffn_b2"][li], f32).reshape(8, 128).T
    d["c_l2b2"] = b2
    bq = np.asarray(inputs["L2_attn_b_q"], f32).reshape(8, 128).T
    bk = np.asarray(inputs["L2_attn_b_k"], f32).reshape(8, 128).T
    d["c_l2bq"] = np.ascontiguousarray(bq); d["c_l2bk"] = np.ascontiguousarray(bk)
    d["c_l2bvr"] = np.asarray(inputs["L2_attn_b_v"], f32).reshape(1, 1024)
    d["c_l2bo"] = np.ascontiguousarray(np.asarray(inputs["L2_attn_b_o"], f32).reshape(8, 128).T)
    qs = np.asarray(inputs["L2_ffn_q_scale"], f32)
    ks = np.asarray(inputs["L2_ffn_k_scale"], f32)
    d["c_l2qs"] = np.ascontiguousarray(np.concatenate([qs, qs], axis=1).T)
    d["c_l2ks"] = np.ascontiguousarray(np.concatenate([ks, ks], axis=1).T)
    d["c_l2m"] = l2_window_classes()
    d["c_l2rot0"] = None
    half = np.arange(0, 64, 2, dtype=np.float64)
    inv_freq = 1.0 / 10000.0 ** (half / 64.0)
    t = np.arange(T, dtype=np.float64)
    freqs = t[:, None] * inv_freq[None, :]
    freqs = np.concatenate([freqs, freqs], -1)
    base_scale = (half + 0.4 * 64.0) / (1.4 * 64.0)
    power = (t - 4096 // 2) / (256 // 2)
    scale = base_scale[None, :] ** power[:, None]
    scale = np.concatenate([scale, scale], -1)
    cos = np.cos(freqs); sin = np.sin(freqs)
    def sperm(s):
        s2 = np.empty_like(s)
        s2[:, 0:32] = s[:, 32:64]
        s2[:, 32:64] = -s[:, 0:32]
        return s2
    csq = (cos * scale).T; ssq = sperm(sin * scale).T
    csk = (cos / scale).T; ssk = sperm(sin / scale).T
    st2 = lambda a: np.ascontiguousarray(np.concatenate([a, a], axis=0).astype(f32))
    d["c_csq"] = st2(csq); d["c_ssq"] = st2(ssq); d["c_csk"] = st2(csk); d["c_ssk"] = st2(ssk)
    r0 = np.zeros((128, 128), f32)
    for blk in range(2):
        for dd in range(64):
            r0[blk * 64 + (dd + 32) % 64, blk * 64 + dd] = 1.0
    d["c_rot0"] = r0
    d["c_ones1"] = np.ones((128, 128), f32)
    d["c_one64"] = np.ones((128, 64), f32)
    del d["c_l2rot0"]
    return d


def build_prog_l2a():
    nc = bass.Bass("TRN2", target_bir_lowering=False)
    es = contextlib.ExitStack()
    W = decl_l2(nc)
    xres_in = nc.dram_tensor("xres_in", [D, T], F32, kind="ExternalInput").ap()
    xn_out = nc.dram_tensor("xn_out", [D, T], BF16, kind="ExternalOutput").ap()
    xl_out = nc.dram_tensor("xl_out", [D, T], BF16, kind="ExternalOutput").ap()
    ao_dram = nc.dram_tensor("ao_scr", [D, T], BF16).ap()
    with es:
        ctx, bufs, ctoks = build_common(nc, es)
        ltoks = setup_l2_bufs(ctx, bufs, es)
        ctx.global_deps = ctoks + ltoks
        esA = contextlib.ExitStack()
        bufs["XN"] = esA.enter_context(nc.sbuf_tensor(uname("XN"), [128, NCH * T], BF16))
        t_ln = emit_ln(ctx, bufs, xres_in, [], xn_dram=xn_out)
        esA.close()
        esB = contextlib.ExitStack()
        XL = esB.enter_context(nc.sbuf_tensor(uname("XL"), [128, NCH * (T + TPAD)], BF16))
        t0 = emit_l2_xl_init(ctx, bufs, XL, xn_out, t_ln)
        t1 = emit_l2_inner_span(ctx, bufs, W, 0, 2, XL, ao_dram, t0)
        t2 = emit_l2_xl_save(ctx, bufs, XL, xl_out, t1)
        esB.close()
        ctx.finish_dma_waits("sy")
    return nc


def build_prog_l2mid(li0, li1):
    nc = bass.Bass("TRN2", target_bir_lowering=False)
    es = contextlib.ExitStack()
    W = decl_l2(nc)
    xl_in = nc.dram_tensor("xl_in", [D, T], BF16, kind="ExternalInput").ap()
    xl_out = nc.dram_tensor("xl_out", [D, T], BF16, kind="ExternalOutput").ap()
    ao_dram = nc.dram_tensor("ao_scr", [D, T], BF16).ap()
    with es:
        ctx, bufs, ctoks = build_common(nc, es)
        ltoks = setup_l2_bufs(ctx, bufs, es)
        ctx.global_deps = ctoks + ltoks
        esB = contextlib.ExitStack()
        XL = esB.enter_context(nc.sbuf_tensor(uname("XL"), [128, NCH * (T + TPAD)], BF16))
        t0 = emit_l2_xl_init(ctx, bufs, XL, xl_in, [])
        t1 = emit_l2_inner_span(ctx, bufs, W, li0, li1, XL, ao_dram, t0)
        t2 = emit_l2_xl_save(ctx, bufs, XL, xl_out, t1)
        esB.close()
        ctx.finish_dma_waits("sy")
    return nc


def build_prog_l2d():
    nc = bass.Bass("TRN2", target_bir_lowering=False)
    es = contextlib.ExitStack()
    W = decl_l2(nc)
    xres_in = nc.dram_tensor("xres_in", [D, T], F32, kind="ExternalInput").ap()
    xn_in = nc.dram_tensor("xn_in", [D, T], BF16, kind="ExternalInput").ap()
    xl_in = nc.dram_tensor("xl_in", [D, T], BF16, kind="ExternalInput").ap()
    xres_out = nc.dram_tensor("xres_out", [D, T], F32, kind="ExternalOutput").ap()
    g_dram = nc.dram_tensor("g_scr", [D, T], BF16).ap()
    with es:
        ctx, bufs, ctoks = build_common(nc, es)
        ltoks = setup_l2_bufs(ctx, bufs, es)
        ctx.global_deps = ctoks + ltoks
        t1 = emit_l2_post(ctx, bufs, W, xres_in, xres_out, xn_in, xl_in, g_dram, [])
        ctx.finish_dma_waits("sy")
    return nc


def build_common(nc, es, with_l0=True):
    blk = es.enter_context(nc.Block())
    ctx = Ctx(nc, blk, es)
    bufs = {}
    bufs["identf"] = es.enter_context(nc.sbuf_tensor(uname("identf"), [128, 128], F32))
    bufs["identb"] = es.enter_context(nc.sbuf_tensor(uname("identb"), [128, 128], BF16))
    bufs["inv1024"] = es.enter_context(nc.sbuf_tensor(uname("inv1024"), [128, 128], BF16))
    bufs["inv64"] = es.enter_context(nc.sbuf_tensor(uname("inv64"), [128, 64], BF16))
    bufs["m_tri"] = es.enter_context(nc.sbuf_tensor(uname("m_tri"), [128, 128], F32))
    bufs["qln"] = es.enter_context(nc.sbuf_tensor(uname("qln"), [128, 8], F32))
    bufs["cvals"] = es.enter_context(nc.sbuf_tensor(uname("cvals"), [128, 8], F32))
    cst = nc.dram_tensor("c_cst", [128, 4 * 128], F32, kind="ExternalInput").ap()
    msk = nc.dram_tensor("c_mtri", [128, 128], F32, kind="ExternalInput").ap()
    cvals_in = nc.dram_tensor("c_cvals", [128, 8], F32, kind="ExternalInput").ap()
    mr = misc_ring(ctx, "gp")
    toks = [
        mr.dma("gp", bufs["identf"][:, :], cst[:, 0:128]),
        mr.dma("gp", bufs["identb"][:, :], cst[:, 0:128]),
        mr.dma("gp", bufs["inv1024"][:, :], cst[:, 128:256]),
        mr.dma("gp", bufs["inv64"][:, :], cst[:, 256:320]),
        mr.dma("gp", bufs["m_tri"][:, :], msk),
        mr.dma("gp", bufs["cvals"][:, :], cvals_in),
    ]
    for i, v in enumerate(CVALS):
        nc.const_aps.register(F32, float(v), bufs["cvals"][:, i:i + 1])
    return ctx, bufs, toks


def decl_l3(nc):
    W = {}
    for nm, shp, dt in (("l3_wq", [D, D], F32), ("l3_wk", [D, D], F32), ("l3_wv", [D, D], F32),
                        ("l3_wo", [D, D], F32), ("l3_wg1", [D, 64], F32), ("l3_wg2", [64, D], F32),
                        ("l3_wgT", [D, 3072], I32), ("l3_wuT", [D, 3072], I32),
                        ("l3_wdT", [3072, D], I32)):
        W[nm] = nc.dram_tensor(nm, shp, dt, kind="ExternalInput").ap()
    return W


def setup_l3_bufs(ctx, bufs, es, deps=()):
    nc = ctx.nc
    CPP = 4 * 128 + 8
    bufs["l3c"] = es.enter_context(nc.sbuf_tensor(uname("l3c"), [128, 8 * CPP], F32))
    bufs["l3ab"] = es.enter_context(nc.sbuf_tensor(uname("l3ab"), [128, 16], F32))
    c1 = nc.dram_tensor("c_l3c", [128, 8 * CPP], F32, kind="ExternalInput").ap()
    c2 = nc.dram_tensor("c_l3ab", [128, 16], F32, kind="ExternalInput").ap()
    return [misc_ring(ctx, "gp").dma("gp", bufs["l3c"][:, :], c1, after=deps),
            misc_ring(ctx, "gp").dma("gp", bufs["l3ab"][:, :], c2, after=deps)]


def stage_l3(inputs):
    f = lambda a: np.ascontiguousarray(np.asarray(a, np.float32))
    tr = lambda a: np.ascontiguousarray(np.asarray(a).T)
    d = {
        "l3_wq": f(inputs["L3_attn_Wq"]), "l3_wk": f(inputs["L3_attn_Wk"]),
        "l3_wv": f(inputs["L3_attn_Wv"]), "l3_wo": f(inputs["L3_attn_Wo"]),
        "l3_wg1": f(inputs["L3_attn_Wg1"]), "l3_wg2": f(inputs["L3_attn_Wg2"]),
        "l3_wgT": np.ascontiguousarray(np.asarray(inputs["L3_ffn_w_gate"], np.int32).T),
        "l3_wuT": np.ascontiguousarray(np.asarray(inputs["L3_ffn_w_up"], np.int32).T),
        "l3_wdT": np.ascontiguousarray(np.asarray(inputs["L3_ffn_w_down"], np.int32).T),
    }
    ab = np.zeros((128, 16), np.float32)
    ab[:, 0:8] = np.asarray(inputs["L3_attn_gamma"], np.float32).reshape(8, 128).T
    ab[:, 8:16] = np.asarray(inputs["L3_attn_beta"], np.float32).reshape(8, 128).T
    d["c_l3ab"] = ab
    def build_slopes(n):
        def po2(m):
            start = 2.0 ** (-2.0 ** (-(np.log2(m) - 3)))
            return [start * start ** i for i in range(m)]
        if np.log2(n).is_integer():
            return po2(n)
        closest = 2 ** int(np.floor(np.log2(n)))
        return po2(closest) + build_slopes(2 * closest)[0::2][:n - closest]
    s = np.array(build_slopes(16), np.float64) * (1.0 - 0.0 / 11.0 + 1e-5)
    CPP = 4 * 128 + 8
    c = np.zeros((128, 8 * CPP), np.float32)
    idx = np.arange(128, dtype=np.float64)
    for p in range(8):
        o = p * CPP
        for h in range(2):
            sh = s[2 * p + h]
            diff = idx[None, :] - idx[:, None]
            m = np.where(diff >= 0, np.exp(-sh * diff), 0.0)
            c[:, o + h * 128: o + (h + 1) * 128] = m.astype(np.float32)
        lq = np.zeros((128, 128))
        lk = np.zeros((128, 128))
        for h in range(2):
            sh = s[2 * p + h]
            lq[h * 64:(h + 1) * 64, :] = np.exp(-sh * (idx[None, :] + 1.0))
            lk[:, h * 64:(h + 1) * 64] = np.exp(-sh * (127.0 - idx[:, None]))
        c[:, o + 256: o + 384] = lq.astype(np.float32)
        c[:, o + 384: o + 512] = lk.astype(np.float32)
        for h in range(2):
            c[h * 64:(h + 1) * 64, o + 512] = np.float32(np.exp(-s[2 * p + h] * 128.0))
    d["c_l3c"] = c
    return d


def build_prog_l3():
    nc = bass.Bass("TRN2", target_bir_lowering=False)
    es = contextlib.ExitStack()
    W = decl_l3(nc)
    xres_in = nc.dram_tensor("xres_in", [D, T], F32, kind="ExternalInput").ap()
    out = nc.dram_tensor("out", [T, D], F32, kind="ExternalOutput").ap()
    xres_out = nc.dram_tensor("xres_out_d", [D, T], F32).ap()
    g_dram = nc.dram_tensor("g_scr", [D, T], BF16).ap()
    gate_dram = nc.dram_tensor("gate_scr", [D, T], BF16).ap()
    with es:
        ctx, bufs, ctoks = build_common(nc, es)
        ltoks = setup_l3_bufs(ctx, bufs, es)
        ctx.global_deps = ctoks + ltoks
        t1 = emit_layer3(ctx, bufs, W, xres_in, xres_out, {"g": g_dram, "gate": gate_dram}, [])
        t2 = emit_final_transpose(ctx, bufs, xres_out, out, t1)
        ctx.finish_dma_waits("sy")
    return nc


def decl_l0(nc):
    W = {}
    for nm, shp in (("L0_attn_WqT", [D, D]), ("L0_attn_WkT", [D, D]), ("L0_attn_WvT", [D, D]),
                    ("L0_attn_WuT", [D, D]), ("L0_ffn_Win", [D, 64]), ("L0_ffn_Wout", [64, D]),
                    ("L0_ffn_W1", [64, 16]), ("L0_ffn_b1", [64, 16]), ("L0_ffn_W2", [64, 16])):
        W[nm] = nc.dram_tensor(nm, shp, F32, kind="ExternalInput").ap()
    for nm in ("L0_ffn_bin_", "L0_ffn_b2"):
        W[nm] = nc.dram_tensor(nm, [64], F32, kind="ExternalInput").ap()
    W["L0_ffn_boutT"] = nc.dram_tensor("L0_ffn_boutT", [128, NCH], F32, kind="ExternalInput").ap()
    W["qln8"] = nc.dram_tensor("qln8", [128, 8], F32, kind="ExternalInput").ap()
    return W


def build_prog_l0():
    nc = bass.Bass("TRN2", target_bir_lowering=False)
    es = contextlib.ExitStack()
    x0 = nc.dram_tensor("x0", [T, D], F32, kind="ExternalInput").ap()
    W = decl_l0(nc)

    xres_out = nc.dram_tensor("xres_out", [D, T], F32, kind="ExternalOutput").ap()
    xres_mid = nc.dram_tensor("xres_mid", [D, T], F32).ap()
    g_dram = nc.dram_tensor("g_scr", [D, T], BF16).ap()

    with es:
        ctx, bufs, ctoks = build_common(nc, es)
        qtok = misc_ring(ctx, "gp").dma("gp", bufs["qln"][:, :], W["qln8"])
        ctx.global_deps = ctoks + [qtok]
        t0 = emit_x0_transpose(ctx, bufs, x0, xres_mid)
        t1 = emit_layer0(ctx, bufs, W, xres_mid, xres_out, {"g": g_dram}, t0 + ctoks + [qtok])
        ctx.finish_dma_waits("sy")
    return nc


def make_consts():
    ident = np.eye(128, dtype=np.float32)
    ones1024 = np.full((128, 128), 1.0 / 1024.0, np.float32)
    ones64 = np.full((128, 128), 1.0 / 64.0, np.float32)
    cst = np.concatenate([ident, ones1024, ones64, np.zeros((128, 128), np.float32)], axis=1)
    tri = np.where(np.arange(128)[None, :] <= np.arange(128)[:, None], 0.0, NEG).astype(np.float32)
    cvals = np.zeros((128, 8), np.float32)
    for i, v in enumerate(CVALS):
        cvals[:, i] = v
    return cst, tri, cvals


def stage_l0(inputs):
    tr = lambda a: np.ascontiguousarray(np.asarray(a).T.astype(np.float32))
    d = {
        "L0_attn_WqT": tr(inputs["L0_attn_Wq"]),
        "L0_attn_WkT": tr(inputs["L0_attn_Wk"]),
        "L0_attn_WvT": tr(inputs["L0_attn_Wv"]),
        "L0_attn_WuT": tr(inputs["L0_attn_Wu"]),
        "L0_ffn_Win": np.ascontiguousarray(inputs["L0_ffn_Win"], np.float32),
        "L0_ffn_Wout": np.ascontiguousarray(inputs["L0_ffn_Wout"], np.float32),
        "L0_ffn_W1": np.ascontiguousarray(inputs["L0_ffn_W1"], np.float32),
        "L0_ffn_b1": np.ascontiguousarray(inputs["L0_ffn_b1"], np.float32),
        "L0_ffn_W2": np.ascontiguousarray(inputs["L0_ffn_W2"], np.float32),
        "L0_ffn_bin_": np.ascontiguousarray(inputs["L0_ffn_bin_"], np.float32),
        "L0_ffn_b2": np.ascontiguousarray(inputs["L0_ffn_b2"], np.float32),
        "L0_ffn_boutT": np.ascontiguousarray(np.asarray(inputs["L0_ffn_bout"], np.float32).reshape(8, 128).T),
    }
    qw = np.concatenate([inputs["L0_attn_qln_w"], inputs["L0_attn_qln_w"]]).astype(np.float32)
    qb = np.concatenate([inputs["L0_attn_qln_b"], inputs["L0_attn_qln_b"]]).astype(np.float32)
    kw = np.concatenate([inputs["L0_attn_kln_w"], inputs["L0_attn_kln_w"]]).astype(np.float32)
    kb = np.concatenate([inputs["L0_attn_kln_b"], inputs["L0_attn_kln_b"]]).astype(np.float32)
    qln8 = np.zeros((128, 8), np.float32)
    qln8[:, 0] = qw; qln8[:, 1] = qb; qln8[:, 2] = kw; qln8[:, 3] = kb
    qln8[:, 4] = qw * 0.125; qln8[:, 5] = qb * 0.125
    d["qln8"] = qln8
    return d


def emit_layer2_full(ctx, bufs, W, xres_in, xres_out, deps):
    nc = ctx.nc
    xn2 = nc.dram_tensor(uname("xn2d"), [D, T], BF16).ap()
    xl2 = nc.dram_tensor(uname("xl2d"), [D, T], BF16).ap()
    ao2 = nc.dram_tensor(uname("ao2d"), [D, T], BF16).ap()
    g2 = nc.dram_tensor(uname("g2d"), [D, T], BF16).ap()
    esA = contextlib.ExitStack()
    bufs["XN"] = esA.enter_context(nc.sbuf_tensor(uname("XN"), [128, NCH * T], BF16))
    t_ln = emit_ln(ctx, bufs, xres_in, deps, xn_dram=xn2)
    esA.close()
    esB = contextlib.ExitStack()
    XL = esB.enter_context(nc.sbuf_tensor(uname("XL"), [128, NCH * (T + TPAD)], BF16))
    t0 = emit_l2_xl_init(ctx, bufs, XL, xn2, t_ln)
    t1 = emit_l2_inner_span(ctx, bufs, W, 0, 6, XL, ao2, t0)
    t2 = emit_l2_xl_save(ctx, bufs, XL, xl2, t1)
    esB.close()
    return emit_l2_post(ctx, bufs, W, xres_in, xres_out, xn2, xl2, g2, t2)


def build_prog_span(a, b):
    nc = bass.Bass("TRN2", target_bir_lowering=False)
    es = contextlib.ExitStack()
    decls = (decl_l0, decl_l1, decl_l2, decl_l3)
    Ws = {li: decls[li](nc) for li in range(a, b)}
    if a == 0:
        x0 = nc.dram_tensor("x0", [T, D], F32, kind="ExternalInput").ap()
    else:
        xin = nc.dram_tensor("xspan_in", [D, T], F32, kind="ExternalInput").ap()
    if b == 4:
        out = nc.dram_tensor("out", [T, D], F32, kind="ExternalOutput").ap()
    else:
        xout = nc.dram_tensor("xspan_out", [D, T], F32, kind="ExternalOutput").ap()
    with es:
        ctx, bufs, ctoks = build_common(nc, es)
        base_gd = list(ctoks)
        ctx.global_deps = base_gd
        if a == 0:
            xcur = nc.dram_tensor("xres_tm", [D, T], F32).ap()
            t = emit_x0_transpose(ctx, bufs, x0, xcur)
        else:
            xcur = xin
            t = []
        for li in range(a, b):
            xdst = xout if (li == b - 1 and b < 4) else nc.dram_tensor(uname(f"xr{li}"), [D, T], F32).ap()
            les = contextlib.ExitStack()
            if li == 0:
                qtok = misc_ring(ctx, "gp").dma("gp", bufs["qln"][:, :], Ws[0]["qln8"])
                ctx.global_deps = base_gd + [qtok]
                g0 = nc.dram_tensor(uname("g0d"), [D, T], BF16).ap()
                t = emit_layer0(ctx, bufs, Ws[0], xcur, xdst, {"g": g0}, t + [qtok])
            elif li == 1:
                ltoks = setup_l1_bufs(ctx, bufs, les, deps=t)
                ctx.global_deps = base_gd + ltoks
                g1 = nc.dram_tensor(uname("g1d"), [D, T], BF16).ap()
                xn1 = nc.dram_tensor(uname("xn1d"), [D, T], BF16).ap()
                gacc1 = nc.dram_tensor(uname("gacc1d"), [D, T], BF16).ap()
                t = emit_layer1(ctx, bufs, Ws[1], xcur, xdst,
                                {"g": g1, "xn": xn1, "gacc": gacc1}, t + ltoks)
            elif li == 2:
                ltoks = setup_l2_bufs(ctx, bufs, les, deps=t)
                ctx.global_deps = base_gd + ltoks
                t = emit_layer2_full(ctx, bufs, Ws[2], xcur, xdst, t + ltoks)
            else:
                ltoks = setup_l3_bufs(ctx, bufs, les, deps=t)
                ctx.global_deps = base_gd + ltoks
                g3 = nc.dram_tensor(uname("g3d"), [D, T], BF16).ap()
                gate3 = nc.dram_tensor(uname("gate3d"), [D, T], BF16).ap()
                t = emit_layer3(ctx, bufs, Ws[3], xcur, xdst, {"g": g3, "gate": gate3}, t + ltoks)
            ctx.global_deps = base_gd
            les.close()
            xcur = xdst
        if b == 4:
            t = emit_final_transpose(ctx, bufs, xcur, out, t)
        ctx.finish_dma_waits("sy")
    return nc


SPANS = [(0, 4)]


def kernel(**inputs):
    cst, tri, cvals = make_consts()
    com = {"c_cst": cst, "c_mtri": tri, "c_cvals": cvals}
    x = np.ascontiguousarray(np.asarray(inputs["L0_attn_x"], np.float32))
    stagers = (stage_l0, stage_l1, stage_l2, stage_l3)
    stg = {}
    cur = None
    for (a, b) in SPANS:
        nc = build_prog_span(a, b)
        for li in range(a, b):
            if li not in stg:
                stg[li] = stagers[li](inputs)
        maps = []
        for c in range(8):
            m = dict(com)
            for li in range(a, b):
                m.update(stg[li])
            if a == 0:
                m["x0"] = np.ascontiguousarray(x[c])
            else:
                m["xspan_in"] = cur[c]
            maps.append(m)
        r = run_bass_kernel_spmd(nc, maps, core_ids=list(range(8)))
        if b == 4:
            return np.stack([np.asarray(r.results[c]["out"]) for c in range(8)])
        cur = [np.asarray(r.results[c]["xspan_out"]) for c in range(8)]


# version 09 (final)
# speedup vs baseline: 1.1083x; 1.1083x over previous
import contextlib
import math
import numpy as np
import ml_dtypes
import concourse.bass as bass
import concourse.mybir as mybir
from concourse.bass_utils import run_bass_kernel_spmd

try:
    import resource
    _soft, _hard = resource.getrlimit(resource.RLIMIT_STACK)
    resource.setrlimit(resource.RLIMIT_STACK, (_hard, _hard))
except Exception:
    pass

F32 = mybir.dt.float32
BF16 = mybir.dt.bfloat16
I32 = mybir.dt.int32
AF = mybir.ActivationFunctionType
ALU = mybir.AluOpType
AX = mybir.AxisListType

T = 4096
D = 1024
NCH = 8
NTT = 8
TTW = 512
NTB = 32
NPAIR = 8
NEG = -1.0e30
CVALS = [1e-5, 1e-12, 1e-24, 0.5]

BF = ml_dtypes.bfloat16


NEXP = 8


def set_dims(t, nexp=8):
    global T, NTT, NTB, NEXP
    T = t
    NTT = t // TTW
    NTB = t // 128
    NEXP = nexp


_NAMEC = [0]
def uname(p):
    _NAMEC[0] += 1
    return f"{p}_{_NAMEC[0]}"


class Ctx:
    ENGS = ("te", "dv", "ac", "gp", "sy")

    def __init__(self, nc, blk, es):
        self.nc = nc
        self.blk = blk
        self.es = es
        self.h = {}
        self.cnt = {}
        self.lw = {}
        self.own = {"te": "pe", "dv": "dv", "ac": "ac", "gp": "gp", "sy": "sy"}
        for s in ("pe", "dv", "ac", "gp", "sy"):
            self.h[s] = es.enter_context(nc.semaphore(s))
            self.cnt[s] = 0
        self.global_deps = []
        self.dma_pool = {True: [], False: []}
        self.dma_all = []
        self._dq = 0

    def acquire_dsem(self, sw):
        if self.dma_pool[sw]:
            return self.dma_pool[sw].pop()
        name = f"q{self._dq}"
        self._dq += 1
        self.h[name] = self.es.enter_context(self.nc.semaphore(name))
        self.cnt[name] = 0
        self.dma_all.append(name)
        return name

    def release_dsem(self, name, sw):
        self.dma_pool[sw].append(name)

    def _bf(self, eng):
        return {"te": self.blk.tensor, "dv": self.blk.vector, "ac": self.blk.scalar,
                "gp": self.blk.gpsimd, "sy": self.blk.sync}[eng]

    def run(self, eng, fn, after=(), tok=False, dma=False, dsem=None):
        after = [a for a in list(after) + self.global_deps if a is not None]
        if dma:
            assert dsem is not None
            semname = dsem
            k = 16
        else:
            semname = self.own[eng]
            k = 1
        newtok = None
        if tok:
            self.cnt[semname] += k
            newtok = (semname, self.cnt[semname])

        def body(e):
            for (s, v) in after:
                if self.lw.get((eng, s), 0) >= v:
                    continue
                e.wait_ge(self.h[s], v)
                self.lw[(eng, s)] = v
            inst = fn(e)
            if tok:
                inst.then_inc(self.h[semname], k)

        self._bf(eng)(body)
        return newtok

    def finish_dma_waits(self, eng="sy"):
        for s in self.dma_all:
            if self.cnt[s] > 0:
                self.run(eng, lambda e, s=s: e.wait_ge(self.h[s], self.cnt[s]))


class Ring:

    def __init__(self, ctx, depth=2, sw=False):
        self.ctx = ctx
        self.sw = sw
        self.sems = [ctx.acquire_dsem(sw) for _ in range(depth)]
        self.i = 0

    def dma(self, eng, dst, src, after=()):
        assert (eng == "gp") == self.sw
        s = self.sems[self.i % len(self.sems)]
        self.i += 1
        prev = (s, self.ctx.cnt[s]) if self.ctx.cnt[s] > 0 else None
        return self.ctx.run(eng, lambda e: e.dma_start(dst, src),
                            after=list(after) + [prev], tok=True, dma=True, dsem=s)

    def close(self):
        for s in self.sems:
            self.ctx.release_dsem(s, self.sw)


def misc_ring(ctx, eng="sy"):
    if not hasattr(ctx, "_misc"):
        ctx._misc = {}
    if eng not in ctx._misc:
        ctx._misc[eng] = Ring(ctx, 2, sw=(eng == "gp"))
    return ctx._misc[eng]


class Arena:
    def __init__(self, handle, slots):
        self.h = handle
        self.slots = slots
        self.off = 0
        self.marks = []

    def alloc(self, n):
        assert self.off + n <= self.slots, f"arena overflow {self.off}+{n}>{self.slots}"
        o = self.off
        self.off += n
        return o

    def push(self):
        self.marks.append(self.off)

    def pop(self):
        self.off = self.marks.pop()

    def view(self, off, n):
        return self.h[:, off:off + n]


def emit_x0_transpose(ctx, bufs, x0_ap, xres_ap):
    nc = ctx.nc
    xres_r = xres_ap.rearrange("(ch p) t -> p ch t", p=128)
    with contextlib.ExitStack() as es:
        xt = es.enter_context(nc.sbuf_tensor(uname("i_xt"), [128, 2 * 1024], F32))
        xo = es.enter_context(nc.sbuf_tensor(uname("i_xo"), [128, 2 * 1024], F32))
        pst = es.enter_context(nc.psum_tensor(uname("i_ps"), [128, 1024], F32))
        rld = Ring(ctx, 2)
        rst = Ring(ctx, 2)
        ld = [None, None]
        evb = [None, None]
        evg = {0: None, 1: None}
        stw = [None, None]
        for tb in range(NTB):
            b = tb % 2
            ld[b] = rld.dma("sy", xt[:, b * 1024:(b + 1) * 1024],
                            x0_ap[tb * 128:(tb + 1) * 128, :],
                            after=[evb[b]])
            for g in range(2):
                tk = None
                for c4 in range(4):
                    ch = g * 4 + c4
                    tk = ctx.run("te", lambda e, ch=ch, b=b, g=g, c4=c4: e.transpose(
                        pst[:, g * 512 + c4 * 128: g * 512 + c4 * 128 + 128],
                        xt[:, b * 1024 + ch * 128: b * 1024 + ch * 128 + 128],
                        bufs["identf"][:, :]),
                        after=[ld[b], evg[g], stw[b]], tok=(c4 == 3))
                evg[g] = ctx.run("dv", lambda e, b=b, g=g: e.tensor_copy(
                    xo[:, b * 1024 + g * 512: b * 1024 + g * 512 + 512],
                    pst[:, g * 512:g * 512 + 512]),
                    after=[tk], tok=True)
            evb[b] = evg[1]
            stw[b] = rst.dma("sy", xres_r[:, :, tb * 128:(tb + 1) * 128],
                             xo[:, b * 1024:(b + 1) * 1024].rearrange("p (ch m) -> p ch m", m=128),
                             after=[evg[0], evg[1]])
        rld.close(); rst.close()
        return [stw[0], stw[1]]


def emit_final_transpose(ctx, bufs, xres_ap, out_ap, deps):
    nc = ctx.nc
    xres_r = xres_ap.rearrange("(ch p) t -> p ch t", p=128)
    with contextlib.ExitStack() as es:
        xt = es.enter_context(nc.sbuf_tensor(uname("f_xt"), [128, 2 * 1024], F32))
        xo = es.enter_context(nc.sbuf_tensor(uname("f_xo"), [128, 2 * 1024], F32))
        pst = es.enter_context(nc.psum_tensor(uname("f_ps"), [128, 1024], F32))
        rld = Ring(ctx, 2)
        rst = Ring(ctx, 2)
        ld = [None, None]
        evb = [None, None]
        evg = {0: None, 1: None}
        stw = [None, None]
        for tb in range(NTB):
            b = tb % 2
            ld[b] = rld.dma("sy", xt[:, b * 1024:(b + 1) * 1024].rearrange("p (ch m) -> p ch m", m=128),
                            xres_r[:, :, tb * 128:(tb + 1) * 128],
                            after=[evb[b]] + (deps if tb < 2 else []))
            for g in range(2):
                tk = None
                for c4 in range(4):
                    ch = g * 4 + c4
                    tk = ctx.run("te", lambda e, ch=ch, b=b, g=g, c4=c4: e.transpose(
                        pst[:, g * 512 + c4 * 128: g * 512 + c4 * 128 + 128],
                        xt[:, b * 1024 + ch * 128: b * 1024 + ch * 128 + 128],
                        bufs["identf"][:, :]),
                        after=[ld[b], evg[g], stw[b]], tok=(c4 == 3))
                evg[g] = ctx.run("dv", lambda e, b=b, g=g: e.tensor_copy(
                    xo[:, b * 1024 + g * 512: b * 1024 + g * 512 + 512],
                    pst[:, g * 512:g * 512 + 512]),
                    after=[tk], tok=True)
            evb[b] = evg[1]
            stw[b] = rst.dma("sy", out_ap[tb * 128:(tb + 1) * 128, :],
                             xo[:, b * 1024:(b + 1) * 1024], after=[evg[0], evg[1]])
        rld.close(); rst.close()
        return [stw[0], stw[1]]


def emit_lnx(ctx, bufs, deps, produce, consume, eps=1e-5, bf16_src=None, sub_mean=True):
    nc = ctx.nc
    out_toks = []
    with contextlib.ExitStack() as es:
        XS = es.enter_context(nc.sbuf_tensor(uname("lx_xs"), [128, 4096], BF16))
        SCR = es.enter_context(nc.sbuf_tensor(uname("lx_scr"), [128, 4096], F32))
        MU = es.enter_context(nc.sbuf_tensor(uname("lx_mu"), [128, 512], F32))
        M2 = es.enter_context(nc.sbuf_tensor(uname("lx_m2"), [128, 512], F32))
        VAR = es.enter_context(nc.sbuf_tensor(uname("lx_var"), [128, 512], F32))
        RST = es.enter_context(nc.sbuf_tensor(uname("lx_rst"), [128, 512], F32))
        psmu = es.enter_context(nc.psum_tensor(uname("lx_psmu"), [128, 512], F32))
        psms = es.enter_context(nc.psum_tensor(uname("lx_psms"), [128, 512], F32))
        prev_cons = None
        prev_mm = None
        prev_sq = None
        prev_e5 = None
        prev_e1 = None
        prev_e3 = None
        for tt in range(NTT):
            t0 = tt * TTW
            if produce is not None:
                ptok, getsrc = produce(tt, [prev_cons, prev_mm, prev_sq] + (list(deps) if tt == 0 else []))
            else:
                ptok, getsrc = None, (lambda ch, t0=t0: bf16_src[:, ch * T + t0: ch * T + t0 + TTW])
            sq = None
            mm1 = None
            mm2 = None
            for ch in range(NCH):
                sq = ctx.run("ac", lambda e, ch=ch: e.activation(
                    XS[:, ch * 512:ch * 512 + 512], getsrc(ch), AF.Square),
                    after=[ptok, prev_mm] + (list(deps) if tt == 0 else []), tok=True)
            prev_sq = sq
            if sub_mean:
                for ch in range(NCH):
                    mm1 = ctx.run("te", lambda e, ch=ch: e.matmul(
                        psmu[:, :], bufs["inv1024"][:, :], getsrc(ch),
                        start=(ch == 0), stop=(ch == NCH - 1)),
                        after=[ptok, prev_e1] + (list(deps) if tt == 0 else []), tok=(ch == NCH - 1))
            for ch in range(NCH):
                mm2 = ctx.run("te", lambda e, ch=ch: e.matmul(
                    psms[:, :], bufs["inv1024"][:, :], XS[:, ch * 512:ch * 512 + 512],
                    start=(ch == 0), stop=(ch == NCH - 1)),
                    after=[sq, prev_e3], tok=(ch == NCH - 1))
            prev_mm = mm2
            if sub_mean:
                e1 = ctx.run("dv", lambda e: e.tensor_copy(MU[:, :], psmu[:, :]),
                             after=[mm1, prev_e5, prev_cons], tok=True)
                prev_e1 = e1
                e2 = ctx.run("ac", lambda e: e.activation(M2[:, :], MU[:, :], AF.Square), after=[e1], tok=True)
                e3 = ctx.run("dv", lambda e: e.tensor_sub(VAR[:, :], psms[:, :], M2[:, :]), after=[mm2, e2], tok=True)
            else:
                e3 = ctx.run("dv", lambda e: e.tensor_copy(VAR[:, :], psms[:, :]),
                             after=[mm2, prev_e5, prev_cons], tok=True)
            prev_e3 = e3
            e4 = ctx.run("ac", lambda e: e.activation(VAR[:, :], VAR[:, :], AF.Ln, bias=float(eps)), after=[e3], tok=True)
            e5 = ctx.run("ac", lambda e: e.activation(RST[:, :], VAR[:, :], AF.Exp, scale=-0.5),
                         after=[e4, prev_cons], tok=True)
            prev_e5 = e5
            ctok = None
            for ch in range(NCH):
                if sub_mean:
                    a1 = ctx.run("dv", lambda e, ch=ch: e.tensor_sub(
                        SCR[:, ch * 512:ch * 512 + 512], getsrc(ch), MU[:, :]),
                        after=[e5], tok=True)
                else:
                    a1 = ctx.run("dv", lambda e, ch=ch: e.tensor_copy(
                        SCR[:, ch * 512:ch * 512 + 512], getsrc(ch)),
                        after=[e5], tok=True)
                ctok = consume(tt, ch, SCR[:, ch * 512:ch * 512 + 512], RST[:, 0:512], [a1])
            prev_cons = ctok
            out_toks.append(ctok)
    return out_toks


def emit_ln(ctx, bufs, xsrc_ap, deps, xn_dram=None, eps=1e-5):
    nc = ctx.nc
    xsrc_r = xsrc_ap.rearrange("(ch p) t -> p ch t", p=128)
    XN = bufs["XN"]
    es = contextlib.ExitStack()
    XT = es.enter_context(nc.sbuf_tensor(uname("ln_xt"), [128, 4096], F32))
    XB = es.enter_context(nc.sbuf_tensor(uname("ln_xb"), [128, 4096], BF16))
    rlt = Ring(ctx, 1)
    rxw = Ring(ctx, 2)
    state = {"wtk": None}

    def produce(tt, after):
        t0 = tt * TTW
        lt = rlt.dma("sy", XT[:, :].rearrange("p (ch m) -> p ch m", m=512),
                     xsrc_r[:, :, t0:t0 + TTW], after=after + [state["wtk"]])
        ck = ctx.run("dv", lambda e: e.tensor_copy(XB[:, :], XT[:, :]), after=[lt] + after, tok=True)
        def getsrc(ch, which="stats"):
            return XB[:, ch * 512:ch * 512 + 512]
        return ck, getsrc

    def consume(tt, ch, scr, rst, after):
        t0 = tt * TTW
        mtok = ctx.run("dv", lambda e: e.tensor_mul(
            XN[:, ch * T + t0: ch * T + t0 + 512], scr, rst), after=after, tok=True)
        if xn_dram is not None and ch == NCH - 1:
            state["wtk"] = rxw.dma("sy", xn_dram.rearrange("(ch p) t -> p ch t", p=128)[:, :, t0:t0 + TTW],
                                   XN[:, :].rearrange("p (ch t) -> p ch t", t=T)[:, :, t0:t0 + TTW],
                                   after=[mtok])
            return state["wtk"]
        return mtok

    toks = emit_lnx(ctx, bufs, deps, produce, consume, eps=eps)
    rlt.close(); rxw.close()
    es.close()
    return toks


def wslab_load(ctx, ring, dst, w_ap, c0, cols, after=()):
    din = w_ap.shape[0]
    if din <= 128:
        return ring.dma("gp", dst[0:din, 0:cols], w_ap[:, c0:c0 + cols], after=after)
    assert din % 128 == 0
    src = w_ap[:, c0:c0 + cols].rearrange("(kc p) m -> p kc m", p=128)
    dstr = dst[:, 0:(din // 128) * cols].rearrange("p (kc m) -> p kc m", m=cols)
    return ring.dma("gp", dstr, src, after=after)


def emit_proj_fm(ctx, bufs, w_ap, src_sb, src_off, dout_list, evac, deps,
                 din=D, src_stride=None, nt=None, tw=TTW, wdma="gp"):
    nc = ctx.nc
    if src_stride is None:
        src_stride = T
    if nt is None:
        nt = NTT
    din_ch = (din + 127) // 128
    ksz = [min(128, din - 128 * kc) for kc in range(din_ch)]
    toks = []
    with contextlib.ExitStack() as es:
        wsb = [es.enter_context(nc.sbuf_tensor(uname(f"pf_w{i}"), [128, din_ch * 128], BF16)) for i in range(2)]
        pss = [es.enter_context(nc.psum_tensor(uname(f"pf_ps{i}"), [128, tw], F32)) for i in range(2)]
        rw = Ring(ctx, 2, sw=True)
        wld = [None, None]
        lastmm = [None, None]
        ev = [None, None]
        for di, (c0, cols) in enumerate(dout_list):
            wb = di % 2
            wld[wb] = wslab_load(ctx, rw, wsb[wb][:, 0:din_ch * cols], w_ap, c0, cols,
                                 after=[lastmm[wb]] + (list(deps) if di < 2 else []))
            for tt in range(nt):
                pb = tt % 2
                mm = None
                for kc in range(din_ch):
                    mm = ctx.run("te", lambda e, kc=kc, wb=wb, pb=pb, tt=tt, cols=cols: e.matmul(
                        pss[pb][0:cols, :],
                        wsb[wb][0:ksz[kc], kc * cols:kc * cols + cols],
                        src_sb[0:ksz[kc], src_off + kc * src_stride + tt * tw:
                               src_off + kc * src_stride + tt * tw + tw],
                        start=(kc == 0), stop=(kc == din_ch - 1)),
                        after=[wld[wb], ev[pb]] + (list(deps) if di == 0 and tt == 0 else []),
                        tok=(kc == din_ch - 1))
                lastmm[wb] = mm
                ev[pb] = evac(di, tt, pss[pb][0:cols, :], mm)
                toks.append(ev[pb])
        rw.close()
    return toks[-2:]


def emit_l0ffn(ctx, bufs, W, deps, g_dram):
    nc = ctx.nc
    toks = []
    with contextlib.ExitStack() as es:
        SIN = es.enter_context(nc.sbuf_tensor(uname("f0_sin"), [128, T], F32))
        HJ = es.enter_context(nc.sbuf_tensor(uname("f0_hj"), [128, T], F32))
        GA = es.enter_context(nc.sbuf_tensor(uname("f0_ga"), [128, T], F32))
        GB = es.enter_context(nc.sbuf_tensor(uname("f0_gb"), [128, T], BF16))
        PAR = es.enter_context(nc.sbuf_tensor(uname("f0_par"), [128, 4], F32))
        W1C = es.enter_context(nc.sbuf_tensor(uname("f0_w1c"), [128, 64], F32))
        B1C = es.enter_context(nc.sbuf_tensor(uname("f0_b1c"), [128, 64], F32))
        W2C = es.enter_context(nc.sbuf_tensor(uname("f0_w2c"), [128, 64], F32))
        GT = es.enter_context(nc.sbuf_tensor(uname("f0_gt"), [128, 2 * TTW], BF16))
        BOUT = es.enter_context(nc.sbuf_tensor(uname("f0_bout"), [128, NCH], F32))
        l1 = misc_ring(ctx).dma("sy", W1C[0:64, 0:16], W["L0_ffn_W1"], after=deps)
        l2 = misc_ring(ctx).dma("sy", B1C[0:64, 0:16], W["L0_ffn_b1"], after=deps)
        l3 = misc_ring(ctx).dma("sy", W2C[0:64, 0:16], W["L0_ffn_W2"], after=deps)
        l4 = misc_ring(ctx).dma("sy", PAR[0:64, 0:1], W["L0_ffn_bin_"].rearrange("(a b) -> b a", a=1), after=deps)
        l5 = misc_ring(ctx).dma("sy", PAR[0:64, 1:2], W["L0_ffn_b2"].rearrange("(a b) -> b a", a=1), after=deps)
        l6 = misc_ring(ctx).dma("sy", BOUT[:, :], W["L0_ffn_boutT"], after=deps)
        ptoks = [l1, l2, l3, l4, l5, l6]

        def ev_sin(di, tt, ps, mm):
            return ctx.run("dv", lambda e: e.tensor_scalar(
                SIN[0:64, tt * TTW:(tt + 1) * TTW], ps, PAR[0:64, 0:1], None, ALU.add),
                after=[mm] + ptoks, tok=True)
        t1 = emit_proj_fm(ctx, bufs, W["L0_ffn_Win"], bufs["XN"], 0, [(0, 64)], ev_sin, deps)
        htok = None
        gtok = t1[-1]
        atk = None
        for j in range(16):
            htok = ctx.run("ac", lambda e, j=j: e.activation(
                HJ[0:64, :], SIN[0:64, :], AF.Relu, bias=B1C[0:64, j:j + 1], scale=W1C[0:64, j:j + 1]),
                after=t1 + [atk], tok=True)
            if j == 0:
                atk = ctx.run("dv", lambda e: e.tensor_scalar(
                    GA[0:64, :], HJ[0:64, :], W2C[0:64, 0:1], None, ALU.mult),
                    after=[htok], tok=True)
            else:
                atk = ctx.run("dv", lambda e, j=j: e.scalar_tensor_tensor(
                    GA[0:64, :], HJ[0:64, :], W2C[0:64, j:j + 1], GA[0:64, :],
                    ALU.mult, ALU.add),
                    after=[htok], tok=True)
        gtok = ctx.run("dv", lambda e: e.tensor_scalar(
            GB[0:64, :], GA[0:64, :], PAR[0:64, 1:2], None, ALU.add),
            after=[atk], tok=True)
        g_r = g_dram.rearrange("(ch p) t -> p ch t", p=128)
        rgs = Ring(ctx, 2)
        st = [None]

        def ev_out(di, tt, ps, mm):
            b = (di * NTT + tt) % 2
            ak = ctx.run("ac", lambda e, di=di, tt=tt, b=b: e.activation(
                GT[:, b * TTW:(b + 1) * TTW], ps, AF.Silu, bias=BOUT[:, di:di + 1]),
                after=[mm, st[0]], tok=True)
            st[0] = rgs.dma("sy", g_r[:, di, tt * TTW:(tt + 1) * TTW], GT[:, b * TTW:(b + 1) * TTW],
                            after=[ak])
            return st[0]
        t2 = emit_proj_fm(ctx, bufs, W["L0_ffn_Wout"], GB, 0, [(c * 128, 128) for c in range(NCH)],
                          ev_out, [gtok] + ptoks, din=64)
        toks = t2 + [st[0]]
        rgs.close()
    return toks


def emit_attn_pairs(ctx, bufs, cfg, deps):
    nc = ctx.nc
    toks = []
    with contextlib.ExitStack() as es:
        QB = es.enter_context(nc.sbuf_tensor(uname("at_qb"), [128, T], BF16))
        KB = es.enter_context(nc.sbuf_tensor(uname("at_kb"), [128, T], BF16))
        VB = es.enter_context(nc.sbuf_tensor(uname("at_vb"), [128, T], BF16))
        MAXK = cfg.get("maxk", T)
        PB = es.enter_context(nc.sbuf_tensor(uname("at_pb"), [128, 4 * MAXK], BF16))
        AOD = cfg.get("ao_dram")
        AOP = es.enter_context(nc.sbuf_tensor(uname("at_aop"), [128, T], BF16)) if AOD is not None else None
        rao = Ring(ctx, 2) if AOD is not None else None
        PT = es.enter_context(nc.sbuf_tensor(uname("at_pt"), [128, 512], BF16))
        NM = [es.enter_context(nc.sbuf_tensor(uname("at_nm"), [128, 16], F32)) for _ in range(2)]
        SM = [es.enter_context(nc.sbuf_tensor(uname("at_sm"), [128, 16], F32)) for _ in range(2)]
        FC = [es.enter_context(nc.sbuf_tensor(uname("at_fc"), [128, 16], F32)) for _ in range(2)]
        WSB = [es.enter_context(nc.sbuf_tensor(uname(f"at_w{i}"), [128, NCH * 128], BF16)) for i in range(3)]
        psq = [es.enter_context(nc.psum_tensor(uname(f"at_psq{i}"), [128, TTW], F32)) for i in range(2)]
        psS = [es.enter_context(nc.psum_tensor(uname(f"at_s{i}"), [128, 1024], F32)) for i in range(2)]
        psT = es.enter_context(nc.psum_tensor(uname("at_pt_ps"), [128, 1024], BF16))
        psO = es.enter_context(nc.psum_tensor(uname("at_o"), [128, 512], F32))

        SC = 1024
        rw = Ring(ctx, 3, sw=True)
        zf = None
        for bb in (NM[0], NM[1], SM[0], SM[1], FC[0], FC[1]):
            zf = ctx.run("dv", lambda e, bb=bb: e.memset(bb[:, :], 1.0), after=deps, tok=zf is None or True)
        prev_pair_toks = list(deps) + [zf]
        for pair in range(NPAIR):
            wq = wslab_load(ctx, rw, WSB[0], cfg["wq_ap"], cfg.get("wq_off", 0) + pair * 128, 128, after=prev_pair_toks)
            wk = wslab_load(ctx, rw, WSB[1], cfg["wk_ap"], cfg.get("wk_off", 0) + pair * 128, 128, after=prev_pair_toks)
            wv = wslab_load(ctx, rw, WSB[2], cfg["wv_ap"], cfg.get("wv_off", 0) + pair * 128, 128, after=prev_pair_toks)
            qtoks = []
            evq = {0: None, 1: None}
            for tt in range(NTT):
                pb = tt % 2
                mmq = None
                for kc in range(NCH):
                    mmq = ctx.run("te", lambda e, kc=kc, tt=tt, pb=pb: e.matmul(
                        psq[pb][:, :], WSB[0][:, kc * 128:(kc + 1) * 128],
                        bufs["XN"][:, kc * T + tt * TTW: kc * T + tt * TTW + TTW],
                        start=(kc == 0), stop=(kc == NCH - 1)),
                        after=[wq, evq[pb]] + prev_pair_toks, tok=(kc == NCH - 1))
                evq[pb] = cfg["q_evac"](pair, tt, psq[pb][:, :], mmq, QB[:, tt * TTW:(tt + 1) * TTW])
                qtoks.append(evq[pb])
                mmk = None
                for kc in range(NCH):
                    mmk = ctx.run("te", lambda e, kc=kc, tt=tt, pb=pb: e.matmul(
                        psq[1 - pb][:, :], WSB[1][:, kc * 128:(kc + 1) * 128],
                        bufs["XN"][:, kc * T + tt * TTW: kc * T + tt * TTW + TTW],
                        start=(kc == 0), stop=(kc == NCH - 1)),
                        after=[wk, evq[1 - pb]] + prev_pair_toks, tok=(kc == NCH - 1))
                evq[1 - pb] = cfg["k_evac"](pair, tt, psq[1 - pb][:, :], mmk, KB[:, tt * TTW:(tt + 1) * TTW])
                qtoks.append(evq[1 - pb])
            vtoks = []
            vbias = cfg.get("v_bias")
            for tb in range(NTB):
                pb = tb % 2
                mmv = None
                for kc in range(NCH):
                    mmv = ctx.run("te", lambda e, kc=kc, tb=tb, pb=pb: e.matmul(
                        psq[pb][:, 0:128],
                        bufs["XN"][:, kc * T + tb * 128: kc * T + tb * 128 + 128],
                        WSB[2][:, kc * 128:(kc + 1) * 128],
                        start=(kc == 0), stop=(kc == NCH - 1) and vbias is None),
                        after=[wv, evq[pb]], tok=(vbias is None and kc == NCH - 1))
                if vbias is not None:
                    mmv = ctx.run("te", lambda e, pb=pb: e.matmul(
                        psq[pb][:, 0:128], vbias[0], vbias[1](pair),
                        start=False, stop=True), after=[], tok=True)
                evq[pb] = cfg["v_evac"](pair, tb, psq[pb][:, 0:128], mmv, VB[:, tb * 128:(tb + 1) * 128])
                vtoks.append(evq[pb])
            if cfg.get("qkln"):
                qtoks = cfg["qkln"](pair, QB, KB, qtoks, psq, psS)
            pend = None
    
            def stage_a(qb):
                k0, klen, masks = cfg["kset"](qb)
                par = qb % 2
                nsc = (klen + SC - 1) // SC
                sctoks = []
                for h in range(2):
                    hb = 64 * h
                    for sc in range(nsc):
                        scw = min(SC, klen - sc * SC)
                        sb = sc % 2
                        mm = None
                        nkt = (scw + 511) // 512
                        for kt in range(nkt):
                            nw = min(512, scw - kt * 512)
                            mm = ctx.run("te", lambda e, hb=hb, sb=sb, kt=kt, nw=nw, sc=sc, qb=qb, k0=k0: e.matmul(
                                psS[sb][:, kt * 512:kt * 512 + nw],
                                QB[hb:hb + 64, qb * 128:qb * 128 + 128],
                                KB[hb:hb + 64, k0 + sc * SC + kt * 512: k0 + sc * SC + kt * 512 + nw],
                                start=True, stop=True),
                                after=qtoks + [expo.get(sb)], tok=(kt == nkt - 1))
                        mtok = mm
                        for (moff, w, m_ap) in masks:
                            if sc * SC <= moff < sc * SC + scw:
                                mtok = ctx.run("dv", lambda e, moff=moff, w=w, m_ap=m_ap, sb=sb, sc=sc: e.tensor_add(
                                    psS[sb][:, moff - sc * SC: moff - sc * SC + w],
                                    psS[sb][:, moff - sc * SC: moff - sc * SC + w], m_ap),
                                    after=[mm], tok=True)
                        col = h * 4 + sc
                        nmt = ctx.run("dv", lambda e, sb=sb, scw=scw, col=col, par=par: e.tensor_reduce(
                            NM[par][:, col:col + 1], psS[sb][:, 0:scw], AX.X, ALU.max, negate=True),
                            after=[mtok, ftok.get(par)], tok=True)
                        ext = ctx.run("ac", lambda e, h=h, sb=sb, sc=sc, scw=scw, col=col, par=par, qb=qb: e.activation(
                            PB[:, par * 2 * MAXK + h * MAXK + sc * SC: par * 2 * MAXK + h * MAXK + sc * SC + scw],
                            psS[sb][:, 0:scw],
                            AF.Exp, bias=NM[par][:, col:col + 1], accum_out=SM[par][:, col:col + 1]),
                            after=[nmt, pvdone.get(par)], tok=True)
                        expo[sb] = ext
                        sctoks.append(ext)
                return (qb, k0, klen, nsc, sctoks)
    
            def stage_b(st):
                qb, k0, klen, nsc, sctoks = st
                par = qb % 2
                if nsc == 1:
                    f1 = ctx.run("dv", lambda e, par=par: e.reciprocal(
                        FC[par][:, 0:1], SM[par][:, 0:1]), after=sctoks, tok=False)
                    f1 = ctx.run("dv", lambda e, par=par: e.reciprocal(
                        FC[par][:, 4:5], SM[par][:, 4:5]), after=[], tok=True)
                    ftoks = [f1]
                else:
                    t = ctx.run("dv", lambda e, par=par, nsc=nsc: e.tensor_reduce(
                        FC[par][:, 14:16].rearrange("p (a x) -> p a x", a=2) if False else FC[par][:, 14:16],
                        NM[par][:, 0:8].rearrange("p (h s) -> p h s", h=2)[:, :, 0:nsc],
                        AX.X, ALU.min),
                        after=sctoks, tok=True)
                    tno = None
                    for sc in range(nsc):
                        tno = ctx.run("dv", lambda e, par=par, sc=sc: e.tensor_sub(
                            FC[par][:, sc:sc + 1].rearrange("p x -> p x") if False else FC[par][:, sc:sc + 1],
                            FC[par][:, 14:15], NM[par][:, sc:sc + 1]),
                            after=[t], tok=False)
                        tno = ctx.run("dv", lambda e, par=par, sc=sc: e.tensor_sub(
                            FC[par][:, 4 + sc:5 + sc], FC[par][:, 15:16], NM[par][:, 4 + sc:5 + sc]),
                            after=[], tok=(sc == nsc - 1))
                    t3 = ctx.run("ac", lambda e, par=par: e.activation(
                        FC[par][:, 0:8], FC[par][:, 0:8], AF.Exp), after=[tno], tok=True)
                    t4 = ctx.run("dv", lambda e, par=par: e.tensor_mul(
                        SM[par][:, 0:8], SM[par][:, 0:8], FC[par][:, 0:8]), after=[t3], tok=True)
                    t5 = ctx.run("dv", lambda e, par=par, nsc=nsc: e.tensor_reduce(
                        SM[par][:, 14:16], SM[par][:, 0:8].rearrange("p (h s) -> p h s", h=2)[:, :, 0:nsc],
                        AX.X, ALU.add), after=[t4], tok=True)
                    t6 = ctx.run("dv", lambda e, par=par: e.reciprocal(
                        SM[par][:, 12:14], SM[par][:, 14:16]), after=[t5], tok=True)
                    t7 = ctx.run("dv", lambda e, par=par: e.tensor_scalar(
                        FC[par][:, 0:4], FC[par][:, 0:4], SM[par][:, 12:13], None, ALU.mult),
                        after=[t6], tok=False)
                    t7 = ctx.run("dv", lambda e, par=par: e.tensor_scalar(
                        FC[par][:, 4:8], FC[par][:, 4:8], SM[par][:, 13:14], None, ALU.mult),
                        after=[], tok=True)
                    ftoks = [t7]
                ftok[par] = ftoks[-1]
                lastpv = None
                for h in range(2):
                    hb = 64 * h
                    rs = []
                    for sc in range(nsc):
                        scw = min(SC, klen - sc * SC)
                        rs.append(ctx.run("dv", lambda e, h=h, sc=sc, scw=scw, par=par: e.tensor_scalar(
                            PB[:, par * 2 * MAXK + h * MAXK + sc * SC: par * 2 * MAXK + h * MAXK + sc * SC + scw],
                            PB[:, par * 2 * MAXK + h * MAXK + sc * SC: par * 2 * MAXK + h * MAXK + sc * SC + scw],
                            FC[par][:, h * 4 + sc:h * 4 + sc + 1], None, ALU.mult),
                            after=ftoks, tok=(sc == nsc - 1)))
                    nkc = klen // 128
                    mmo = None
                    for g0 in range(0, nkc, 4):
                        gn = min(4, nkc - g0)
                        ttk = None
                        for kc in range(g0, g0 + gn):
                            ttk = ctx.run("te", lambda e, h=h, kc=kc, par=par: e.transpose(
                                psT[:, (kc % 4) * 128: (kc % 4) * 128 + 128],
                                PB[:, par * 2 * MAXK + h * MAXK + kc * 128: par * 2 * MAXK + h * MAXK + kc * 128 + 128],
                                bufs["identb"][:, :]),
                                after=[rs[-1], ptev.get(0)], tok=(kc == g0 + gn - 1))
                        pte = ctx.run("dv", lambda e, gn=gn: e.tensor_copy(
                            PT[:, 0:gn * 128], psT[:, 0:gn * 128]),
                            after=[ttk, mmo], tok=True)
                        ptev[0] = pte
                        for kc in range(g0, g0 + gn):
                            mmo = ctx.run("te", lambda e, h=h, hb=hb, kc=kc, k0=k0, nkc=nkc: e.matmul(
                                psO[hb:hb + 64, h * 128:h * 128 + 128],
                                VB[:, (k0 // 128 + kc) * 128 + hb: (k0 // 128 + kc) * 128 + hb + 64],
                                PT[:, (kc % 4) * 128: (kc % 4) * 128 + 128],
                                start=(kc == 0), stop=(kc == nkc - 1)),
                                after=[pte, evo[0], evo[1]], tok=(kc == nkc - 1))
                    evo[h] = ctx.run("dv", lambda e, h=h, hb=hb, qb=qb: e.tensor_copy(
                        cfg["AO"][hb:hb + 64, ao_base + qb * 128: ao_base + qb * 128 + 128],
                        psO[hb:hb + 64, h * 128:h * 128 + 128]),
                        after=[mmo], tok=True)
                    lastpv = mmo
                pvdone[par] = evo[1]
    
            expo = {}
            evo = {0: None, 1: None}
            ftok = {}
            pvdone = {}
            ptev = {}
            if AOD is not None:
                cfg["AO"] = AOP
                ao_base = 0
            else:
                ao_base = pair * T
            pend = None
            for qb in range(NTB):
                st = stage_a(qb)
                if pend is not None:
                    stage_b(pend)
                pend = st
            stage_b(pend)
            pair_final = [evo[0], evo[1]]
            if AOD is not None:
                aot = rao.dma("sy", AOD.rearrange("(ch p) t -> p ch t", p=128)[:, pair, :],
                              AOP[:, :], after=pair_final)
                pair_final = pair_final + [aot]
            prev_pair_toks = pair_final + [qtoks[-1], vtoks[-1]]
            toks.extend(pair_final)
        rw.close()
    return toks[-4:]


def emit_outproj_combine(ctx, bufs, w_ap, ao_sb, xres_in, xres_out, g_dram, deps,
                         bias_col=None, add_xn=False, xn_dram=None, ao_dram=None):
    nc = ctx.nc
    xin_r = xres_in.rearrange("(ch p) t -> p ch t", p=128)
    xout_r = xres_out.rearrange("(ch p) t -> p ch t", p=128)
    g_r = g_dram.rearrange("(ch p) t -> p ch t", p=128)
    with contextlib.ExitStack() as es:
        XTI = es.enter_context(nc.sbuf_tensor(uname("oc_xi"), [128, 4 * TTW], F32))
        GTI = es.enter_context(nc.sbuf_tensor(uname("oc_gi"), [128, 4 * TTW], BF16))
        TMP = es.enter_context(nc.sbuf_tensor(uname("oc_tmp"), [128, 2 * TTW], F32))
        XO = es.enter_context(nc.sbuf_tensor(uname("oc_xo"), [128, 2 * TTW], F32))
        ld = {}
        stw = [None, None]
        rx = Ring(ctx, 2)
        rg = Ring(ctx, 2)
        rs = Ring(ctx, 2)
        rxn = Ring(ctx, 2) if xn_dram is not None else None
        XNI = es.enter_context(nc.sbuf_tensor(uname("oc_xn"), [128, 4 * TTW], BF16)) if xn_dram is not None else None
        xn_r = xn_dram.rearrange("(ch p) t -> p ch t", p=128) if xn_dram is not None else None

        def prefetch(idx, after):
            di, tt = idx // NTT, idx % NTT
            b = idx % 4
            t1 = rx.dma("sy", XTI[:, b * TTW:(b + 1) * TTW], xin_r[:, di, tt * TTW:(tt + 1) * TTW],
                        after=after)
            t2 = rg.dma("sy", GTI[:, b * TTW:(b + 1) * TTW], g_r[:, di, tt * TTW:(tt + 1) * TTW],
                        after=after)
            t3 = None
            if xn_dram is not None:
                t3 = rxn.dma("sy", XNI[:, b * TTW:(b + 1) * TTW], xn_r[:, di, tt * TTW:(tt + 1) * TTW],
                             after=after)
            ld[idx] = (t1, t2, b, t3)

        prefetch(0, deps)
        prefetch(1, deps)

        def evac(di, tt, ps, mm):
            idx = di * NTT + tt
            if idx + 2 < NCH * NTT:
                prefetch(idx + 2, [stw[idx % 2]])
            t1, t2, b, t3 = ld[idx]
            pb = idx % 2
            seq = [mm, t1, t2, t3]
            if xn_dram is not None:
                tk = ctx.run("dv", lambda e, pb=pb, b=b: e.tensor_add(
                    TMP[:, pb * TTW:(pb + 1) * TTW], ps, XNI[:, b * TTW:(b + 1) * TTW]),
                    after=seq, tok=True)
                tk = ctx.run("dv", lambda e, pb=pb, b=b: e.tensor_mul(
                    TMP[:, pb * TTW:(pb + 1) * TTW], TMP[:, pb * TTW:(pb + 1) * TTW],
                    GTI[:, b * TTW:(b + 1) * TTW]),
                    after=[tk], tok=True)
            elif add_xn:
                tk = ctx.run("dv", lambda e, di=di, tt=tt, pb=pb, b=b: e.tensor_add(
                    TMP[:, pb * TTW:(pb + 1) * TTW], ps,
                    bufs["XN"][:, di * T + tt * TTW: di * T + tt * TTW + TTW]),
                    after=seq, tok=True)
                tk = ctx.run("dv", lambda e, pb=pb, b=b: e.tensor_mul(
                    TMP[:, pb * TTW:(pb + 1) * TTW], TMP[:, pb * TTW:(pb + 1) * TTW],
                    GTI[:, b * TTW:(b + 1) * TTW]),
                    after=[tk], tok=True)
            elif bias_col is not None:
                tk = ctx.run("dv", lambda e, di=di, pb=pb, b=b: e.scalar_tensor_tensor(
                    TMP[:, pb * TTW:(pb + 1) * TTW], ps, bias_col[:, di:di + 1],
                    GTI[:, b * TTW:(b + 1) * TTW], ALU.add, ALU.mult),
                    after=seq, tok=True)
            else:
                tk = ctx.run("dv", lambda e, pb=pb, b=b: e.tensor_mul(
                    TMP[:, pb * TTW:(pb + 1) * TTW], ps, GTI[:, b * TTW:(b + 1) * TTW]),
                    after=seq, tok=True)
            tk2 = ctx.run("dv", lambda e, pb=pb, b=b: e.tensor_add(
                XO[:, pb * TTW:(pb + 1) * TTW], TMP[:, pb * TTW:(pb + 1) * TTW],
                XTI[:, b * TTW:(b + 1) * TTW]),
                after=[tk, stw[pb]], tok=True)
            stw[pb] = rs.dma("sy", xout_r[:, di, tt * TTW:(tt + 1) * TTW], XO[:, pb * TTW:(pb + 1) * TTW],
                             after=[tk2])
            return stw[pb]

        if ao_dram is None:
            toks = emit_proj_fm(ctx, bufs, w_ap, ao_sb, 0, [(c * 128, 128) for c in range(NCH)],
                                evac, deps)
        else:
            ao_r2 = ao_dram.rearrange("(ch p) t -> p ch t", p=128)
            MV = es.enter_context(nc.sbuf_tensor(uname("oc_mv"), [128, 2 * TTW], BF16))
            WSB2 = [es.enter_context(nc.sbuf_tensor(uname("oc_w"), [128, NCH * 128], BF16)) for _ in range(2)]
            pss2 = [es.enter_context(nc.psum_tensor(uname("oc_ps"), [128, TTW], F32)) for _ in range(2)]
            rmw = Ring(ctx, 2, sw=True)
            rmv = Ring(ctx, 2)
            wld = [None, None]
            lastmm = [None, None]
            evt = [None, None]
            mvl = {}
            for di in range(NCH):
                wb = di % 2
                wld[wb] = wslab_load(ctx, rmw, WSB2[wb], w_ap, di * 128, 128,
                                     after=[lastmm[wb]] + (list(deps) if di < 2 else []))
                for tt in range(NTT):
                    pb = tt % 2
                    mm = None
                    for kc in range(NCH):
                        mb = (tt * NCH + kc) % 2
                        mvt = rmv.dma("sy", MV[:, mb * TTW:(mb + 1) * TTW],
                                      ao_r2[:, kc, tt * TTW:(tt + 1) * TTW],
                                      after=[mvl.get(mb)] + (list(deps) if di == 0 and tt == 0 and kc < 2 else []))
                        mm = ctx.run("te", lambda e, kc=kc, wb=wb, pb=pb, mb=mb: e.matmul(
                            pss2[pb][:, :], WSB2[wb][:, kc * 128:(kc + 1) * 128],
                            MV[:, mb * TTW:(mb + 1) * TTW],
                            start=(kc == 0), stop=(kc == NCH - 1)),
                            after=[wld[wb], evt[pb], mvt], tok=True)
                        mvl[mb] = mm
                    lastmm[wb] = mm
                    evt[pb] = evac(di, tt, pss2[pb][:, :], mm)
            rmw.close(); rmv.close()
        rx.close(); rg.close(); rs.close()
        if rxn is not None:
            rxn.close()
    return [stw[0], stw[1]]


def emit_layer0(ctx, bufs, W, xres_in, xres_out, scratch, deps):
    nc = ctx.nc
    les = ctx.es.enter_context(contextlib.ExitStack())
    bufs["XN"] = les.enter_context(nc.sbuf_tensor(uname("XN"), [128, NCH * T], BF16))
    bufs["AO"] = les.enter_context(nc.sbuf_tensor(uname("AO"), [128, NCH * T], BF16))
    g_dram = scratch["g"]
    t_ln = emit_ln(ctx, bufs, xres_in, deps)
    t_g = emit_l0ffn(ctx, bufs, W, t_ln, g_dram)

    QLN = bufs["qln"]
    AO = bufs["AO"]

    def q_evac(pair, tt, ps, mm, dst):
        return ctx.run("dv", lambda e, ps=ps, dst=dst: e.tensor_copy(dst, ps), after=[mm], tok=True)
    k_evac = q_evac

    def v_evac(pair, tb, ps, mm, dst):
        return ctx.run("dv", lambda e, ps=ps, dst=dst: e.tensor_copy(dst, ps), after=[mm], tok=True)

    def qkln(pair, QB, KB, qtoks, psq, psS):
        ntoks = []
        with contextlib.ExitStack() as es:
            SQ = es.enter_context(nc.sbuf_tensor(uname("qk_sq"), [128, TTW], BF16))
            MU = es.enter_context(nc.sbuf_tensor(uname("qk_mu"), [128, TTW], F32))
            VARB = es.enter_context(nc.sbuf_tensor(uname("qk_var"), [128, TTW], F32))
            RST = es.enter_context(nc.sbuf_tensor(uname("qk_rst"), [128, TTW], F32))
            prev = {}
            for which, BUF, wcol, bcol, scl in (("q", QB, QLN[:, 0:1], QLN[:, 1:2], 0.125),
                                               ("k", KB, QLN[:, 2:3], QLN[:, 3:4], 1.0)):
                for tt in range(NTT):
                    t0 = tt * TTW
                    sqt = ctx.run("ac", lambda e, BUF=BUF, t0=t0: e.activation(
                        SQ[:, :], BUF[:, t0:t0 + TTW], AF.Square),
                        after=qtoks + [prev.get("mm")], tok=True)
                    mm1 = None
                    mm2 = None
                    for h in range(2):
                        hb = 64 * h
                        mm1 = ctx.run("te", lambda e, hb=hb, BUF=BUF, t0=t0: e.matmul(
                            psS[0][hb:hb + 64, 0:TTW], bufs["inv64"][hb:hb + 64, 0:64],
                            BUF[hb:hb + 64, t0:t0 + TTW], start=True, stop=True),
                            after=qtoks + [prev.get("e1")], tok=(h == 1))
                        mm2 = ctx.run("te", lambda e, hb=hb, t0=t0: e.matmul(
                            psS[1][hb:hb + 64, 0:TTW], bufs["inv64"][hb:hb + 64, 0:64],
                            SQ[hb:hb + 64, 0:TTW], start=True, stop=True),
                            after=[sqt, prev.get("e3")], tok=(h == 1))
                    prev["mm"] = mm2
                    e1 = ctx.run("dv", lambda e: e.tensor_copy(MU[:, :], psS[0][:, 0:TTW]),
                                 after=[mm1, prev.get("s3"), prev.get("e2")], tok=True)
                    prev["e1"] = e1
                    e2 = ctx.run("ac", lambda e: e.activation(VARB[:, :], MU[:, :], AF.Square),
                                 after=[e1, prev.get("e5")], tok=True)
                    prev["e2"] = e2
                    e3 = ctx.run("dv", lambda e: e.tensor_sub(VARB[:, :], psS[1][:, 0:TTW], VARB[:, :]),
                                 after=[mm2, e2], tok=True)
                    prev["e3"] = e3
                    e4 = ctx.run("ac", lambda e: e.activation(VARB[:, :], VARB[:, :], AF.Ln, bias=1e-5),
                                 after=[e3], tok=True)
                    e5 = ctx.run("ac", lambda e: e.activation(RST[:, :], VARB[:, :], AF.Exp, scale=-0.5),
                                 after=[e4, prev.get("s3")], tok=True)
                    prev["e5"] = e5
                    s1 = ctx.run("dv", lambda e, BUF=BUF, t0=t0: e.tensor_sub(
                        MU[:, :], BUF[:, t0:t0 + TTW], MU[:, :]), after=[e5], tok=True)
                    s2 = ctx.run("dv", lambda e, t0=t0: e.tensor_mul(MU[:, :], MU[:, :], RST[:, :]),
                                 after=[s1], tok=True)
                    if scl == 1.0:
                        s3 = ctx.run("dv", lambda e, BUF=BUF, t0=t0, wcol=wcol, bcol=bcol: e.tensor_scalar(
                            BUF[:, t0:t0 + TTW], MU[:, :], wcol, bcol, ALU.mult, ALU.add),
                            after=[s2], tok=True)
                        prev["s3"] = s3
                    else:
                        s3 = ctx.run("dv", lambda e, BUF=BUF, t0=t0, wcol=wcol, bcol=bcol: e.tensor_scalar(
                            BUF[:, t0:t0 + TTW], MU[:, :], QLN[:, 4:5], QLN[:, 5:6], ALU.mult, ALU.add),
                            after=[s2], tok=True)
                    prev["s3"] = s3
                    ntoks.append(s3)
        return ntoks

    def kset(qb):
        klen = (qb + 1) * 128
        return (0, klen, [(klen - 128, 128, bufs["m_tri"][:, :])])

    cfg = dict(wq_ap=W["L0_attn_WqT"], wk_ap=W["L0_attn_WkT"], wv_ap=W["L0_attn_WvT"],
               q_evac=q_evac, k_evac=k_evac, v_evac=v_evac, qkln=qkln, kset=kset, AO=AO)
    t_at = emit_attn_pairs(ctx, bufs, cfg, t_ln + t_g)
    t_oc = emit_outproj_combine(ctx, bufs, W["L0_attn_WuT"], AO, xres_in, xres_out,
                                g_dram, t_at + t_g)
    les.close()
    return t_oc




def emit_l1_router(ctx, bufs, W, deps):
    nc = ctx.nc
    CWF = bufs["CWF"]
    toks = []
    with contextlib.ExitStack() as es:
        LGT = es.enter_context(nc.sbuf_tensor(uname("r1_lg"), [128, T], F32))
        LTM = es.enter_context(nc.sbuf_tensor(uname("r1_ltm"), [128, 8], F32))
        SML = es.enter_context(nc.sbuf_tensor(uname("r1_sml"), [128, 16], F32))
        pst = es.enter_context(nc.psum_tensor(uname("r1_ps"), [128, 512], F32))
        def ev(di, tt, ps, mm):
            return ctx.run("dv", lambda e, tt=tt, ps=ps: e.tensor_scalar(
                LGT[0:8, tt * TTW:(tt + 1) * TTW], ps, bufs["l1rb"][0:8, 0:1], None, ALU.add),
                after=[mm], tok=True)
        t1 = emit_proj_fm(ctx, bufs, W["l1_routerT"], bufs["XN"], 0, [(0, 8)], ev, deps)
        prev = {}
        for tb in range(NTB):
            tk1 = ctx.run("te", lambda e, tb=tb: e.transpose(
                pst[:, 0:8], LGT[0:8, tb * 128:(tb + 1) * 128], bufs["identf"][0:8, 0:8]),
                after=t1 + [prev.get("ev1"), prev.get("ev2")], tok=True)
            ev1 = ctx.run("dv", lambda e: e.tensor_copy(LTM[:, 0:8], pst[:, 0:8]),
                          after=[tk1, prev.get("cw")], tok=True)
            m1 = ctx.run("dv", lambda e: e.tensor_reduce(SML[:, 0:1], LTM[:, 0:8], AX.X, ALU.max),
                         after=[ev1], tok=True)
            eq1 = ctx.run("dv", lambda e: e.tensor_scalar(SML[:, 2:10], LTM[:, 0:8], SML[:, 0:1], None, ALU.is_equal),
                          after=[m1], tok=True)
            l2 = ctx.run("dv", lambda e: e.scalar_tensor_tensor(LTM[:, 0:8], SML[:, 2:10], NEG, LTM[:, 0:8],
                                                                ALU.mult, ALU.add),
                         after=[eq1], tok=True)
            m2 = ctx.run("dv", lambda e: e.tensor_reduce(SML[:, 1:2], LTM[:, 0:8], AX.X, ALU.max),
                         after=[l2], tok=True)
            eq2 = ctx.run("dv", lambda e: e.tensor_scalar(LTM[:, 0:8], LTM[:, 0:8], SML[:, 1:2], None, ALU.is_equal),
                          after=[m2], tok=True)
            dd = ctx.run("dv", lambda e: e.tensor_sub(SML[:, 1:2], SML[:, 0:1], SML[:, 1:2]), after=[m2, eq2], tok=True)
            w1 = ctx.run("ac", lambda e: e.activation(SML[:, 0:1], SML[:, 1:2], AF.Sigmoid), after=[dd], tok=True)
            w2 = ctx.run("dv", lambda e: e.tensor_scalar(SML[:, 1:2], SML[:, 0:1], -1.0, 1.0, ALU.mult, ALU.add),
                         after=[w1], tok=True)
            c1 = ctx.run("dv", lambda e: e.tensor_scalar(SML[:, 2:10], SML[:, 2:10], SML[:, 0:1], None, ALU.mult),
                         after=[w2], tok=True)
            cw = ctx.run("dv", lambda e: e.scalar_tensor_tensor(LTM[:, 0:8], LTM[:, 0:8], SML[:, 1:2], SML[:, 2:10],
                                                                ALU.mult, ALU.add),
                         after=[c1], tok=True)
            prev["cw"] = cw
            tk2 = ctx.run("te", lambda e, tb=tb: e.transpose(
                pst[0:8, 128:256], LTM[:, 0:8], bufs["identf"][:, :]),
                after=[cw, prev.get("ev2")], tok=True)
            ev2 = ctx.run("dv", lambda e, tb=tb: e.tensor_copy(
                CWF[0:8, tb * 128:(tb + 1) * 128], pst[0:8, 128:256]),
                after=[tk2], tok=True)
            prev["ev1"] = ev1
            prev["ev2"] = ev2
            toks = [ev2]
    return toks


def emit_l1_moe(ctx, bufs, W, deps, xn_dram, gacc_dram):
    nc = ctx.nc
    CWF = bufs["CWF"]
    NHC = 32
    gacc_r = gacc_dram.rearrange("(ch p) t -> p ch t", p=128)
    xn_r = xn_dram.rearrange("(ch p) t -> p ch t", p=128)
    toks = []
    with contextlib.ExitStack() as es:
        W1S = es.enter_context(nc.sbuf_tensor(uname("m_w1"), [128, NCH * 4096], BF16))
        W2S = es.enter_context(nc.sbuf_tensor(uname("m_w2"), [128, NHC * 1024], BF16))
        Y1 = es.enter_context(nc.sbuf_tensor(uname("m_y1"), [128, NHC * 512], BF16))
        XT = es.enter_context(nc.sbuf_tensor(uname("m_xt"), [128, 2 * NCH * 512], BF16))
        CWB = es.enter_context(nc.sbuf_tensor(uname("m_cwb"), [128, 2 * 512], F32))
        GOB = es.enter_context(nc.sbuf_tensor(uname("m_gob"), [128, 2 * 512], BF16))
        pso = [es.enter_context(nc.psum_tensor(uname("m_ps"), [128, 512], F32)) for _ in range(8)]
        rw1 = Ring(ctx, 1, sw=True)
        rw2 = Ring(ctx, 1, sw=True)
        rxt = Ring(ctx, 2)
        rgo = Ring(ctx, 2, sw=True)
        w1tok = None
        w2tok = None
        w1mm = None
        w2mm = None
        xt_t = [None, None]
        xt_guard = {}
        cwb_guard = {}
        prev_y2 = [None]
        y1_t = {}
        cwb_t = [None, None]
        go_t = [None, None]
        evo = {}
        for e_i in range(NEXP):
            w1tok = rw1.dma("gp",
                            W1S[:, :].rearrange("p (kc m) -> p kc m", m=4096),
                            W["l1_w1T"][e_i].rearrange("(kc p) m -> p kc m", p=128),
                            after=[w1mm] + (list(deps) if e_i == 0 else []))
            w2tok = rw2.dma("gp",
                            W2S[:, :].rearrange("p (kc m) -> p kc m", m=1024),
                            W["l1_w2T"][e_i].rearrange("(kc p) m -> p kc m", p=128),
                            after=[w2mm] + (list(deps) if e_i == 0 else []))
            for ts_ in range(NTT):
                t0 = ts_ * TTW
                xb = ts_ % 2
                xt_t[xb] = rxt.dma("sy", XT[:, xb * 4096:(xb + 1) * 4096].rearrange("p (ch m) -> p ch m", m=512),
                                   xn_r[:, :, t0:t0 + TTW],
                                   after=[xt_guard.get(xb)] + (list(deps) if e_i == 0 and ts_ < 2 else []))
                cb = ts_ % 2
                cmm = ctx.run("te", lambda e, e_i=e_i, t0=t0: e.matmul(
                    pso[2][:, :], bufs["l1sel"][0:8, e_i * 128:(e_i + 1) * 128], CWF[0:8, t0:t0 + TTW],
                    start=True, stop=True),
                    after=[cwb_t[cb], evo.get(2)] + list(deps if e_i == 0 and ts_ == 0 else []), tok=True)
                cwb_t[cb] = ctx.run("dv", lambda e, cb=cb: e.tensor_copy(CWB[:, cb * 512:(cb + 1) * 512], pso[2][:, :]),
                                    after=[cmm, cwb_guard.get(cb)], tok=True)
                lastmm = None
                for hc in range(NHC):
                    pb = hc % 2
                    mm = None
                    for kc in range(NCH):
                        mm = ctx.run("te", lambda e, hc=hc, kc=kc, pb=pb, xb=xb: e.matmul(
                            pso[pb][:, :], W1S[:, kc * 4096 + hc * 128: kc * 4096 + hc * 128 + 128],
                            XT[0:128, xb * 4096 + kc * 512: xb * 4096 + kc * 512 + 512],
                            start=(kc == 0), stop=(kc == NCH - 1)),
                            after=[w1tok, xt_t[xb], y1_t.get(pb), evo.get(pb)], tok=(kc == NCH - 1))
                    y1_t[pb] = ctx.run("ac", lambda e, hc=hc, pb=pb, e_i=e_i: e.activation(
                        Y1[:, hc * 512:(hc + 1) * 512], pso[pb][:, :], AF.Gelu,
                        bias=bufs["l1b1"][:, e_i * 32 + hc:e_i * 32 + hc + 1]),
                        after=[mm, prev_y2[0]], tok=True)
                    lastmm = mm
                xt_guard[xb] = lastmm
                w1mm = lastmm
                l2mm = None
                for oc in range(NCH):
                    for hc in range(NHC):
                        l2mm = ctx.run("te", lambda e, oc=oc, hc=hc: e.matmul(
                            pso[oc][:, :], W2S[:, hc * 1024 + oc * 128: hc * 1024 + oc * 128 + 128],
                            Y1[:, hc * 512:(hc + 1) * 512],
                            start=(hc == 0), stop=(hc == NHC - 1)),
                            after=[w2tok, y1_t[0], y1_t[1], evo.get(oc)] +
                                  ([cwb_t[0], cwb_t[1]] if oc == 2 and hc == 0 else []),
                            tok=(hc == NHC - 1))
                    gb = oc % 2
                    gev = ctx.run("dv", lambda e, oc=oc, gb=gb, cb=cb, e_i=e_i: e.scalar_tensor_tensor(
                        GOB[:, gb * 512:(gb + 1) * 512], pso[oc][:, :], bufs["l1b2"][:, e_i * 8 + oc:e_i * 8 + oc + 1],
                        CWB[:, cb * 512:(cb + 1) * 512], ALU.add, ALU.mult),
                        after=[l2mm, go_t[gb], cwb_t[cb]], tok=True)
                    go_t[gb] = ctx.run("gp", lambda e, oc=oc, gb=gb, t0=t0, e_i=e_i: e.dma_start(
                        gacc_r[:, oc, t0:t0 + TTW], GOB[:, gb * 512:(gb + 1) * 512],
                        accum_op=(ALU.bypass if e_i == 0 else ALU.add)),
                        after=[gev], tok=True, dma=True, dsem=rgo.sems[gb]) or None
                    go_t[gb] = (rgo.sems[gb], ctx.cnt[rgo.sems[gb]])
                    evo[oc] = go_t[gb]
                w2mm = l2mm
                prev_y2[0] = l2mm
                cwb_guard[cb] = gev
        out = [go_t[0], go_t[1]]
        rw1.close(); rw2.close(); rxt.close(); rgo.close()
    return out


def emit_layer1(ctx, bufs, W, xres_in, xres_out, scratch, deps):
    nc = ctx.nc
    g_dram = scratch["g"]
    xn_dram = scratch["xn"]
    gacc_dram = scratch["gacc"]
    lces = contextlib.ExitStack()
    bufs["CWF"] = lces.enter_context(nc.sbuf_tensor(uname("CWF"), [8, T], F32))
    esA = contextlib.ExitStack()
    bufs["XN"] = esA.enter_context(nc.sbuf_tensor(uname("XN"), [128, NCH * T], BF16))
    t_ln = emit_ln(ctx, bufs, xres_in, deps, xn_dram=xn_dram)
    t_rt = emit_l1_router(ctx, bufs, W, t_ln)
    esA.close()
    t_moe = emit_l1_moe(ctx, bufs, W, t_ln + t_rt, xn_dram, gacc_dram)
    esC = contextlib.ExitStack()
    GS = esC.enter_context(nc.sbuf_tensor(uname("c1_gs"), [128, 4096], BF16))
    GAI = esC.enter_context(nc.sbuf_tensor(uname("c1_ga"), [128, 4096], BF16))
    XNI = esC.enter_context(nc.sbuf_tensor(uname("c1_xn"), [128, 4096], BF16))
    GT = esC.enter_context(nc.sbuf_tensor(uname("c1_gt"), [128, 2 * 512], BF16))
    rga = Ring(ctx, 2)
    rxn = Ring(ctx, 2)
    rgw = Ring(ctx, 2)
    gacc_r = gacc_dram.rearrange("(ch p) t -> p ch t", p=128)
    xn_r = xn_dram.rearrange("(ch p) t -> p ch t", p=128)
    g_r = g_dram.rearrange("(ch p) t -> p ch t", p=128)
    stt_ = [None, None]
    addtok = [None]

    def produce(tt, after):
        t0 = tt * TTW
        l1_ = rga.dma("sy", GAI[:, :].rearrange("p (ch m) -> p ch m", m=512),
                      gacc_r[:, :, t0:t0 + TTW], after=after)
        l2_ = rxn.dma("sy", XNI[:, :].rearrange("p (ch m) -> p ch m", m=512),
                      xn_r[:, :, t0:t0 + TTW], after=after)
        at = ctx.run("dv", lambda e: e.tensor_add(GS[:, :], GAI[:, :], XNI[:, :]),
                     after=[l1_, l2_] + after, tok=True)
        addtok[0] = at
        return at, (lambda ch: GS[:, ch * 512:ch * 512 + 512])

    def consume(tt, ch, scr, rst, after):
        t0 = tt * TTW
        k1 = ctx.run("dv", lambda e, scr=scr, rst=rst: e.tensor_mul(scr, scr, rst), after=after, tok=True)
        k2 = ctx.run("dv", lambda e, ch=ch, scr=scr: e.tensor_scalar(
            scr, scr, bufs["l1ab"][:, ch:ch + 1], bufs["l1ab"][:, 8 + ch:8 + ch + 1], ALU.mult, ALU.add),
            after=[k1], tok=True)
        sb = ch % 2
        k3 = ctx.run("ac", lambda e, scr=scr, sb=sb: e.activation(
            GT[:, sb * 512:(sb + 1) * 512], scr, AF.Silu), after=[k2, stt_[sb]], tok=True)
        stt_[sb] = rgw.dma("sy", g_r[:, ch, t0:t0 + TTW], GT[:, sb * 512:(sb + 1) * 512], after=[k3])
        return stt_[sb]

    t_g = emit_lnx(ctx, bufs, t_moe, produce, consume, eps=1e-12) + [stt_[0], stt_[1]]
    rga.close(); rxn.close(); rgw.close()
    esC.close()
    esD = ctx.es.enter_context(contextlib.ExitStack())
    bufs["XN"] = esD.enter_context(nc.sbuf_tensor(uname("XN"), [128, NCH * T], BF16))
    bufs["AO"] = esD.enter_context(nc.sbuf_tensor(uname("AO"), [128, NCH * T], BF16))
    rxl = Ring(ctx, 2)

    def produceD(tt, after):
        t0 = tt * TTW
        lt = rxl.dma("sy", bufs["XN"][:, :].rearrange("p (ch t) -> p ch t", t=T)[:, :, t0:t0 + TTW],
                     xn_r[:, :, t0:t0 + TTW], after=after)
        return lt, (lambda ch, t0=t0: bufs["XN"][:, ch * T + t0: ch * T + t0 + TTW])

    def consumeD(tt, ch, scr, rst, after):
        t0 = tt * TTW
        k1 = ctx.run("dv", lambda e, scr=scr, rst=rst: e.tensor_mul(scr, scr, rst), after=after, tok=True)
        k2 = ctx.run("dv", lambda e, ch=ch, t0=t0, scr=scr: e.tensor_scalar(
            bufs["XN"][:, ch * T + t0: ch * T + t0 + TTW], scr,
            bufs["l1gb"][:, ch:ch + 1], bufs["l1gb"][:, 8 + ch:8 + ch + 1], ALU.mult, ALU.add),
            after=[k1], tok=True)
        return k2
    t_ln2 = emit_lnx(ctx, bufs, t_g, produceD, consumeD, eps=1e-5)
    rxl.close()

    def q_evac(pair, tt, ps, mm, dst):
        return ctx.run("dv", lambda e, ps=ps, dst=dst: e.tensor_scalar(dst, ps, 0.125, None, ALU.mult),
                       after=[mm], tok=True)

    def k_evac(pair, tt, ps, mm, dst):
        return ctx.run("dv", lambda e, ps=ps, dst=dst: e.tensor_copy(dst, ps), after=[mm], tok=True)

    def v_evac(pair, tb, ps, mm, dst):
        return ctx.run("dv", lambda e, ps=ps, dst=dst: e.tensor_copy(dst, ps), after=[mm], tok=True)

    def kset(qb):
        if qb == 0:
            return (0, 128, [(0, 128, bufs["m_tri"][:, :])])
        if qb == 1:
            return (0, 256, [(128, 128, bufs["m_tri"][:, :])])
        return ((qb - 2) * 128, 384, [(0, 128, bufs["m_win"][:, :]),
                                      (256, 128, bufs["m_tri"][:, :])])

    cfg = dict(wq_ap=W["l1_qkvT"], wk_ap=W["l1_qkvT"], wv_ap=W["l1_qkvT"],
               wq_off=0, wk_off=1024, wv_off=2048,
               q_evac=q_evac, k_evac=k_evac, v_evac=v_evac, qkln=None, kset=kset,
               AO=bufs["AO"], maxk=512)
    t_at = emit_attn_pairs(ctx, bufs, cfg, t_ln2)
    t_oc = emit_outproj_combine(ctx, bufs, W["l1_woT"], bufs["AO"], xres_in, xres_out,
                                g_dram, t_at + t_g, xn_dram=xn_dram)
    esD.close()
    lces.close()
    return t_oc


def decl_l1(nc):
    W = {}
    W["l1_routerT"] = nc.dram_tensor("l1_routerT", [D, 8], F32, kind="ExternalInput").ap()
    W["l1_qkvT"] = nc.dram_tensor("l1_qkvT", [D, 3 * D], F32, kind="ExternalInput").ap()
    W["l1_woT"] = nc.dram_tensor("l1_woT", [D, D], F32, kind="ExternalInput").ap()
    W["l1_w1T"] = nc.dram_tensor("l1_w1T", [8, D, 4096], F32, kind="ExternalInput").ap()
    W["l1_w2T"] = nc.dram_tensor("l1_w2T", [8, 4096, D], F32, kind="ExternalInput").ap()
    return W


def setup_l1_bufs(ctx, bufs, es, deps=()):
    nc = ctx.nc
    names = [("l1rb", [128, 1]), ("l1sel", [8, 8 * 128]), ("l1b1", [128, 256]),
             ("l1b2", [128, 64]), ("l1ab", [128, 16]), ("l1gb", [128, 16])]
    toks = []
    mr = misc_ring(ctx, "gp")
    for nm, shp in names:
        bufs[nm] = es.enter_context(nc.sbuf_tensor(uname(nm), shp, F32))
        ap = nc.dram_tensor("c_" + nm, shp, F32, kind="ExternalInput").ap()
        toks.append(mr.dma("gp", bufs[nm][:, :] if shp[0] == 128 else bufs[nm][0:shp[0], :], ap,
                           after=deps))
    bufs["m_win"] = es.enter_context(nc.sbuf_tensor(uname("m_win"), [128, 128], F32))
    ap = nc.dram_tensor("c_mwin", [128, 128], F32, kind="ExternalInput").ap()
    toks.append(mr.dma("gp", bufs["m_win"][:, :], ap, after=deps))
    return toks


def stage_l1(inputs):
    f32 = np.float32
    tr = lambda a: np.ascontiguousarray(np.asarray(a, f32).T)
    d = {
        "l1_routerT": tr(inputs["L1_ffn_router_w"]),
        "l1_qkvT": tr(inputs["L1_attn_W_qkv"]),
        "l1_woT": tr(inputs["L1_attn_W_out"]),
        "l1_w1T": np.ascontiguousarray(np.asarray(inputs["L1_ffn_w1"], f32).transpose(0, 2, 1)),
        "l1_w2T": np.ascontiguousarray(np.asarray(inputs["L1_ffn_w2"], f32).transpose(0, 2, 1)),
    }
    rb = np.zeros((128, 1), f32); rb[0:8, 0] = np.asarray(inputs["L1_ffn_router_b"], f32)
    d["c_l1rb"] = rb
    sel = np.zeros((8, 8 * 128), f32)
    for e in range(8):
        sel[e, e * 128:(e + 1) * 128] = 1.0
    d["c_l1sel"] = sel
    b1 = np.zeros((128, 256), f32)
    for e in range(8):
        b1[:, e * 32:(e + 1) * 32] = np.asarray(inputs["L1_ffn_b1"][e], f32).reshape(32, 128).T
    d["c_l1b1"] = b1
    b2 = np.zeros((128, 64), f32)
    for e in range(8):
        b2[:, e * 8:(e + 1) * 8] = np.asarray(inputs["L1_ffn_b2"][e], f32).reshape(8, 128).T
    d["c_l1b2"] = b2
    ab = np.zeros((128, 16), f32)
    ab[:, 0:8] = np.asarray(inputs["L1_ffn_ln_w"], f32).reshape(8, 128).T
    ab[:, 8:16] = np.asarray(inputs["L1_ffn_ln_b"], f32).reshape(8, 128).T
    d["c_l1ab"] = ab
    gb = np.zeros((128, 16), f32)
    gb[:, 0:8] = np.asarray(inputs["L1_attn_gamma"], f32).reshape(8, 128).T
    gb[:, 8:16] = np.asarray(inputs["L1_attn_beta"], f32).reshape(8, 128).T
    d["c_l1gb"] = gb
    col = np.arange(128)[None, :]; row = np.arange(128)[:, None]
    d["c_mwin"] = np.where(col >= row, 0.0, NEG).astype(f32)
    return d


def build_prog_l1():
    nc = bass.Bass("TRN2", target_bir_lowering=False)
    es = contextlib.ExitStack()
    W = decl_l1(nc)
    xres_in = nc.dram_tensor("xres_in", [D, T], F32, kind="ExternalInput").ap()
    xres_out = nc.dram_tensor("xres_out", [D, T], F32, kind="ExternalOutput").ap()
    g_dram = nc.dram_tensor("g_scr", [D, T], BF16).ap()
    xn_dram = nc.dram_tensor("xn_scr", [D, T], BF16).ap()
    gacc_dram = nc.dram_tensor("gacc_scr", [D, T], BF16).ap()
    with es:
        ctx, bufs, ctoks = build_common(nc, es)
        ltoks = setup_l1_bufs(ctx, bufs, es)
        ctx.global_deps = ctoks + ltoks
        t1 = emit_layer1(ctx, bufs, W, xres_in, xres_out,
                         {"g": g_dram, "xn": xn_dram, "gacc": gacc_dram}, [])
        ctx.finish_dma_waits("sy")
    return nc


def emit_l3ffn(ctx, bufs, W, deps, g_dram):
    nc = ctx.nc
    g_r = g_dram.rearrange("(ch p) t -> p ch t", p=128)
    NH = 24
    with contextlib.ExitStack() as es:
        HALL = es.enter_context(nc.sbuf_tensor(uname("f3_h"), [128, NH * 512], BF16))
        WG = [es.enter_context(nc.sbuf_tensor(uname("f3_wg"), [128, NCH * 128], BF16)) for _ in range(2)]
        WU = [es.enter_context(nc.sbuf_tensor(uname("f3_wu"), [128, NCH * 128], BF16)) for _ in range(2)]
        WD = [es.enter_context(nc.sbuf_tensor(uname("f3_wd"), [128, 1024], BF16)) for _ in range(2)]
        SG = es.enter_context(nc.sbuf_tensor(uname("f3_sg"), [128, 2 * 512], BF16))
        GT = es.enter_context(nc.sbuf_tensor(uname("f3_gt"), [128, 2 * 512], BF16))
        pso = [es.enter_context(nc.psum_tensor(uname("f3_ps"), [128, 512], F32)) for _ in range(8)]
        rg = Ring(ctx, 2, sw=True)
        ru = Ring(ctx, 2, sw=True)
        rd = Ring(ctx, 2, sw=True)
        rstg = Ring(ctx, 2)
        wg_l = [None, None]; wu_l = [None, None]; wd_l = [None, None]
        mmg = [None, None]; mmu = [None, None]; wdmm = [None, None]
        sg_t = [None, None]
        h_t = {}
        evo = {}
        stok = [None, None]
        p2last = None
        for tt in range(NTT):
            t0 = tt * TTW
            for hj in range(NH):
                b = hj % 2
                wg_l[b] = wslab_load(ctx, rg, WG[b], W["l3_wgT"], hj * 128, 128,
                                     after=[mmg[b]] + (list(deps) if tt == 0 and hj < 2 else []))
                wu_l[b] = wslab_load(ctx, ru, WU[b], W["l3_wuT"], hj * 128, 128,
                                     after=[mmu[b]] + (list(deps) if tt == 0 and hj < 2 else []))
                gmm = None
                for kc in range(NCH):
                    gmm = ctx.run("te", lambda e, kc=kc, b=b, t0=t0: e.matmul(
                        pso[0][:, :], WG[b][:, kc * 128:(kc + 1) * 128],
                        bufs["XN"][:, kc * T + t0: kc * T + t0 + TTW],
                        start=(kc == 0), stop=(kc == NCH - 1)),
                        after=[wg_l[b], sg_t[0], evo.get(0)], tok=(kc == NCH - 1))
                mmg[b] = gmm
                sg_t[0] = ctx.run("ac", lambda e, hj=hj: e.activation(
                    SG[:, (hj % 2) * 512:(hj % 2) * 512 + 512], pso[0][:, :], AF.Silu, scale=1.0 / 127.0),
                    after=[gmm, h_t.get(hj % 2)], tok=True)
                umm = None
                for kc in range(NCH):
                    umm = ctx.run("te", lambda e, kc=kc, b=b, t0=t0: e.matmul(
                        pso[1][:, :], WU[b][:, kc * 128:(kc + 1) * 128],
                        bufs["XN"][:, kc * T + t0: kc * T + t0 + TTW],
                        start=(kc == 0), stop=(kc == NCH - 1)),
                        after=[wu_l[b], h_t.get("u"), evo.get(1)], tok=(kc == NCH - 1))
                mmu[b] = umm
                ht = ctx.run("dv", lambda e, hj=hj: e.scalar_tensor_tensor(
                    HALL[:, hj * 512:(hj + 1) * 512], pso[1][:, :], 1.0 / 127.0,
                    SG[:, (hj % 2) * 512:(hj % 2) * 512 + 512], ALU.mult, ALU.mult),
                    after=[umm, sg_t[0], p2last], tok=True)
                h_t[hj % 2] = ht
                h_t["u"] = ht
            lastmm = None
            for hj in range(NH):
                b = hj % 2
                wd_l[b] = rd.dma("gp", WD[b][:, :], W["l3_wdT"][hj * 128:(hj + 1) * 128, :],
                                 after=[wdmm[b]])
                for oc in range(NCH):
                    lastmm = ctx.run("te", lambda e, hj=hj, oc=oc, b=b: e.matmul(
                        pso[oc][:, :], WD[b][:, oc * 128:(oc + 1) * 128],
                        HALL[:, hj * 512:(hj + 1) * 512],
                        start=(hj == 0), stop=(hj == NH - 1)),
                        after=[wd_l[b], h_t[0], h_t[1], evo.get(oc)], tok=(oc == NCH - 1))
                wdmm[b] = lastmm
            p2last = lastmm
            for oc in range(NCH):
                sb = oc % 2
                ev = ctx.run("ac", lambda e, oc=oc, sb=sb: e.activation(
                    GT[:, sb * 512:sb * 512 + 512], pso[oc][:, :], AF.Silu, scale=1.0 / 127.0),
                    after=[lastmm, stok[sb]], tok=True)
                stok[sb] = rstg.dma("sy", g_r[:, oc, t0:t0 + TTW], GT[:, sb * 512:sb * 512 + 512],
                                    after=[ev])
                evo[oc] = stok[sb]
        out = [stok[0], stok[1]]
        rg.close(); ru.close(); rd.close(); rstg.close()
    return out


def emit_l3attn(ctx, bufs, W, deps, gate_dram, xres_in, xres_out, g_dram):
    nc = ctx.nc
    gate_r = gate_dram.rearrange("(ch p) t -> p ch t", p=128)
    L3C = bufs["l3c"]
    CPP = 4 * 128 + 8
    toks = []
    with contextlib.ExitStack() as es:
        QB = es.enter_context(nc.sbuf_tensor(uname("a3_qb"), [128, T], BF16))
        KB = es.enter_context(nc.sbuf_tensor(uname("a3_kb"), [128, T], BF16))
        KTB = es.enter_context(nc.sbuf_tensor(uname("a3_kt"), [128, T], BF16))
        VB = es.enter_context(nc.sbuf_tensor(uname("a3_vb"), [128, T], BF16))
        QSB = es.enter_context(nc.sbuf_tensor(uname("a3_qs"), [128, 128], BF16))
        KDB = es.enter_context(nc.sbuf_tensor(uname("a3_kd"), [128, 128], BF16))
        PSB = es.enter_context(nc.sbuf_tensor(uname("a3_pb"), [128, 256], BF16))
        KVS = es.enter_context(nc.sbuf_tensor(uname("a3_kvs"), [128, 64], F32))
        KVB = es.enter_context(nc.sbuf_tensor(uname("a3_kvb"), [128, 64], BF16))
        WSB = [es.enter_context(nc.sbuf_tensor(uname("a3_w"), [128, NCH * 128], BF16)) for _ in range(3)]
        psq = [es.enter_context(nc.psum_tensor(uname("a3_psq"), [128, 512], F32)) for _ in range(2)]
        psS = [es.enter_context(nc.psum_tensor(uname("a3_s"), [128, 128], F32)) for _ in range(2)]
        psO = es.enter_context(nc.psum_tensor(uname("a3_o"), [128, 128], F32))
        psD = es.enter_context(nc.psum_tensor(uname("a3_d"), [128, 64], F32))
        rw = Ring(ctx, 3, sw=True)
        AO = bufs["AO"]
        prev_pair = list(deps)
        for pair in range(NPAIR):
            c0 = pair * CPP
            mA = L3C[:, c0:c0 + 128]
            mB = L3C[:, c0 + 128:c0 + 256]
            LQ = L3C[:, c0 + 256:c0 + 384]
            LK = L3C[:, c0 + 384:c0 + 512]
            lamc = L3C[:, c0 + 512:c0 + 513]
            wq = wslab_load(ctx, rw, WSB[0], W["l3_wq"], pair * 128, 128, after=prev_pair)
            wk = wslab_load(ctx, rw, WSB[1], W["l3_wk"], pair * 128, 128, after=prev_pair)
            wv = wslab_load(ctx, rw, WSB[2], W["l3_wv"], pair * 128, 128, after=prev_pair)
            evq = {0: None, 1: None}
            qtoks = []
            for tt in range(NTT):
                pb = tt % 2
                mmq = None
                for kc in range(NCH):
                    mmq = ctx.run("te", lambda e, kc=kc, tt=tt, pb=pb: e.matmul(
                        psq[pb][:, :], WSB[0][:, kc * 128:(kc + 1) * 128],
                        bufs["XN"][:, kc * T + tt * TTW: kc * T + tt * TTW + TTW],
                        start=(kc == 0), stop=(kc == NCH - 1)),
                        after=[wq, evq[pb]] + prev_pair, tok=(kc == NCH - 1))
                evq[pb] = ctx.run("ac", lambda e, tt=tt, pb=pb: e.activation(
                    QB[:, tt * TTW:(tt + 1) * TTW], psq[pb][:, :], AF.Silu), after=[mmq], tok=True)
                qtoks.append(evq[pb])
                mmk = None
                for kc in range(NCH):
                    mmk = ctx.run("te", lambda e, kc=kc, tt=tt, pb=pb: e.matmul(
                        psq[1 - pb][:, :], WSB[1][:, kc * 128:(kc + 1) * 128],
                        bufs["XN"][:, kc * T + tt * TTW: kc * T + tt * TTW + TTW],
                        start=(kc == 0), stop=(kc == NCH - 1)),
                        after=[wk, evq[1 - pb]] + prev_pair, tok=(kc == NCH - 1))
                evq[1 - pb] = ctx.run("ac", lambda e, tt=tt, pb=pb: e.activation(
                    KB[:, tt * TTW:(tt + 1) * TTW], psq[1 - pb][:, :], AF.Silu), after=[mmk], tok=True)
                qtoks.append(evq[1 - pb])
            vtoks = []
            for tb in range(NTB):
                pb = tb % 2
                mmk = None
                for kc in range(NCH):
                    mmk = ctx.run("te", lambda e, kc=kc, tb=tb, pb=pb: e.matmul(
                        psq[pb][:, 0:128],
                        bufs["XN"][:, kc * T + tb * 128: kc * T + tb * 128 + 128],
                        WSB[1][:, kc * 128:(kc + 1) * 128],
                        start=(kc == 0), stop=(kc == NCH - 1)),
                        after=[wk, evq[pb]], tok=(kc == NCH - 1))
                evq[pb] = ctx.run("ac", lambda e, tb=tb, pb=pb: e.activation(
                    KTB[:, tb * 128:(tb + 1) * 128], psq[pb][:, 0:128], AF.Silu), after=[mmk], tok=True)
                mmv = None
                for kc in range(NCH):
                    mmv = ctx.run("te", lambda e, kc=kc, tb=tb, pb=pb: e.matmul(
                        psq[1 - pb][:, 0:128],
                        bufs["XN"][:, kc * T + tb * 128: kc * T + tb * 128 + 128],
                        WSB[2][:, kc * 128:(kc + 1) * 128],
                        start=(kc == 0), stop=(kc == NCH - 1)),
                        after=[wv, evq[1 - pb]], tok=(kc == NCH - 1))
                evq[1 - pb] = ctx.run("ac", lambda e, tb=tb, pb=pb: e.activation(
                    VB[:, tb * 128:(tb + 1) * 128], psq[1 - pb][:, 0:128], AF.Silu), after=[mmv], tok=True)
                vtoks.append(evq[1 - pb])
                vtoks.append(evq[pb])
            ready = qtoks[-2:] + vtoks[-2:]
            z1 = ctx.run("dv", lambda e: e.memset(KVS[:, :], 0.0), after=prev_pair, tok=True)
            z2 = ctx.run("dv", lambda e: e.memset(KVB[:, :], 0.0), after=[z1], tok=True)
            kvb_t = z2
            kvs_t = z1
            ev_o = None
            pmask = {}
            dmm = None
            qs_t = None
            kd_t = None
            for i in range(NTB):
                i0 = i * 128
                qs_t = ctx.run("dv", lambda e, i0=i0: e.tensor_mul(
                    QSB[:, :], QB[:, i0:i0 + 128], LQ), after=ready + [pmask.get("qs_r")], tok=True)
                kd_t = ctx.run("dv", lambda e, i0=i0: e.tensor_mul(
                    KDB[:, :], KTB[:, i0:i0 + 128], LK), after=ready + [pmask.get("kd_r")], tok=True)
                smm = {}
                for h in range(2):
                    hb = 64 * h
                    smm[h] = ctx.run("te", lambda e, h=h, hb=hb, i0=i0: e.matmul(
                        psS[h][:, :],
                        KB[hb:hb + 64, i0:i0 + 128], QB[hb:hb + 64, i0:i0 + 128],
                        start=True, stop=True),
                        after=ready + [pmask.get(h)], tok=True)
                for h in range(2):
                    pmask[h] = ctx.run("dv", lambda e, h=h: e.tensor_mul(
                        PSB[:, h * 128:h * 128 + 128], psS[h][:, :],
                        (mA if h == 0 else mB)), after=[smm[h], ev_o], tok=True)
                omm = None
                for h in range(2):
                    hb = 64 * h
                    omm = ctx.run("te", lambda e, h=h, hb=hb, i0=i0: e.matmul(
                        psO[hb:hb + 64, 0:128],
                        VB[:, i0 + hb:i0 + hb + 64], PSB[:, h * 128:h * 128 + 128],
                        start=True, stop=False),
                        after=[pmask[h], ev_o, kvb_t], tok=True)
                    omm = ctx.run("te", lambda e, h=h, hb=hb: e.matmul(
                        psO[hb:hb + 64, 0:128],
                        KVB[hb:hb + 64, 0:64], QSB[hb:hb + 64, 0:128],
                        start=False, stop=True),
                        after=[qs_t, kvb_t], tok=True)
                for h in range(2):
                    hb = 64 * h
                    dmm = ctx.run("te", lambda e, h=h, hb=hb, i0=i0: e.matmul(
                        psD[hb:hb + 64, 0:64],
                        KDB[:, hb:hb + 64], VB[:, i0 + hb:i0 + hb + 64],
                        start=True, stop=True),
                        after=[kd_t, kvs_t], tok=True)
                ev_o = ctx.run("dv", lambda e, i0=i0, pair=pair: e.tensor_copy(
                    AO[:, pair * T + i0: pair * T + i0 + 128], psO[:, 0:128]),
                    after=[omm], tok=True)
                pmask["qs_r"] = omm
                pmask["kd_r"] = dmm
                kvs_t = ctx.run("dv", lambda e: e.tensor_scalar(
                    KVS[:, :], KVS[:, :], lamc[:, 0:1], None, ALU.mult), after=[kvs_t, kvb_t], tok=True)
                kvs_t = ctx.run("dv", lambda e: e.tensor_add(
                    KVS[:, :], KVS[:, :], psD[:, 0:64]), after=[kvs_t, dmm], tok=True)
                kvb_t = ctx.run("dv", lambda e: e.tensor_copy(KVB[:, :], KVS[:, :]),
                                after=[kvs_t, omm], tok=True)
            prev_pair = [ev_o, kvb_t]
            toks = prev_pair
    return toks


def emit_layer3(ctx, bufs, W, xres_in, xres_out, scratch, deps):
    nc = ctx.nc
    les = ctx.es.enter_context(contextlib.ExitStack())
    bufs["XN"] = les.enter_context(nc.sbuf_tensor(uname("XN"), [128, NCH * T], BF16))
    bufs["AO"] = les.enter_context(nc.sbuf_tensor(uname("AO"), [128, NCH * T], BF16))
    g_dram = scratch["g"]
    gate_dram = scratch["gate"]
    t_ln = emit_ln(ctx, bufs, xres_in, deps)
    t_g = emit_l3ffn(ctx, bufs, W, t_ln, g_dram)
    with contextlib.ExitStack() as es:
        G1 = es.enter_context(nc.sbuf_tensor(uname("g3_1"), [128, T], BF16))
        GO = es.enter_context(nc.sbuf_tensor(uname("g3_o"), [128, 2 * TTW], BF16))
        rgs = Ring(ctx, 2)
        gate_r = gate_dram.rearrange("(ch p) t -> p ch t", p=128)
        st = [None, None]

        def ev1(di, tt, ps, mm):
            return ctx.run("dv", lambda e, tt=tt: e.tensor_copy(G1[0:64, tt * TTW:(tt + 1) * TTW], ps),
                           after=[mm], tok=True)
        tg1 = emit_proj_fm(ctx, bufs, W["l3_wg1"], bufs["XN"], 0, [(0, 64)], ev1, t_ln + t_g)

        def ev2(di, tt, ps, mm):
            sb = (di * NTT + tt) % 2
            ak = ctx.run("ac", lambda e, ps=ps, sb=sb: e.activation(
                GO[:, sb * TTW:(sb + 1) * TTW], ps, AF.Sigmoid), after=[mm, st[sb]], tok=True)
            st[sb] = rgs.dma("sy", gate_r[:, di, tt * TTW:(tt + 1) * TTW], GO[:, sb * TTW:(sb + 1) * TTW],
                             after=[ak])
            return st[sb]
        tg2 = emit_proj_fm(ctx, bufs, W["l3_wg2"], G1, 0, [(c * 128, 128) for c in range(NCH)],
                           ev2, tg1, din=64)
        t_gate = [st[0], st[1]]
        rgs.close()
    t_at = emit_l3attn(ctx, bufs, W, t_ln + t_gate, gate_dram, xres_in, xres_out, g_dram)
    with contextlib.ExitStack() as es:
        GTI = es.enter_context(nc.sbuf_tensor(uname("o3_g"), [128, NCH * 512], BF16))
        rgl = Ring(ctx, 2)
        gate_r = gate_dram.rearrange("(ch p) t -> p ch t", p=128)
        gb_cols = bufs["l3ab"]

        def produce(tt, after):
            t0 = tt * TTW
            gt = rgl.dma("sy", GTI[:, :].rearrange("p (ch m) -> p ch m", m=512),
                         gate_r[:, :, t0:t0 + TTW], after=after)
            return gt, (lambda ch, t0=t0: bufs["AO"][:, ch * T + t0: ch * T + t0 + TTW])

        def consume(tt, ch, scr, rst, after):
            t0 = tt * TTW
            k1 = ctx.run("dv", lambda e, scr=scr, rst=rst: e.tensor_mul(scr, scr, rst), after=after, tok=True)
            k2 = ctx.run("dv", lambda e, ch=ch, scr=scr: e.tensor_scalar(
                scr, scr, gb_cols[:, ch:ch + 1], gb_cols[:, 8 + ch:8 + ch + 1], ALU.mult, ALU.add),
                after=[k1], tok=True)
            k3 = ctx.run("dv", lambda e, ch=ch, t0=t0, scr=scr: e.tensor_mul(
                bufs["AO"][:, ch * T + t0: ch * T + t0 + TTW], scr,
                GTI[:, ch * 512:ch * 512 + 512]), after=[k2], tok=True)
            return k3
        t_m = emit_lnx(ctx, bufs, t_at, produce, consume, eps=1e-5)
        rgl.close()
    t_oc = emit_outproj_combine(ctx, bufs, W["l3_wo"], bufs["AO"], xres_in, xres_out,
                                g_dram, t_m + t_g)
    les.close()
    return t_oc


TPAD = 8


def _xl_slice(XL, ch, t0, w, shifted):
    base = ch * (T + TPAD) + (TPAD - 1 if shifted else TPAD)
    return XL[:, base + t0: base + t0 + w]


def emit_l2_inner_attn(ctx, bufs, W, li, XL, ao_dram, deps):
    nc = ctx.nc
    ao_r = ao_dram.rearrange("(ch p) t -> p ch t", p=128)
    toks = []
    tes = contextlib.ExitStack()
    for nm in ("csq", "ssq", "csk", "ssk"):
        bufs[nm] = tes.enter_context(nc.sbuf_tensor(uname(nm), [128, T], BF16))
    tbl_toks = []
    mr = misc_ring(ctx, "gp")
    for nm, col in (("csq", ("l2qs", 8.0)), ("ssq", ("l2qs", 8.0)),
                    ("csk", ("l2ks", None)), ("ssk", ("l2ks", None))):
        lt = mr.dma("gp", bufs[nm][:, :], W["c_" + nm], after=list(deps) + [bufs.get("_tblg")])
        cn, extra = col
        ft = ctx.run("dv", lambda e, nm=nm, cn=cn, extra=extra, li=li: (
            e.tensor_scalar(bufs[nm][:, :], bufs[nm][:, :], bufs[cn][:, li:li + 1], extra,
                            ALU.mult, ALU.mult) if extra is not None else
            e.tensor_scalar(bufs[nm][:, :], bufs[nm][:, :], bufs[cn][:, li:li + 1], None, ALU.mult)),
            after=[lt], tok=True)
        tbl_toks.append(ft)
    bufs["_tblg"] = None
    deps = list(deps) + tbl_toks
    with contextlib.ExitStack() as es:
        QB = es.enter_context(nc.sbuf_tensor(uname("i2_qb"), [128, T], BF16))
        KB = es.enter_context(nc.sbuf_tensor(uname("i2_kb"), [128, T], BF16))
        VB = es.enter_context(nc.sbuf_tensor(uname("i2_vb"), [128, T], BF16))
        SQ = es.enter_context(nc.sbuf_tensor(uname("i2_sq"), [128, 512], BF16))
        SSB = es.enter_context(nc.sbuf_tensor(uname("i2_ss"), [128, 512], F32))
        QH = es.enter_context(nc.sbuf_tensor(uname("i2_qh"), [128, 512], BF16))
        Q2 = es.enter_context(nc.sbuf_tensor(uname("i2_q2"), [128, 512], BF16))
        PB = es.enter_context(nc.sbuf_tensor(uname("i2_pb"), [128, 2 * 512], BF16))
        PT = es.enter_context(nc.sbuf_tensor(uname("i2_pt"), [128, 512], BF16))
        NM = es.enter_context(nc.sbuf_tensor(uname("i2_nm"), [128, 4], F32))
        AOP = es.enter_context(nc.sbuf_tensor(uname("i2_ao"), [128, T], BF16))
        WSB = [es.enter_context(nc.sbuf_tensor(uname("i2_w"), [128, NCH * 128], BF16)) for _ in range(3)]
        psq = [es.enter_context(nc.psum_tensor(uname("i2_psq"), [128, 512], F32)) for _ in range(2)]
        psS = [es.enter_context(nc.psum_tensor(uname("i2_s"), [128, 512], F32)) for _ in range(2)]
        psT = es.enter_context(nc.psum_tensor(uname("i2_t"), [128, 1024], BF16))
        psO = es.enter_context(nc.psum_tensor(uname("i2_o"), [128, 512], F32))
        psR = es.enter_context(nc.psum_tensor(uname("i2_r"), [128, 512], F32))
        rw = Ring(ctx, 3, sw=True)
        rao = Ring(ctx, 2)
        prev_pair = list(deps)
        ao_st = None
        for pair in range(NPAIR):
            wq = wslab_load(ctx, rw, WSB[0], W["l2f_qkv"][li], pair * 128, 128, after=prev_pair)
            wk = wslab_load(ctx, rw, WSB[1], W["l2f_qkv"][li], 1024 + pair * 128, 128, after=prev_pair)
            wv = wslab_load(ctx, rw, WSB[2], W["l2f_qkv"][li], 2048 + pair * 128, 128, after=prev_pair)
            evq = {0: None, 1: None}
            qtoks = []
            for which, WS, wtok, TBL_C, TBL_S, scol in (
                    (0, WSB[0], wq, "csq", "ssq", 0), (1, WSB[1], wk, "csk", "ssk", 1)):
                BUF = QB if which == 0 else KB
                for tt in range(NTT):
                    t0 = tt * TTW
                    pb = tt % 2
                    mmq = None
                    for kc in range(NCH):
                        mmq = ctx.run("te", lambda e, kc=kc, t0=t0, pb=pb, WS=WS: e.matmul(
                            psq[pb][:, :], WS[:, kc * 128:(kc + 1) * 128],
                            _xl_slice(XL, kc, t0, TTW, True),
                            start=(kc == 0), stop=(kc == NCH - 1)),
                            after=[wtok, evq[pb]] + prev_pair, tok=(kc == NCH - 1))
                    sqt = ctx.run("ac", lambda e, pb=pb: e.activation(SQ[:, :], psq[pb][:, :], AF.Square),
                                  after=[mmq, qtoks[-1] if qtoks else None], tok=True)
                    ssm = None
                    for h in range(2):
                        hb = 64 * h
                        ssm = ctx.run("te", lambda e, hb=hb: e.matmul(
                            psR[hb:hb + 64, 0:TTW], bufs["one64"][hb:hb + 64, 0:64], SQ[hb:hb + 64, 0:TTW],
                            start=True, stop=True),
                            after=[sqt, qtoks[-1] if qtoks else None], tok=(h == 1))
                    l1_ = ctx.run("ac", lambda e: e.activation(SSB[:, :], psR[:, 0:TTW], AF.Ln, bias=1e-24),
                                  after=[ssm], tok=True)
                    l2_ = ctx.run("ac", lambda e: e.activation(SSB[:, :], SSB[:, :], AF.Exp, scale=-0.5),
                                  after=[l1_], tok=True)
                    qh = ctx.run("dv", lambda e, pb=pb: e.tensor_mul(QH[:, :], psq[pb][:, :], SSB[:, :]),
                                 after=[l2_, mmq], tok=True)
                    evq[pb] = qh
                    q2 = ctx.run("gp", lambda e, t0=t0, TBL_S=TBL_S: e.tensor_mul(
                        Q2[:, :], QH[:, :], bufs[TBL_S][:, t0:t0 + TTW]), after=[qh], tok=True)
                    rmm = ctx.run("te", lambda e: e.matmul(
                        psR[:, 0:TTW], bufs["rot0"][:, :], Q2[:, :], start=True, stop=True),
                        after=[q2, l2_], tok=True)
                    t3 = ctx.run("gp", lambda e, t0=t0, TBL_C=TBL_C: e.tensor_mul(
                        Q2[:, :], QH[:, :], bufs[TBL_C][:, t0:t0 + TTW]), after=[q2, rmm], tok=True)
                    fin = ctx.run("dv", lambda e, BUF=BUF, t0=t0: e.tensor_add(
                        BUF[:, t0:t0 + TTW], Q2[:, :], psR[:, 0:TTW]), after=[t3, rmm], tok=True)
                    qtoks.append(fin)
            vtoks = []
            for tb in range(NTB):
                pb = tb % 2
                mmv = None
                for kc in range(NCH):
                    mmv = ctx.run("te", lambda e, kc=kc, tb=tb, pb=pb: e.matmul(
                        psq[pb][:, 0:128],
                        _xl_slice(XL, kc, tb * 128, 128, True),
                        WSB[2][:, kc * 128:(kc + 1) * 128],
                        start=(kc == 0), stop=(kc == NCH - 1)),
                        after=[wv, evq[pb], qtoks[-1]], tok=(kc == NCH - 1))
                evq[pb] = ctx.run("dv", lambda e, tb=tb, pb=pb: e.tensor_copy(
                    VB[:, tb * 128:(tb + 1) * 128], psq[pb][:, 0:128]), after=[mmv], tok=True)
                vtoks.append(evq[pb])
            ready = qtoks[-2:] + vtoks[-2:]
            expo = {}
            evo = {0: None, 1: None}
            for qb in range(NTB):
                if qb == 0:
                    k0, klen, mcls = 0, 256, 0
                elif qb == 1:
                    k0, klen, mcls = 0, 256, 1
                elif qb % 2 == 0:
                    k0, klen, mcls = (qb - 2) * 128, 512, 2
                else:
                    k0, klen, mcls = (qb - 3) * 128, 512, 3
                mask = bufs["l2m"][:, mcls * 512: mcls * 512 + klen]
                for h in range(2):
                    hb = 64 * h
                    sb = h
                    mm = ctx.run("te", lambda e, h=h, hb=hb, sb=sb, qb=qb, k0=k0, klen=klen: e.matmul(
                        psS[sb][:, 0:klen],
                        QB[hb:hb + 64, qb * 128:qb * 128 + 128],
                        KB[hb:hb + 64, k0:k0 + klen],
                        start=True, stop=True),
                        after=ready + [expo.get(sb)], tok=True)
                    mtok = ctx.run("dv", lambda e, sb=sb, klen=klen, mask=mask: e.tensor_add(
                        psS[sb][:, 0:klen], psS[sb][:, 0:klen], mask), after=[mm], tok=True)
                    nmt = ctx.run("dv", lambda e, sb=sb, klen=klen, h=h: e.tensor_reduce(
                        NM[:, h:h + 1], psS[sb][:, 0:klen], AX.X, ALU.max, negate=True),
                        after=[mtok], tok=True)
                    ext = ctx.run("ac", lambda e, h=h, sb=sb, klen=klen: e.activation(
                        PB[:, h * 512: h * 512 + klen], psS[sb][:, 0:klen],
                        AF.Exp, bias=NM[:, h:h + 1], accum_out=NM[:, 2 + h:3 + h]),
                        after=[nmt], tok=True)
                    expo[sb] = ext
                    rc = ctx.run("dv", lambda e, h=h: e.reciprocal(NM[:, h:h + 1], NM[:, 2 + h:3 + h]),
                                 after=[ext], tok=True)
                    rs = ctx.run("dv", lambda e, h=h, klen=klen: e.tensor_scalar(
                        PB[:, h * 512:h * 512 + klen], PB[:, h * 512:h * 512 + klen],
                        NM[:, h:h + 1], None, ALU.mult), after=[rc], tok=True)
                    nkc = klen // 128
                    evt = None
                    mmo = None
                    for kc in range(nkc):
                        tb4 = kc % 4
                        ttk = ctx.run("te", lambda e, h=h, kc=kc, tb4=tb4: e.transpose(
                            psT[:, tb4 * 128:tb4 * 128 + 128],
                            PB[:, h * 512 + kc * 128: h * 512 + kc * 128 + 128],
                            bufs["identb"][:, :]),
                            after=[rs, evt], tok=True)
                        evt = ctx.run("dv", lambda e, tb4=tb4: e.tensor_copy(
                            PT[:, tb4 * 128:tb4 * 128 + 128], psT[:, tb4 * 128:tb4 * 128 + 128]),
                            after=[ttk], tok=True)
                        mmo = ctx.run("te", lambda e, h=h, hb=hb, kc=kc, tb4=tb4, k0=k0, nkc=nkc: e.matmul(
                            psO[hb:hb + 64, h * 128:h * 128 + 128],
                            VB[:, k0 + kc * 128 + hb: k0 + kc * 128 + hb + 64],
                            PT[:, tb4 * 128:tb4 * 128 + 128],
                            start=(kc == 0), stop=(kc == nkc - 1)),
                            after=[evt, evo[0], evo[1]], tok=(kc == nkc - 1))
                    evo[h] = ctx.run("dv", lambda e, h=h, hb=hb, qb=qb: e.tensor_copy(
                        AOP[hb:hb + 64, qb * 128:qb * 128 + 128],
                        psO[hb:hb + 64, h * 128:h * 128 + 128]),
                        after=[mmo], tok=True)
            ao_st = rao.dma("sy", ao_r[:, pair, :], AOP[:, :], after=[evo[0], evo[1], ao_st])
            prev_pair = [evo[0], evo[1], ao_st]
            toks = prev_pair
        rw.close(); rao.close()
    bufs["_tblg"] = toks[0]
    tes.close()
    return toks


def emit_l2_inner_wo(ctx, bufs, W, li, XL, ao_dram, deps):
    nc = ctx.nc
    ao_r = ao_dram.rearrange("(ch p) t -> p ch t", p=128)
    toks = []
    with contextlib.ExitStack() as es:
        MV = es.enter_context(nc.sbuf_tensor(uname("wo_mv"), [128, 2 * TTW], BF16))
        WSB = [es.enter_context(nc.sbuf_tensor(uname("wo_w"), [128, NCH * 128], BF16)) for _ in range(2)]
        pss = [es.enter_context(nc.psum_tensor(uname("wo_ps"), [128, TTW], F32)) for _ in range(2)]
        rw = Ring(ctx, 2, sw=True)
        rm = Ring(ctx, 2)
        wld = [None, None]
        lastmm = [None, None]
        ev = [None, None]
        mvl = {}
        for di in range(NCH):
            wb = di % 2
            wld[wb] = wslab_load(ctx, rw, WSB[wb], W["l2f_wo"][li], di * 128, 128,
                                 after=[lastmm[wb]] + (list(deps) if di < 2 else []))
            for tt in range(NTT):
                pb = tt % 2
                mm = None
                for kc in range(NCH):
                    mb = (tt * NCH + kc) % 2
                    mvt = rm.dma("sy", MV[:, mb * TTW:(mb + 1) * TTW], ao_r[:, kc, tt * TTW:(tt + 1) * TTW],
                                 after=[mvl.get(mb)] + (list(deps) if di == 0 and tt == 0 and kc < 2 else []))
                    mm = ctx.run("te", lambda e, kc=kc, wb=wb, pb=pb, mb=mb: e.matmul(
                        pss[pb][:, :], WSB[wb][:, kc * 128:(kc + 1) * 128],
                        MV[:, mb * TTW:(mb + 1) * TTW],
                        start=(kc == 0), stop=(kc == NCH - 1)),
                        after=[wld[wb], ev[pb], mvt], tok=True)
                    mvl[mb] = mm
                lastmm[wb] = mm
                ev[pb] = ctx.run("dv", lambda e, di=di, tt=tt, pb=pb: e.tensor_add(
                    _xl_slice(XL, di, tt * TTW, TTW, False),
                    _xl_slice(XL, di, tt * TTW, TTW, False), pss[pb][:, :]),
                    after=[mm], tok=True)
                toks.append(ev[pb])
        rw.close(); rm.close()
    return toks[-2:]


def emit_l2_inner_ffn(ctx, bufs, W, li, XL, deps):
    nc = ctx.nc
    NHJ = 22
    hsz = [min(128, 2730 - 128 * j) for j in range(NHJ)]
    toks = []
    with contextlib.ExitStack() as es:
        XSN = es.enter_context(nc.sbuf_tensor(uname("fi_xsn"), [128, NCH * T], BF16))
        def produce(tt, after):
            return None, (lambda ch, t0=tt * TTW: _xl_slice(XL, ch, t0, TTW, True))

        def consume(tt, ch, scr, rst, after):
            t0 = tt * TTW
            k1 = ctx.run("dv", lambda e, scr=scr, rst=rst: e.tensor_mul(scr, scr, rst), after=after, tok=True)
            k2 = ctx.run("dv", lambda e, ch=ch, t0=t0, scr=scr: e.tensor_scalar(
                XSN[:, ch * T + t0: ch * T + t0 + TTW], scr,
                bufs["l2lng"][:, li * 8 + ch:li * 8 + ch + 1] if False else bufs["l2lng"][:, li * 16 + ch:li * 16 + ch + 1],
                bufs["l2lng"][:, li * 16 + 8 + ch:li * 16 + 8 + ch + 1], ALU.mult, ALU.add),
                after=[k1], tok=True)
            return k2
        t_ln = emit_lnx(ctx, bufs, deps, produce, consume, eps=1e-5)
        with contextlib.ExitStack() as es2:
            HA = es2.enter_context(nc.sbuf_tensor(uname("fi_h"), [128, NHJ * 1024], BF16))
            WA = [es2.enter_context(nc.sbuf_tensor(uname("fi_wa"), [128, NCH * 128], BF16)) for _ in range(2)]
            WG = [es2.enter_context(nc.sbuf_tensor(uname("fi_wg"), [128, NCH * 128], BF16)) for _ in range(2)]
            W2 = [es2.enter_context(nc.sbuf_tensor(uname("fi_w2"), [128, 1024], BF16)) for _ in range(2)]
            GEL = es2.enter_context(nc.sbuf_tensor(uname("fi_gel"), [128, 2 * 512], BF16))
            pso = [es2.enter_context(nc.psum_tensor(uname("fi_ps"), [128, 512], F32)) for _ in range(8)]
            rwa = Ring(ctx, 2, sw=True)
            rwg = Ring(ctx, 2, sw=True)
            rw2 = Ring(ctx, 2, sw=True)
            wa_l = [None, None]; wg_l = [None, None]; w2_l = [None, None]
            wamm = [None, None]; wgmm = [None, None]; w2mm = [None, None]
            gel_t = [None, None]
            h_t = {}
            evo = {}
            p2last = None
            TSUB = min(1024, T)
            for th in range(T // TSUB):
                tof = th * TSUB
                for hj in range(NHJ):
                    b = hj % 2
                    wa_l[b] = wslab_load(ctx, rwa, WA[b], W["l2f_w1"][li], hj * 128, hsz[hj],
                                         after=[wamm[b]] + (list(t_ln) if th == 0 and hj < 2 else []))
                    wg_l[b] = wslab_load(ctx, rwg, WG[b], W["l2f_w1"][li], 2730 + hj * 128, hsz[hj],
                                         after=[wgmm[b]] + (list(t_ln) if th == 0 and hj < 2 else []))
                    for half in range(TSUB // 512):
                        t0 = tof + half * 512
                        hb2 = half
                        cols = hsz[hj]
                        gmm = None
                        for kc in range(NCH):
                            gmm = ctx.run("te", lambda e, kc=kc, b=b, t0=t0, cols=cols: e.matmul(
                                pso[1][0:cols, :], WG[b][:, kc * cols:(kc + 1) * cols],
                                XSN[:, kc * T + t0: kc * T + t0 + 512],
                                start=(kc == 0), stop=(kc == NCH - 1)),
                                after=[wg_l[b], gel_t[0], gel_t[1], evo.get(1)], tok=(kc == NCH - 1))
                        wgmm[b] = gmm
                        gel_t[hb2] = ctx.run("ac", lambda e, hj=hj, hb2=hb2, cols=cols, li=li: e.activation(
                            GEL[0:cols, hb2 * 512:(hb2 + 1) * 512], pso[1][0:cols, :], AF.Gelu,
                            bias=bufs["l2b1g"][0:cols, li * NHJ + hj:li * NHJ + hj + 1]),
                            after=[gmm, h_t.get(hb2)], tok=True)
                        amm = None
                        for kc in range(NCH):
                            amm = ctx.run("te", lambda e, kc=kc, b=b, t0=t0, cols=cols: e.matmul(
                                pso[0][0:cols, :], WA[b][:, kc * cols:(kc + 1) * cols],
                                XSN[:, kc * T + t0: kc * T + t0 + 512],
                                start=(kc == 0), stop=(kc == NCH - 1)),
                                after=[wa_l[b], h_t.get("a"), evo.get(0)], tok=(kc == NCH - 1))
                        wamm[b] = amm
                        ht = ctx.run("dv", lambda e, hj=hj, hb2=hb2, half=half, cols=cols, li=li: e.scalar_tensor_tensor(
                            HA[0:cols, hj * 1024 + half * 512: hj * 1024 + half * 512 + 512],
                            pso[0][0:cols, :], bufs["l2b1a"][0:cols, li * NHJ + hj:li * NHJ + hj + 1],
                            GEL[0:cols, hb2 * 512:(hb2 + 1) * 512], ALU.add, ALU.mult),
                            after=[amm, gel_t[hb2], p2last], tok=True)
                        h_t[hb2] = ht
                        h_t["a"] = ht
                for half in range(TSUB // 512):
                    t0 = tof + half * 512
                    lastmm = None
                    for hj in range(NHJ):
                        b = hj % 2
                        if half == 0:
                            w2_l[b] = rw2.dma("gp", W2[b][0:hsz[hj], :],
                                              W["l2f_w2"][li][hj * 128: hj * 128 + hsz[hj], :],
                                              after=[w2mm[b]])
                        for oc in range(NCH):
                            lastmm = ctx.run("te", lambda e, hj=hj, oc=oc, b=b, half=half, cols=hsz[hj]: e.matmul(
                                pso[oc][:, :], W2[b][0:cols, oc * 128:(oc + 1) * 128],
                                HA[0:cols, hj * 1024 + half * 512: hj * 1024 + half * 512 + 512],
                                start=(hj == 0), stop=(hj == NHJ - 1)),
                                after=[w2_l[b], h_t.get(0), h_t.get(1), evo.get(oc)], tok=(oc == NCH - 1))
                        if half == TSUB // 512 - 1:
                            w2mm[b] = lastmm
                    p2last = lastmm
                    for oc in range(NCH):
                        ev = ctx.run("dv", lambda e, oc=oc, t0=t0, li=li: e.scalar_tensor_tensor(
                            _xl_slice(XL, oc, t0, 512, False), pso[oc][:, :],
                            bufs["l2b2"][:, li * 8 + oc:li * 8 + oc + 1],
                            _xl_slice(XL, oc, t0, 512, False), ALU.add, ALU.add),
                            after=[lastmm, evo.get(oc)], tok=True)
                        evo[oc] = ev
                        toks.append(ev)
    return toks[-2:]


def emit_l2_xl_init(ctx, bufs, XL, src_dram, deps):
    nc = ctx.nc
    rxl = Ring(ctx, 2)
    src_r = src_dram.rearrange("(ch p) t -> p ch t", p=128)
    zt = ctx.run("dv", lambda e: e.memset(XL[:, :], 0.0), after=deps, tok=True)
    lts2 = []
    for ch in range(NCH):
        lts2.append(rxl.dma("sy", XL[:, ch * (T + TPAD) + TPAD: ch * (T + TPAD) + TPAD + T],
                            src_r[:, ch, :], after=[zt]))
    rxl.close()
    return lts2[-2:] + [zt]


def emit_l2_xl_save(ctx, bufs, XL, dst_dram, deps):
    ctx_toks = []
    rxl = Ring(ctx, 2)
    dst_r = dst_dram.rearrange("(ch p) t -> p ch t", p=128)
    for ch in range(NCH):
        ctx_toks.append(rxl.dma("sy", dst_r[:, ch, :],
                                XL[:, ch * (T + TPAD) + TPAD: ch * (T + TPAD) + TPAD + T],
                                after=deps if ch < 2 else []))
    rxl.close()
    return ctx_toks[-2:]


def emit_l2_inner_span(ctx, bufs, W, li0, li1, XL, ao_dram, deps):
    t_cur = deps
    for li in range(li0, li1):
        t_a = emit_l2_inner_attn(ctx, bufs, W, li, XL, ao_dram, t_cur)
        t_w = emit_l2_inner_wo(ctx, bufs, W, li, XL, ao_dram, t_a)
        t_f = emit_l2_inner_ffn(ctx, bufs, W, li, XL, t_w)
        t_cur = t_f
    return t_cur


def emit_l2_post(ctx, bufs, W, xres_in, xres_out, xn_dram, xl_dram, g_dram, deps):
    nc = ctx.nc
    esG = contextlib.ExitStack()
    XLI = esG.enter_context(nc.sbuf_tensor(uname("g2_xl"), [128, 4096], BF16))
    GT = esG.enter_context(nc.sbuf_tensor(uname("g2_gt"), [128, 2 * 512], BF16))
    rgw = Ring(ctx, 2)
    rxr = Ring(ctx, 2)
    xl_r = xl_dram.rearrange("(ch p) t -> p ch t", p=128)
    g_r = g_dram.rearrange("(ch p) t -> p ch t", p=128)
    stt_ = [None, None]
    lt = None
    for tt in range(NTT):
        lt = rxr.dma("sy", XLI[:, :].rearrange("p (ch m) -> p ch m", m=512),
                     xl_r[:, :, tt * TTW:(tt + 1) * TTW],
                     after=[stt_[0], stt_[1]] + (list(deps) if tt == 0 else []))
        for ch in range(NCH):
            sb = ch % 2
            ak = ctx.run("ac", lambda e, ch=ch, sb=sb: e.activation(
                GT[:, sb * 512:(sb + 1) * 512], XLI[:, ch * 512:ch * 512 + 512], AF.Silu),
                after=[lt, stt_[sb]], tok=True)
            stt_[sb] = rgw.dma("sy", g_r[:, ch, tt * TTW:(tt + 1) * TTW], GT[:, sb * 512:(sb + 1) * 512],
                               after=[ak])
    t_g = [stt_[0], stt_[1]]
    rgw.close(); rxr.close()
    esG.close()
    esD = ctx.es.enter_context(contextlib.ExitStack())
    bufs["XN"] = esD.enter_context(nc.sbuf_tensor(uname("XN"), [128, NCH * T], BF16))
    ao2_dram = nc.dram_tensor(uname("ao2"), [D, T], BF16).ap()
    xn_r = xn_dram.rearrange("(ch p) t -> p ch t", p=128)
    rx2 = Ring(ctx, 2)
    lts = []
    for tt in range(NTT):
        lts.append(rx2.dma("sy", bufs["XN"][:, :].rearrange("p (ch t) -> p ch t", t=T)[:, :, tt * TTW:(tt + 1) * TTW],
                           xn_r[:, :, tt * TTW:(tt + 1) * TTW], after=t_g if tt == 0 else []))
    rx2.close()

    def q_evac(pair, tt, ps, mm, dst):
        return ctx.run("dv", lambda e, ps=ps, dst=dst, pair=pair: e.tensor_scalar(
            dst, ps, bufs["l2bq"][:, pair:pair + 1], 0.125, ALU.add, ALU.mult),
            after=[mm], tok=True)

    def k_evac(pair, tt, ps, mm, dst):
        return ctx.run("dv", lambda e, ps=ps, dst=dst, pair=pair: e.tensor_scalar(
            dst, ps, bufs["l2bk"][:, pair:pair + 1], None, ALU.add),
            after=[mm], tok=True)

    def v_evac(pair, tb, ps, mm, dst):
        return ctx.run("dv", lambda e, ps=ps, dst=dst: e.tensor_copy(dst, ps), after=[mm], tok=True)

    def kset(qb):
        klen = (qb + 1) * 128
        return (0, klen, [(klen - 128, 128, bufs["m_tri"][:, :])])

    cfg = dict(wq_ap=W["l2_wq"], wk_ap=W["l2_wk"], wv_ap=W["l2_wv"],
               q_evac=q_evac, k_evac=k_evac, v_evac=v_evac, qkln=None, kset=kset,
               AO=None, ao_dram=ao2_dram,
               v_bias=(bufs["ones1"][0:1, 0:128], lambda pair: bufs["l2bvr"][0:1, pair * 128:(pair + 1) * 128]))
    t_at = emit_attn_pairs(ctx, bufs, cfg, lts[-2:])
    t_oc = emit_outproj_combine(ctx, bufs, W["l2_woT"], None, xres_in, xres_out,
                                g_dram, t_at + t_g, bias_col=bufs["l2bo"], ao_dram=ao2_dram)
    esD.close()
    return t_oc


def l2_window_classes():
    import numpy as np
    m = np.full((128, 4 * 512), NEG, np.float32)
    ir = np.arange(128)[:, None]
    j = np.arange(512)[None, :]
    for cls, d, klen in ((0, 0, 256), (1, 128, 256), (2, 256, 512), (3, 384, 512)):
        ok = (j <= ir + d) & (j >= ir + d - 256) & (j < klen)
        m[:, cls * 512: cls * 512 + 512] = np.where(ok, 0.0, NEG)
    return m


def decl_l2(nc):
    W = {}
    W["l2f_qkv"] = nc.dram_tensor("l2f_qkv", [6, D, 3072], F32, kind="ExternalInput").ap()
    W["l2f_wo"] = nc.dram_tensor("l2f_wo", [6, D, D], F32, kind="ExternalInput").ap()
    W["l2f_w1"] = nc.dram_tensor("l2f_w1", [6, D, 5460], F32, kind="ExternalInput").ap()
    W["l2f_w2"] = nc.dram_tensor("l2f_w2", [6, 2730, D], F32, kind="ExternalInput").ap()
    W["l2_wq"] = nc.dram_tensor("l2_wq", [D, D], F32, kind="ExternalInput").ap()
    W["l2_wk"] = nc.dram_tensor("l2_wk", [D, D], F32, kind="ExternalInput").ap()
    W["l2_wv"] = nc.dram_tensor("l2_wv", [D, D], F32, kind="ExternalInput").ap()
    W["l2_woT"] = nc.dram_tensor("l2_woT", [D, D], F32, kind="ExternalInput").ap()
    for nm in ("c_csq", "c_ssq", "c_csk", "c_ssk"):
        W[nm] = nc.dram_tensor(nm, [128, T], F32, kind="ExternalInput").ap()
    return W


def setup_l2_bufs(ctx, bufs, es, deps=()):
    nc = ctx.nc
    toks = []
    mr = misc_ring(ctx, "gp")
    specs = [("l2lng", [128, 96]), ("l2b1a", [128, 132]), ("l2b1g", [128, 132]),
             ("l2b2", [128, 48]), ("l2bq", [128, 8]), ("l2bk", [128, 8]),
             ("l2bo", [128, 8]), ("l2qs", [128, 6]), ("l2ks", [128, 6])]
    for nm, shp in specs:
        bufs[nm] = es.enter_context(nc.sbuf_tensor(uname(nm), shp, F32))
        ap = nc.dram_tensor("c_" + nm, shp, F32, kind="ExternalInput").ap()
        toks.append(mr.dma("gp", bufs[nm][:, :], ap, after=deps))
    for nm, shp in (("l2m", [128, 4 * 512]), ("l2bvr", [1, 8 * 128]),
                    ("rot0", [128, 128]), ("ones1", [128, 128]), ("one64", [128, 64])):
        bufs[nm] = es.enter_context(nc.sbuf_tensor(uname(nm), shp, BF16 if nm != "l2m" else F32))
        ap = nc.dram_tensor("c_" + nm, shp, F32, kind="ExternalInput").ap()
        toks.append(mr.dma("gp", bufs[nm][0:shp[0], :], ap, after=deps))
    return toks


def stage_l2(inputs):
    f32 = np.float32
    asf = lambda a: np.ascontiguousarray(np.asarray(a, f32))
    d = {
        "l2f_qkv": asf(inputs["L2_ffn_Wqkv"]),
        "l2f_wo": asf(inputs["L2_ffn_Wo"]),
        "l2f_w1": asf(inputs["L2_ffn_W1"]),
        "l2f_w2": asf(inputs["L2_ffn_W2"]),
        "l2_wq": np.ascontiguousarray(np.asarray(inputs["L2_attn_W_q"], f32).transpose(1, 0, 2).reshape(D, D)),
        "l2_wk": np.ascontiguousarray(np.asarray(inputs["L2_attn_W_k"], f32).transpose(1, 0, 2).reshape(D, D)),
        "l2_wv": np.ascontiguousarray(np.asarray(inputs["L2_attn_W_v"], f32).transpose(1, 0, 2).reshape(D, D)),
        "l2_woT": np.ascontiguousarray(np.asarray(inputs["L2_attn_W_o"], f32).T),
    }
    lng = np.zeros((128, 96), f32)
    for li in range(6):
        lng[:, li * 16:li * 16 + 8] = np.asarray(inputs["L2_ffn_ln_g"][li], f32).reshape(8, 128).T
        lng[:, li * 16 + 8:li * 16 + 16] = np.asarray(inputs["L2_ffn_ln_b"][li], f32).reshape(8, 128).T
    d["c_l2lng"] = lng
    b1a = np.zeros((128, 132), f32); b1g = np.zeros((128, 132), f32)
    for li in range(6):
        b1 = np.asarray(inputs["L2_ffn_b1"][li], f32)
        for hj in range(22):
            n = min(128, 2730 - hj * 128)
            b1a[0:n, li * 22 + hj] = b1[hj * 128: hj * 128 + n]
            b1g[0:n, li * 22 + hj] = b1[2730 + hj * 128: 2730 + hj * 128 + n]
    d["c_l2b1a"] = b1a; d["c_l2b1g"] = b1g
    b2 = np.zeros((128, 48), f32)
    for li in range(6):
        b2[:, li * 8:(li + 1) * 8] = np.asarray(inputs["L2_ffn_b2"][li], f32).reshape(8, 128).T
    d["c_l2b2"] = b2
    bq = np.asarray(inputs["L2_attn_b_q"], f32).reshape(8, 128).T
    bk = np.asarray(inputs["L2_attn_b_k"], f32).reshape(8, 128).T
    d["c_l2bq"] = np.ascontiguousarray(bq); d["c_l2bk"] = np.ascontiguousarray(bk)
    d["c_l2bvr"] = np.asarray(inputs["L2_attn_b_v"], f32).reshape(1, 1024)
    d["c_l2bo"] = np.ascontiguousarray(np.asarray(inputs["L2_attn_b_o"], f32).reshape(8, 128).T)
    qs = np.asarray(inputs["L2_ffn_q_scale"], f32)
    ks = np.asarray(inputs["L2_ffn_k_scale"], f32)
    d["c_l2qs"] = np.ascontiguousarray(np.concatenate([qs, qs], axis=1).T)
    d["c_l2ks"] = np.ascontiguousarray(np.concatenate([ks, ks], axis=1).T)
    d["c_l2m"] = l2_window_classes()
    d["c_l2rot0"] = None
    half = np.arange(0, 64, 2, dtype=np.float64)
    inv_freq = 1.0 / 10000.0 ** (half / 64.0)
    t = np.arange(T, dtype=np.float64)
    freqs = t[:, None] * inv_freq[None, :]
    freqs = np.concatenate([freqs, freqs], -1)
    base_scale = (half + 0.4 * 64.0) / (1.4 * 64.0)
    power = (t - 4096 // 2) / (256 // 2)
    scale = base_scale[None, :] ** power[:, None]
    scale = np.concatenate([scale, scale], -1)
    cos = np.cos(freqs); sin = np.sin(freqs)
    def sperm(s):
        s2 = np.empty_like(s)
        s2[:, 0:32] = s[:, 32:64]
        s2[:, 32:64] = -s[:, 0:32]
        return s2
    csq = (cos * scale).T; ssq = sperm(sin * scale).T
    csk = (cos / scale).T; ssk = sperm(sin / scale).T
    st2 = lambda a: np.ascontiguousarray(np.concatenate([a, a], axis=0).astype(f32))
    d["c_csq"] = st2(csq); d["c_ssq"] = st2(ssq); d["c_csk"] = st2(csk); d["c_ssk"] = st2(ssk)
    r0 = np.zeros((128, 128), f32)
    for blk in range(2):
        for dd in range(64):
            r0[blk * 64 + (dd + 32) % 64, blk * 64 + dd] = 1.0
    d["c_rot0"] = r0
    d["c_ones1"] = np.ones((128, 128), f32)
    d["c_one64"] = np.ones((128, 64), f32)
    del d["c_l2rot0"]
    return d


def build_prog_l2a():
    nc = bass.Bass("TRN2", target_bir_lowering=False)
    es = contextlib.ExitStack()
    W = decl_l2(nc)
    xres_in = nc.dram_tensor("xres_in", [D, T], F32, kind="ExternalInput").ap()
    xn_out = nc.dram_tensor("xn_out", [D, T], BF16, kind="ExternalOutput").ap()
    xl_out = nc.dram_tensor("xl_out", [D, T], BF16, kind="ExternalOutput").ap()
    ao_dram = nc.dram_tensor("ao_scr", [D, T], BF16).ap()
    with es:
        ctx, bufs, ctoks = build_common(nc, es)
        ltoks = setup_l2_bufs(ctx, bufs, es)
        ctx.global_deps = ctoks + ltoks
        esA = contextlib.ExitStack()
        bufs["XN"] = esA.enter_context(nc.sbuf_tensor(uname("XN"), [128, NCH * T], BF16))
        t_ln = emit_ln(ctx, bufs, xres_in, [], xn_dram=xn_out)
        esA.close()
        esB = contextlib.ExitStack()
        XL = esB.enter_context(nc.sbuf_tensor(uname("XL"), [128, NCH * (T + TPAD)], BF16))
        t0 = emit_l2_xl_init(ctx, bufs, XL, xn_out, t_ln)
        t1 = emit_l2_inner_span(ctx, bufs, W, 0, 2, XL, ao_dram, t0)
        t2 = emit_l2_xl_save(ctx, bufs, XL, xl_out, t1)
        esB.close()
        ctx.finish_dma_waits("sy")
    return nc


def build_prog_l2mid(li0, li1):
    nc = bass.Bass("TRN2", target_bir_lowering=False)
    es = contextlib.ExitStack()
    W = decl_l2(nc)
    xl_in = nc.dram_tensor("xl_in", [D, T], BF16, kind="ExternalInput").ap()
    xl_out = nc.dram_tensor("xl_out", [D, T], BF16, kind="ExternalOutput").ap()
    ao_dram = nc.dram_tensor("ao_scr", [D, T], BF16).ap()
    with es:
        ctx, bufs, ctoks = build_common(nc, es)
        ltoks = setup_l2_bufs(ctx, bufs, es)
        ctx.global_deps = ctoks + ltoks
        esB = contextlib.ExitStack()
        XL = esB.enter_context(nc.sbuf_tensor(uname("XL"), [128, NCH * (T + TPAD)], BF16))
        t0 = emit_l2_xl_init(ctx, bufs, XL, xl_in, [])
        t1 = emit_l2_inner_span(ctx, bufs, W, li0, li1, XL, ao_dram, t0)
        t2 = emit_l2_xl_save(ctx, bufs, XL, xl_out, t1)
        esB.close()
        ctx.finish_dma_waits("sy")
    return nc


def build_prog_l2d():
    nc = bass.Bass("TRN2", target_bir_lowering=False)
    es = contextlib.ExitStack()
    W = decl_l2(nc)
    xres_in = nc.dram_tensor("xres_in", [D, T], F32, kind="ExternalInput").ap()
    xn_in = nc.dram_tensor("xn_in", [D, T], BF16, kind="ExternalInput").ap()
    xl_in = nc.dram_tensor("xl_in", [D, T], BF16, kind="ExternalInput").ap()
    xres_out = nc.dram_tensor("xres_out", [D, T], F32, kind="ExternalOutput").ap()
    g_dram = nc.dram_tensor("g_scr", [D, T], BF16).ap()
    with es:
        ctx, bufs, ctoks = build_common(nc, es)
        ltoks = setup_l2_bufs(ctx, bufs, es)
        ctx.global_deps = ctoks + ltoks
        t1 = emit_l2_post(ctx, bufs, W, xres_in, xres_out, xn_in, xl_in, g_dram, [])
        ctx.finish_dma_waits("sy")
    return nc


def build_common(nc, es, with_l0=True):
    blk = es.enter_context(nc.Block())
    ctx = Ctx(nc, blk, es)
    bufs = {}
    bufs["identf"] = es.enter_context(nc.sbuf_tensor(uname("identf"), [128, 128], F32))
    bufs["identb"] = es.enter_context(nc.sbuf_tensor(uname("identb"), [128, 128], BF16))
    bufs["inv1024"] = es.enter_context(nc.sbuf_tensor(uname("inv1024"), [128, 128], BF16))
    bufs["inv64"] = es.enter_context(nc.sbuf_tensor(uname("inv64"), [128, 64], BF16))
    bufs["m_tri"] = es.enter_context(nc.sbuf_tensor(uname("m_tri"), [128, 128], F32))
    bufs["qln"] = es.enter_context(nc.sbuf_tensor(uname("qln"), [128, 8], F32))
    bufs["cvals"] = es.enter_context(nc.sbuf_tensor(uname("cvals"), [128, 8], F32))
    cst = nc.dram_tensor("c_cst", [128, 4 * 128], F32, kind="ExternalInput").ap()
    msk = nc.dram_tensor("c_mtri", [128, 128], F32, kind="ExternalInput").ap()
    cvals_in = nc.dram_tensor("c_cvals", [128, 8], F32, kind="ExternalInput").ap()
    mr = misc_ring(ctx, "gp")
    toks = [
        mr.dma("gp", bufs["identf"][:, :], cst[:, 0:128]),
        mr.dma("gp", bufs["identb"][:, :], cst[:, 0:128]),
        mr.dma("gp", bufs["inv1024"][:, :], cst[:, 128:256]),
        mr.dma("gp", bufs["inv64"][:, :], cst[:, 256:320]),
        mr.dma("gp", bufs["m_tri"][:, :], msk),
        mr.dma("gp", bufs["cvals"][:, :], cvals_in),
    ]
    for i, v in enumerate(CVALS):
        nc.const_aps.register(F32, float(v), bufs["cvals"][:, i:i + 1])
    return ctx, bufs, toks


def decl_l3(nc):
    W = {}
    for nm, shp, dt in (("l3_wq", [D, D], F32), ("l3_wk", [D, D], F32), ("l3_wv", [D, D], F32),
                        ("l3_wo", [D, D], F32), ("l3_wg1", [D, 64], F32), ("l3_wg2", [64, D], F32),
                        ("l3_wgT", [D, 3072], I32), ("l3_wuT", [D, 3072], I32),
                        ("l3_wdT", [3072, D], I32)):
        W[nm] = nc.dram_tensor(nm, shp, dt, kind="ExternalInput").ap()
    return W


def setup_l3_bufs(ctx, bufs, es, deps=()):
    nc = ctx.nc
    CPP = 4 * 128 + 8
    bufs["l3c"] = es.enter_context(nc.sbuf_tensor(uname("l3c"), [128, 8 * CPP], F32))
    bufs["l3ab"] = es.enter_context(nc.sbuf_tensor(uname("l3ab"), [128, 16], F32))
    c1 = nc.dram_tensor("c_l3c", [128, 8 * CPP], F32, kind="ExternalInput").ap()
    c2 = nc.dram_tensor("c_l3ab", [128, 16], F32, kind="ExternalInput").ap()
    return [misc_ring(ctx, "gp").dma("gp", bufs["l3c"][:, :], c1, after=deps),
            misc_ring(ctx, "gp").dma("gp", bufs["l3ab"][:, :], c2, after=deps)]


def stage_l3(inputs):
    f = lambda a: np.ascontiguousarray(np.asarray(a, np.float32))
    tr = lambda a: np.ascontiguousarray(np.asarray(a).T)
    d = {
        "l3_wq": f(inputs["L3_attn_Wq"]), "l3_wk": f(inputs["L3_attn_Wk"]),
        "l3_wv": f(inputs["L3_attn_Wv"]), "l3_wo": f(inputs["L3_attn_Wo"]),
        "l3_wg1": f(inputs["L3_attn_Wg1"]), "l3_wg2": f(inputs["L3_attn_Wg2"]),
        "l3_wgT": np.ascontiguousarray(np.asarray(inputs["L3_ffn_w_gate"], np.int32).T),
        "l3_wuT": np.ascontiguousarray(np.asarray(inputs["L3_ffn_w_up"], np.int32).T),
        "l3_wdT": np.ascontiguousarray(np.asarray(inputs["L3_ffn_w_down"], np.int32).T),
    }
    ab = np.zeros((128, 16), np.float32)
    ab[:, 0:8] = np.asarray(inputs["L3_attn_gamma"], np.float32).reshape(8, 128).T
    ab[:, 8:16] = np.asarray(inputs["L3_attn_beta"], np.float32).reshape(8, 128).T
    d["c_l3ab"] = ab
    def build_slopes(n):
        def po2(m):
            start = 2.0 ** (-2.0 ** (-(np.log2(m) - 3)))
            return [start * start ** i for i in range(m)]
        if np.log2(n).is_integer():
            return po2(n)
        closest = 2 ** int(np.floor(np.log2(n)))
        return po2(closest) + build_slopes(2 * closest)[0::2][:n - closest]
    s = np.array(build_slopes(16), np.float64) * (1.0 - 0.0 / 11.0 + 1e-5)
    CPP = 4 * 128 + 8
    c = np.zeros((128, 8 * CPP), np.float32)
    idx = np.arange(128, dtype=np.float64)
    for p in range(8):
        o = p * CPP
        for h in range(2):
            sh = s[2 * p + h]
            diff = idx[None, :] - idx[:, None]
            m = np.where(diff >= 0, np.exp(-sh * diff), 0.0)
            c[:, o + h * 128: o + (h + 1) * 128] = m.astype(np.float32)
        lq = np.zeros((128, 128))
        lk = np.zeros((128, 128))
        for h in range(2):
            sh = s[2 * p + h]
            lq[h * 64:(h + 1) * 64, :] = np.exp(-sh * (idx[None, :] + 1.0))
            lk[:, h * 64:(h + 1) * 64] = np.exp(-sh * (127.0 - idx[:, None]))
        c[:, o + 256: o + 384] = lq.astype(np.float32)
        c[:, o + 384: o + 512] = lk.astype(np.float32)
        for h in range(2):
            c[h * 64:(h + 1) * 64, o + 512] = np.float32(np.exp(-s[2 * p + h] * 128.0))
    d["c_l3c"] = c
    return d


def build_prog_l3():
    nc = bass.Bass("TRN2", target_bir_lowering=False)
    es = contextlib.ExitStack()
    W = decl_l3(nc)
    xres_in = nc.dram_tensor("xres_in", [D, T], F32, kind="ExternalInput").ap()
    out = nc.dram_tensor("out", [T, D], F32, kind="ExternalOutput").ap()
    xres_out = nc.dram_tensor("xres_out_d", [D, T], F32).ap()
    g_dram = nc.dram_tensor("g_scr", [D, T], BF16).ap()
    gate_dram = nc.dram_tensor("gate_scr", [D, T], BF16).ap()
    with es:
        ctx, bufs, ctoks = build_common(nc, es)
        ltoks = setup_l3_bufs(ctx, bufs, es)
        ctx.global_deps = ctoks + ltoks
        t1 = emit_layer3(ctx, bufs, W, xres_in, xres_out, {"g": g_dram, "gate": gate_dram}, [])
        t2 = emit_final_transpose(ctx, bufs, xres_out, out, t1)
        ctx.finish_dma_waits("sy")
    return nc


def decl_l0(nc):
    W = {}
    for nm, shp in (("L0_attn_WqT", [D, D]), ("L0_attn_WkT", [D, D]), ("L0_attn_WvT", [D, D]),
                    ("L0_attn_WuT", [D, D]), ("L0_ffn_Win", [D, 64]), ("L0_ffn_Wout", [64, D]),
                    ("L0_ffn_W1", [64, 16]), ("L0_ffn_b1", [64, 16]), ("L0_ffn_W2", [64, 16])):
        W[nm] = nc.dram_tensor(nm, shp, F32, kind="ExternalInput").ap()
    for nm in ("L0_ffn_bin_", "L0_ffn_b2"):
        W[nm] = nc.dram_tensor(nm, [64], F32, kind="ExternalInput").ap()
    W["L0_ffn_boutT"] = nc.dram_tensor("L0_ffn_boutT", [128, NCH], F32, kind="ExternalInput").ap()
    W["qln8"] = nc.dram_tensor("qln8", [128, 8], F32, kind="ExternalInput").ap()
    return W


def build_prog_l0():
    nc = bass.Bass("TRN2", target_bir_lowering=False)
    es = contextlib.ExitStack()
    x0 = nc.dram_tensor("x0", [T, D], F32, kind="ExternalInput").ap()
    W = decl_l0(nc)

    xres_out = nc.dram_tensor("xres_out", [D, T], F32, kind="ExternalOutput").ap()
    xres_mid = nc.dram_tensor("xres_mid", [D, T], F32).ap()
    g_dram = nc.dram_tensor("g_scr", [D, T], BF16).ap()

    with es:
        ctx, bufs, ctoks = build_common(nc, es)
        qtok = misc_ring(ctx, "gp").dma("gp", bufs["qln"][:, :], W["qln8"])
        ctx.global_deps = ctoks + [qtok]
        t0 = emit_x0_transpose(ctx, bufs, x0, xres_mid)
        t1 = emit_layer0(ctx, bufs, W, xres_mid, xres_out, {"g": g_dram}, t0 + ctoks + [qtok])
        ctx.finish_dma_waits("sy")
    return nc


def make_consts():
    ident = np.eye(128, dtype=np.float32)
    ones1024 = np.full((128, 128), 1.0 / 1024.0, np.float32)
    ones64 = np.full((128, 128), 1.0 / 64.0, np.float32)
    cst = np.concatenate([ident, ones1024, ones64, np.zeros((128, 128), np.float32)], axis=1)
    tri = np.where(np.arange(128)[None, :] <= np.arange(128)[:, None], 0.0, NEG).astype(np.float32)
    cvals = np.zeros((128, 8), np.float32)
    for i, v in enumerate(CVALS):
        cvals[:, i] = v
    return cst, tri, cvals


def stage_l0(inputs):
    tr = lambda a: np.ascontiguousarray(np.asarray(a).T.astype(np.float32))
    d = {
        "L0_attn_WqT": tr(inputs["L0_attn_Wq"]),
        "L0_attn_WkT": tr(inputs["L0_attn_Wk"]),
        "L0_attn_WvT": tr(inputs["L0_attn_Wv"]),
        "L0_attn_WuT": tr(inputs["L0_attn_Wu"]),
        "L0_ffn_Win": np.ascontiguousarray(inputs["L0_ffn_Win"], np.float32),
        "L0_ffn_Wout": np.ascontiguousarray(inputs["L0_ffn_Wout"], np.float32),
        "L0_ffn_W1": np.ascontiguousarray(inputs["L0_ffn_W1"], np.float32),
        "L0_ffn_b1": np.ascontiguousarray(inputs["L0_ffn_b1"], np.float32),
        "L0_ffn_W2": np.ascontiguousarray(inputs["L0_ffn_W2"], np.float32),
        "L0_ffn_bin_": np.ascontiguousarray(inputs["L0_ffn_bin_"], np.float32),
        "L0_ffn_b2": np.ascontiguousarray(inputs["L0_ffn_b2"], np.float32),
        "L0_ffn_boutT": np.ascontiguousarray(np.asarray(inputs["L0_ffn_bout"], np.float32).reshape(8, 128).T),
    }
    qw = np.concatenate([inputs["L0_attn_qln_w"], inputs["L0_attn_qln_w"]]).astype(np.float32)
    qb = np.concatenate([inputs["L0_attn_qln_b"], inputs["L0_attn_qln_b"]]).astype(np.float32)
    kw = np.concatenate([inputs["L0_attn_kln_w"], inputs["L0_attn_kln_w"]]).astype(np.float32)
    kb = np.concatenate([inputs["L0_attn_kln_b"], inputs["L0_attn_kln_b"]]).astype(np.float32)
    qln8 = np.zeros((128, 8), np.float32)
    qln8[:, 0] = qw; qln8[:, 1] = qb; qln8[:, 2] = kw; qln8[:, 3] = kb
    qln8[:, 4] = qw * 0.125; qln8[:, 5] = qb * 0.125
    d["qln8"] = qln8
    return d


def emit_layer2_full(ctx, bufs, W, xres_in, xres_out, deps):
    nc = ctx.nc
    xn2 = nc.dram_tensor(uname("xn2d"), [D, T], BF16).ap()
    xl2 = nc.dram_tensor(uname("xl2d"), [D, T], BF16).ap()
    ao2 = nc.dram_tensor(uname("ao2d"), [D, T], BF16).ap()
    g2 = nc.dram_tensor(uname("g2d"), [D, T], BF16).ap()
    esA = contextlib.ExitStack()
    bufs["XN"] = esA.enter_context(nc.sbuf_tensor(uname("XN"), [128, NCH * T], BF16))
    t_ln = emit_ln(ctx, bufs, xres_in, deps, xn_dram=xn2)
    esA.close()
    esB = contextlib.ExitStack()
    XL = esB.enter_context(nc.sbuf_tensor(uname("XL"), [128, NCH * (T + TPAD)], BF16))
    t0 = emit_l2_xl_init(ctx, bufs, XL, xn2, t_ln)
    t1 = emit_l2_inner_span(ctx, bufs, W, 0, 6, XL, ao2, t0)
    t2 = emit_l2_xl_save(ctx, bufs, XL, xl2, t1)
    esB.close()
    return emit_l2_post(ctx, bufs, W, xres_in, xres_out, xn2, xl2, g2, t2)


def build_prog_span(a, b):
    nc = bass.Bass("TRN2", target_bir_lowering=False)
    es = contextlib.ExitStack()
    decls = (decl_l0, decl_l1, decl_l2, decl_l3)
    Ws = {li: decls[li](nc) for li in range(a, b)}
    if a == 0:
        x0 = nc.dram_tensor("x0", [T, D], F32, kind="ExternalInput").ap()
    else:
        xin = nc.dram_tensor("xspan_in", [D, T], F32, kind="ExternalInput").ap()
    if b == 4:
        out = nc.dram_tensor("out", [T, D], F32, kind="ExternalOutput").ap()
    else:
        xout = nc.dram_tensor("xspan_out", [D, T], F32, kind="ExternalOutput").ap()
    with es:
        ctx, bufs, ctoks = build_common(nc, es)
        base_gd = list(ctoks)
        ctx.global_deps = base_gd
        if a == 0:
            xcur = nc.dram_tensor("xres_tm", [D, T], F32).ap()
            t = emit_x0_transpose(ctx, bufs, x0, xcur)
        else:
            xcur = xin
            t = []
        for li in range(a, b):
            xdst = xout if (li == b - 1 and b < 4) else nc.dram_tensor(uname(f"xr{li}"), [D, T], F32).ap()
            les = contextlib.ExitStack()
            if li == 0:
                qtok = misc_ring(ctx, "gp").dma("gp", bufs["qln"][:, :], Ws[0]["qln8"])
                ctx.global_deps = base_gd + [qtok]
                g0 = nc.dram_tensor(uname("g0d"), [D, T], BF16).ap()
                t = emit_layer0(ctx, bufs, Ws[0], xcur, xdst, {"g": g0}, t + [qtok])
            elif li == 1:
                ltoks = setup_l1_bufs(ctx, bufs, les, deps=t)
                ctx.global_deps = base_gd + ltoks
                g1 = nc.dram_tensor(uname("g1d"), [D, T], BF16).ap()
                xn1 = nc.dram_tensor(uname("xn1d"), [D, T], BF16).ap()
                gacc1 = nc.dram_tensor(uname("gacc1d"), [D, T], BF16).ap()
                t = emit_layer1(ctx, bufs, Ws[1], xcur, xdst,
                                {"g": g1, "xn": xn1, "gacc": gacc1}, t + ltoks)
            elif li == 2:
                ltoks = setup_l2_bufs(ctx, bufs, les, deps=t)
                ctx.global_deps = base_gd + ltoks
                t = emit_layer2_full(ctx, bufs, Ws[2], xcur, xdst, t + ltoks)
            else:
                ltoks = setup_l3_bufs(ctx, bufs, les, deps=t)
                ctx.global_deps = base_gd + ltoks
                g3 = nc.dram_tensor(uname("g3d"), [D, T], BF16).ap()
                gate3 = nc.dram_tensor(uname("gate3d"), [D, T], BF16).ap()
                t = emit_layer3(ctx, bufs, Ws[3], xcur, xdst, {"g": g3, "gate": gate3}, t + ltoks)
            ctx.global_deps = base_gd
            les.close()
            xcur = xdst
        if b == 4:
            t = emit_final_transpose(ctx, bufs, xcur, out, t)
        ctx.finish_dma_waits("sy")
    return nc


SPANS = [(0, 4)]


def kernel(**inputs):
    cst, tri, cvals = make_consts()
    com = {"c_cst": cst, "c_mtri": tri, "c_cvals": cvals}
    x = np.ascontiguousarray(np.asarray(inputs["L0_attn_x"], np.float32))
    stagers = (stage_l0, stage_l1, stage_l2, stage_l3)
    stg = {}
    cur = None
    for (a, b) in SPANS:
        nc = build_prog_span(a, b)
        for li in range(a, b):
            if li not in stg:
                stg[li] = stagers[li](inputs)
        maps = []
        for c in range(8):
            m = dict(com)
            for li in range(a, b):
                m.update(stg[li])
            if a == 0:
                m["x0"] = np.ascontiguousarray(x[c])
            else:
                m["xspan_in"] = cur[c]
            maps.append(m)
        r = run_bass_kernel_spmd(nc, maps, core_ids=list(range(8)))
        if b == 4:
            return np.stack([np.asarray(r.results[c]["out"]) for c in range(8)])
        cur = [np.asarray(r.results[c]["xspan_out"]) for c in range(8)]
